# version 02
# baseline (speedup 1.0000x reference)
import numpy as np
import concourse.bass as bass
import concourse.mybir as mybir
from concourse.bass_utils import run_bass_kernel_spmd

F32 = mybir.dt.float32
BF16 = mybir.dt.bfloat16
AF = mybir.ActivationFunctionType
ALU = mybir.AluOpType

NCORES = 8
BC = 2
L = 2048
H = 2048
HC = 16
E = 4096
EC = 32
EO = 64
N = 16
R = 128
LB = 4
LT = 512
KF = 8
RF = 64
KR = 512
KRC = 4

ENGINES = ("sync", "tensor", "vector", "scalar", "gpsimd")


class Prog:
    def __init__(self):
        self.ops = {e: [] for e in ENGINES}
        self.cnt = {}

    def on(self, eng, fn):
        self.ops[eng].append(fn)

    def bump(self, name, k=1):
        v = self.cnt.get(name, 0) + k
        self.cnt[name] = v
        return v

    def val(self, name):
        return self.cnt.get(name, 0)

    def wait(self, eng, sem_h, v):
        if v > 0:
            self.on(eng, lambda e, s=sem_h, vv=v: e.wait_ge(s, vv))

    def dma(self, eng, out, in_, sem_h, name):
        v = self.bump(name, 16)
        self.on(eng, lambda e, o=out, i=in_, s=sem_h: e.dma_start(out=o, in_=i).then_inc(s, 16))
        return v

    def step(self, eng, fn, sem_h, name):
        v = self.bump(name, 1)
        self.on(eng, lambda e, f=fn, s=sem_h: f(e).then_inc(s, 1))
        return v

    def run(self, eng, fn):
        self.on(eng, fn)


def build_nc(debug=False):
    nc = bass.Bass("TRN2", target_bir_lowering=False)

    x_fm = nc.dram_tensor("x_fm", [BC, H, L], F32, kind="ExternalInput").ap()
    w_in_t = nc.dram_tensor("w_in_t", [H, 2 * E], F32, kind="ExternalInput").ap()
    conv_wt = nc.dram_tensor("conv_wt", [128, EC * 4], F32, kind="ExternalInput").ap()
    conv_bt = nc.dram_tensor("conv_bt", [128, EC], F32, kind="ExternalInput").ap()
    w_xp_t = nc.dram_tensor("w_xp_t", [E, 160], F32, kind="ExternalInput").ap()
    w_dt_t = nc.dram_tensor("w_dt_t", [R, E], F32, kind="ExternalInput").ap()
    dt_bt = nc.dram_tensor("dt_bt", [128, EC], F32, kind="ExternalInput").ap()
    alog_t = nc.dram_tensor("alog_t", [128, EC * N], F32, kind="ExternalInput").ap()
    d_t = nc.dram_tensor("d_t", [128, EC], F32, kind="ExternalInput").ap()
    w_out_t = nc.dram_tensor("w_out_t", [E, H], F32, kind="ExternalInput").ap()
    w1 = nc.dram_tensor("w1", [H, H], F32, kind="ExternalInput").ap()
    b1_t = nc.dram_tensor("b1_t", [128, HC], F32, kind="ExternalInput").ap()
    w2 = nc.dram_tensor("w2", [H, KF], F32, kind="ExternalInput").ap()
    b2_t = nc.dram_tensor("b2_t", [KF, 1], F32, kind="ExternalInput").ap()
    vt = nc.dram_tensor("vt", [H, KR], F32, kind="ExternalInput").ap()
    ut = nc.dram_tensor("ut", [KR, H], F32, kind="ExternalInput").ap()

    out_fm = nc.dram_tensor("out_fm", [BC, H, L], F32, kind="ExternalOutput").ap()
    kind_s = "ExternalOutput" if debug else "Internal"
    xcv_f = nc.dram_tensor("xcv_f", [BC, EC, 128, L], F32, kind=kind_s).ap()
    xcv_h = nc.dram_tensor("xcv_h", [BC, EC, 128, L], BF16, kind=kind_s).ap()
    sg_f = nc.dram_tensor("sg_f", [BC, EC, 128, L], F32, kind=kind_s).ap()
    lnt_f = nc.dram_tensor("lnt_f", [BC, EC, 128, L], F32, kind=kind_s).ap()
    bn_r = nc.dram_tensor("bn_r", [BC, N, L], F32, kind=kind_s).ap()
    c_r = nc.dram_tensor("c_r", [BC, N, L], F32, kind=kind_s).ap()
    y_hb = nc.dram_tensor("y_hb", [BC, EC, 128, L], BF16, kind=kind_s).ap()
    mo_f = nc.dram_tensor("mo_f", [BC, HC, 128, L], F32, kind=kind_s).ap()
    x2_f = nc.dram_tensor("x2_f", [BC, HC, 128, L], F32, kind=kind_s).ap()
    x2ln_h = nc.dram_tensor("x2ln_h", [BC, HC, 128, L], BF16, kind=kind_s).ap()
    h1_h = nc.dram_tensor("h1_h", [BC, HC, 128, L], BF16, kind=kind_s).ap()
    al_f = nc.dram_tensor("al_f", [BC, KF, L], F32, kind=kind_s).ap()
    if debug:
        xln_d = nc.dram_tensor("xln_d", [128, HC * BC * L], BF16, kind="ExternalOutput").ap()
        u_d = nc.dram_tensor("u_d", [BC, EC, 128, L], F32, kind="ExternalOutput").ap()
        scan_d = nc.dram_tensor("scan_d", [6, 128, N * LT], F32, kind="ExternalOutput").ap()
        dbg2 = nc.dram_tensor("dbg2", [3, 128, 512], F32, kind="ExternalOutput").ap()

    prog = Prog()
    SB_F32 = 52000

    from contextlib import ExitStack
    with ExitStack() as _stack:
        arena_t = _stack.enter_context(nc.sbuf_tensor("arena", [128, SB_F32], F32))
        pers_t = _stack.enter_context(nc.sbuf_tensor("persist", [128, 1200], F32))
        psum = [_stack.enter_context(nc.psum_tensor(f"pb{i}", [128, 512], F32)) for i in range(8)]
        pb0, pb1, pb2, pb3, pb4, pb5, pb6, pb7 = psum
        s_c = _stack.enter_context(nc.semaphore("s_c"))
        s_ld2 = _stack.enter_context(nc.semaphore("s_ld2"))
        s_w = [_stack.enter_context(nc.semaphore(f"s_w{i}")) for i in range(2)]
        s_b = [_stack.enter_context(nc.semaphore(f"s_b{i}")) for i in range(2)]
        st_sems = {k: _stack.enter_context(nc.semaphore(f"st_{k}"))
                   for k in ("sg", "xcv", "lnt", "bc", "y", "mo", "x2", "h1", "out", "dbg")}
        s_ld = _stack.enter_context(nc.semaphore("s_ld"))
        s_st = _stack.enter_context(nc.semaphore("s_st"))
        s_t = _stack.enter_context(nc.semaphore("s_t"))
        s_v = _stack.enter_context(nc.semaphore("s_v"))
        s_s = _stack.enter_context(nc.semaphore("s_s"))
        block = _stack.enter_context(nc.Block())

        def store_barrier(eng):
            for k, sem in st_sems.items():
                prog.wait(eng, sem, prog.val("st_" + k))
        psum = [pb0, pb1, pb2, pb3, pb4, pb5, pb6, pb7]
        arena = arena_t.ap()
        pers = pers_t.ap()

        po = [0]

        def pcarve(n):
            a = po[0]
            po[0] += n
            assert po[0] <= 1200
            return pers[:, a:a + n]

        eA = pcarve(EC * N)
        cw_sb = pcarve(EC * 4)
        cb_sb = pcarve(EC)
        dtb_sb = pcarve(EC)
        dtb_neg = pcarve(EC)
        dD_sb = pcarve(EC)
        b1_sb = pcarve(HC)
        b2_col = pcarve(1)[0:KF, :]
        ones_col = pcarve(1)
        one_row1 = pcarve(128)[0:1, :]
        one_col8 = pcarve(1)[0:KF, :]
        eps_col = pcarve(1)

        ao = [0]

        def carve(n):
            a = ao[0]
            ao[0] += n
            assert ao[0] <= SB_F32, f"arena overflow {ao[0]}"
            return arena[:, a:a + n]

        def carve_h(n_bf16):
            assert n_bf16 % 2 == 0
            return carve(n_bf16 // 2).bitcast(BF16)

        def reset(keep=0):
            ao[0] = keep

        prog.dma("sync", eA, alog_t, s_c, "c")
        prog.dma("sync", cw_sb, conv_wt, s_c, "c")
        prog.dma("sync", cb_sb, conv_bt, s_c, "c")
        prog.dma("sync", dtb_sb, dt_bt, s_c, "c")
        prog.dma("sync", dD_sb, d_t, s_c, "c")
        prog.dma("sync", b1_sb, b1_t, s_c, "c")
        cldv = prog.dma("sync", b2_col, b2_t, s_c, "c")
        prog.wait("scalar", s_c, cldv)
        prog.step("scalar", lambda e: e.activation(out=eA, in_=eA, func=AF.Exp), s_s, "s")
        prog.step("scalar", lambda e: e.activation(out=dtb_neg, in_=dtb_sb, func=AF.Copy, scale=-1.0), s_s, "s")
        prog.run("vector", lambda e: e.memset(ones_col, 1.0))
        prog.run("vector", lambda e: e.memset(one_row1, 1.0))
        prog.run("vector", lambda e: e.memset(one_col8, 1.0))
        prog.run("vector", lambda e: e.memset(eps_col, 1e-5))
        consts_v = prog.step("vector", lambda e: e.engine_nop(), s_v, "v")

        def ln_block(xt3, sq3, row_m, row_q, row_r, pm, pr, pst_a, pst_b, prev_v, nch):
            prog.wait("scalar", s_v, prev_v)
            sv = prog.step("scalar", lambda e: e.activation(out=sq3, in_=xt3, func=AF.Square), s_s, "s")
            prog.wait("tensor", s_s, sv)
            for c in range(nch):
                fn = lambda e, c=c: e.matmul(pst_a[0:1, :], ones_col, xt3[:, c, :],
                                             start=(c == 0), stop=(c == nch - 1))
                prog.on("tensor", fn)
            for c in range(nch):
                fn = lambda e, c=c: e.matmul(pst_b[0:1, :], ones_col, sq3[:, c, :],
                                             start=(c == 0), stop=(c == nch - 1))
                if c == nch - 1:
                    prog.step("tensor", fn, s_t, "t")
                else:
                    prog.on("tensor", fn)
            tv = prog.val("t")
            prog.wait("scalar", s_t, tv)
            prog.step("scalar", lambda e: e.activation(out=row_m, in_=pst_a[0:1, :],
                                                       func=AF.Copy, scale=1.0 / (nch * 128)), s_s, "s")
            prog.step("scalar", lambda e: e.activation(out=row_q, in_=pst_b[0:1, :],
                                                       func=AF.Copy, scale=1.0 / (nch * 128)), s_s, "s")
            sv2 = prog.step("scalar", lambda e: e.activation(out=row_r, in_=row_m, func=AF.Square), s_s, "s")
            prog.wait("vector", s_s, sv2)
            vv = prog.step("vector", lambda e: e.tensor_sub(row_q, row_q, row_r), s_v, "v")
            prog.wait("scalar", s_v, vv)
            sv3 = prog.step("scalar", lambda e: e.activation(out=row_r, in_=row_q, func=AF.Sqrt, bias=eps_col[0:1, :]), s_s, "s")
            prog.wait("vector", s_s, sv3)
            vv2 = prog.step("vector", lambda e: e.reciprocal(row_r, row_r), s_v, "v")
            prog.wait("tensor", s_v, vv2)
            prog.wait("tensor", s_v, prev_v)
            prog.on("tensor", lambda e: e.matmul(pm[:], one_row1, row_m, start=True, stop=True))
            prog.step("tensor", lambda e: e.matmul(pr[:], one_row1, row_r, start=True, stop=True), s_t, "t")
            return prog.val("t")

        reset()
        xln_h = carve_h(HC * (BC * L))
        xt_a = carve(HC * LT)
        sq_a = carve(HC * LT)
        row_m = carve(512)[0:1, :]
        row_q = carve(512)[0:1, :]
        row_r = carve(512)[0:1, :]
        A0_KEEP = (HC * (BC * L)) // 2

        xt3 = xt_a.rearrange("p (hc l) -> p hc l", hc=HC)
        sq3 = sq_a.rearrange("p (hc l) -> p hc l", hc=HC)
        xln3 = xln_h.rearrange("p (hc t) -> p hc t", hc=HC)
        prev_v = consts_v
        for uu in range(BC * LB):
            b, lb = divmod(uu, LB)
            prog.wait("sync", s_v, prev_v)
            lv = prog.dma("sync", xt3,
                          x_fm[b, :, lb * LT:(lb + 1) * LT].rearrange("(hc p) l -> p hc l", p=128),
                          s_c, "c")
            prog.wait("scalar", s_c, lv)
            prog.wait("tensor", s_c, lv)
            ln_block(xt3, sq3, row_m, row_q, row_r, psum[0], psum[1], psum[2], psum[3],
                     prev_v, HC)
            tv = prog.val("t")
            prog.wait("vector", s_t, tv)
            prog.on("vector", lambda e: e.tensor_tensor(
                out=sq3, in0=xt3,
                in1=psum[0][:].unsqueeze(1).broadcast_to([128, HC, 512]),
                op=ALU.subtract))
            dst = xln3[:, :, uu * LT:(uu + 1) * LT]
            prev_v = prog.step("vector", lambda e, d=dst: e.tensor_tensor(
                out=d, in0=sq3,
                in1=psum[1][:].unsqueeze(1).broadcast_to([128, HC, 512]),
                op=ALU.mult), s_v, "v")
        ln1_done_v = prev_v
        if debug:
            prog.wait("gpsimd", s_v, ln1_done_v)
            prog.dma("gpsimd", xln_d, xln_h, st_sems["dbg"], "st_dbg")

        reset(A0_KEEP)
        wbuf = [carve(16 * 128), carve(16 * 128)]
        wbuf_h = [carve_h(16 * 128), carve_h(16 * 128)]
        u_row = [carve(3 + L), carve(3 + L)]
        c0 = carve(L)
        c1 = carve(L)
        sg_t = [carve(LT), carve(LT)]
        xcv_t = [carve(L)]
        xcvh_t = [carve_h(L)]

        prog.run("vector", lambda e: e.memset(u_row[0][:, 0:3], 0.0))
        prog.run("vector", lambda e: e.memset(u_row[1][:, 0:3], 0.0))

        wb3_src = w_in_t
        def phase_a1(eo):
            pp = eo % 2
            t_prev2 = prog.cnt.get(f"a1_t_{eo - 2}", 0)
            prog.wait("sync", s_t, t_prev2)
            if eo < 2:
                prog.wait("sync", s_v, ln1_done_v)
            lv = prog.dma("sync", wbuf[pp].rearrange("p (hc e) -> p hc e", e=128),
                          wb3_src[:, eo * 128:(eo + 1) * 128].rearrange("(hc p) e -> p hc e", p=128),
                          s_w[pp], f"w{pp}")
            prog.wait("vector", s_w[pp], lv)
            prog.wait("vector", s_t, t_prev2)
            if eo == 0:
                prog.wait("vector", s_v, ln1_done_v)
            cv = prog.step("vector", lambda e, pp=pp: e.tensor_copy(wbuf_h[pp], wbuf[pp]), s_v, "v")

            prog.wait("tensor", s_v, cv)
            if eo == 0:
                prog.wait("tensor", s_v, ln1_done_v)
            is_u = eo < EC
            ec = eo if is_u else eo - EC
            for b in range(BC):
                for lb in range(LB):
                    pt = psum[b * LB + lb]
                    w_ev = prog.cnt.get(f"a1_ev_{eo - 1}_{b}_{lb}", None)
                    if w_ev is not None:
                        prog.wait("tensor", s_v if w_ev[0] == "v" else s_s, w_ev[1])
                    for hc in range(HC):
                        fn = lambda e, pt=pt, pp=pp, hc=hc, b=b, lb=lb: e.matmul(
                            pt[:], wbuf_h[pp][:, hc * 128:(hc + 1) * 128],
                            xln3[:, hc, (b * LB + lb) * LT:(b * LB + lb + 1) * LT],
                            start=(hc == 0), stop=(hc == HC - 1))
                        if hc == HC - 1:
                            tv = prog.step("tensor", fn, s_t, "t")
                            prog.cnt[f"a1_t_{eo}"] = tv
                        else:
                            prog.on("tensor", fn)
                    if is_u:
                        prog.wait("vector", s_t, tv)
                        ev = prog.step("vector", lambda e, b=b, lb=lb, pt=pt: e.tensor_copy(
                            u_row[b][:, 3 + lb * LT: 3 + (lb + 1) * LT], pt[:]), s_v, "v")
                        prog.cnt[f"a1_ev_{eo}_{b}_{lb}"] = ("v", ev)
                    else:
                        gp = (eo * BC * LB + b * LB + lb) % 2
                        prog.wait("scalar", st_sems["sg"], prog.val("st_sg"))
                        prog.wait("scalar", s_t, tv)
                        ev = prog.step("scalar", lambda e, gp=gp, pt=pt: e.activation(
                            out=sg_t[gp], in_=pt[:], func=AF.Silu), s_s, "s")
                        prog.cnt[f"a1_ev_{eo}_{b}_{lb}"] = ("s", ev)
                        prog.wait("gpsimd", s_s, ev)
                        stv = prog.dma("gpsimd", sg_f[b, ec, :, lb * LT:(lb + 1) * LT],
                                       sg_t[gp], st_sems["sg"], "st_sg")
            if is_u:
                for b in range(BC):
                    if debug:
                        lastev = prog.cnt[f"a1_ev_{eo}_{b}_{LB - 1}"][1]
                        prog.wait("gpsimd", s_v, lastev)
                        prog.dma("gpsimd", u_d[b, ec], u_row[b][:, 3:3 + L], st_sems["dbg"], "st_dbg")
                    pp2 = 0
                    prog.wait("vector", st_sems["xcv"], prog.val("st_xcv"))
                    prog.on("vector", lambda e, b=b, ec=ec: e.tensor_scalar_mul(
                        c0, u_row[b][:, 0:L], cw_sb[:, ec * 4 + 0:ec * 4 + 1]))
                    prog.on("vector", lambda e, b=b, ec=ec: e.scalar_tensor_tensor(
                        out=c1, in0=u_row[b][:, 1:1 + L], scalar=cw_sb[:, ec * 4 + 1:ec * 4 + 2],
                        in1=c0, op0=ALU.mult, op1=ALU.add))
                    prog.on("vector", lambda e, b=b, ec=ec: e.scalar_tensor_tensor(
                        out=c0, in0=u_row[b][:, 2:2 + L], scalar=cw_sb[:, ec * 4 + 2:ec * 4 + 3],
                        in1=c1, op0=ALU.mult, op1=ALU.add))
                    vv = prog.step("vector", lambda e, b=b, ec=ec: e.scalar_tensor_tensor(
                        out=c1, in0=u_row[b][:, 3:3 + L], scalar=cw_sb[:, ec * 4 + 3:ec * 4 + 4],
                        in1=c0, op0=ALU.mult, op1=ALU.add), s_v, "v")
                    prog.wait("scalar", s_v, vv)
                    sv = prog.step("scalar", lambda e, pp2=pp2, ec=ec: e.activation(
                        out=xcv_t[pp2], in_=c1, func=AF.Silu, bias=cb_sb[:, ec:ec + 1]), s_s, "s")
                    prog.wait("vector", s_s, sv)
                    vv2 = prog.step("vector", lambda e, pp2=pp2: e.tensor_copy(
                        xcvh_t[pp2], xcv_t[pp2]), s_v, "v")
                    prog.wait("gpsimd", s_v, vv2)
                    prog.dma("gpsimd", xcv_f[b, ec], xcv_t[pp2], st_sems["xcv"], "st_xcv")
                    stv = prog.dma("gpsimd", xcv_h[b, ec], xcvh_t[pp2], st_sems["xcv"], "st_xcv")

        for eo in range(EO):
            phase_a1(eo)

        reset(0)
        xp_w = carve(EC * 160)
        xp_wh = carve_h(EC * 160)
        dtlow_h = carve_h(BC * L)
        xcvs = [carve_h(LT), carve_h(LT)]
        bc_ev = carve(LT)
        A2_END = ao[0]

        prog.wait("sync", s_t, prog.val("t"))
        prog.wait("sync", s_v, prog.val("v"))
        prog.wait("sync", s_s, prog.val("s"))
        lv = prog.dma("sync", xp_w.rearrange("p (ec r) -> p ec r", r=160),
                      w_xp_t.rearrange("(ec p) r -> p ec r", p=128), s_c, "c")
        prog.wait("vector", s_c, lv)
        xpcv = prog.step("vector", lambda e: e.tensor_copy(xp_wh, xp_w), s_v, "v")

        for uu in range(BC * LB):
            b, lb = divmod(uu, LB)
            for ec in range(EC):
                pp = ec % 2
                idx = uu * EC + ec
                prog.wait("sync", s_t, prog.cnt.get(f"a2_t_{idx - 2}", 0))
                if uu == 0 and ec == 0:
                    store_barrier("sync")
                lv = prog.dma("sync", xcvs[pp], xcv_h[b, ec, :, lb * LT:(lb + 1) * LT],
                              s_b[pp], f"b{pp}")
                prog.wait("tensor", s_b[pp], lv)
                if idx == 0:
                    prog.wait("tensor", s_v, xpcv)
                if ec == 0:
                    prog.wait("tensor", s_v, prog.cnt.get(f"a2_ev_{uu - 1}", 0))
                    prog.wait("tensor", s_s, prog.cnt.get(f"a2_es_{uu - 1}", 0))
                prog.on("tensor", lambda e, pp=pp, ec=ec: e.matmul(
                    psum[0][:], xp_wh[:, ec * 160:ec * 160 + 128], xcvs[pp],
                    start=(ec == 0), stop=(ec == EC - 1)))
                tv = prog.step("tensor", lambda e, pp=pp, ec=ec: e.matmul(
                    psum[1][0:32, :], xp_wh[:, ec * 160 + 128:(ec + 1) * 160], xcvs[pp],
                    start=(ec == 0), stop=(ec == EC - 1)), s_t, "t")
                prog.cnt[f"a2_t_{idx}"] = tv
            prog.wait("vector", s_t, tv)
            ev = prog.step("vector", lambda e, uu=uu: e.tensor_copy(
                dtlow_h[:, uu * LT:(uu + 1) * LT], psum[0][:]), s_v, "v")
            prog.cnt[f"a2_ev_{uu}"] = ev
            prog.wait("scalar", s_t, tv)
            prog.wait("scalar", st_sems["bc"], prog.val("st_bc"))
            prog.step("scalar", lambda e: e.activation(
                out=bc_ev[0:32, :], in_=psum[1][0:32, :], func=AF.Copy), s_s, "s")
            es = prog.step("scalar", lambda e: e.activation(
                out=bc_ev[0:16, :], in_=bc_ev[0:16, :], func=AF.Copy, scale=-1.0), s_s, "s")
            prog.cnt[f"a2_es_{uu}"] = es
            prog.wait("gpsimd", s_s, es)
            prog.dma("gpsimd", bn_r[b, :, lb * LT:(lb + 1) * LT], bc_ev[0:16, :], st_sems["bc"], "st_bc")
            stv = prog.dma("gpsimd", c_r[b, :, lb * LT:(lb + 1) * LT], bc_ev[16:32, :], st_sems["bc"], "st_bc")

        reset(A2_END)
        dt_w = carve(EC * 128)
        dt_wh = carve_h(EC * 128)
        lnt_t = [carve(LT), carve(LT)]
        sig_t = carve(LT)

        lv = prog.dma("sync", dt_w.rearrange("p (ec c) -> p ec c", c=128),
                      w_dt_t.rearrange("r (ec c) -> r ec c", c=128), s_c, "c")
        prog.wait("vector", s_c, lv)
        dtcv = prog.step("vector", lambda e: e.tensor_copy(dt_wh, dt_w), s_v, "v")
        prog.wait("tensor", s_v, dtcv)

        for ec in range(EC):
            for uu in range(BC * LB):
                b, lb = divmod(uu, LB)
                idx = ec * BC * LB + uu
                pp = idx % 2
                pt = psum[2 + pp]
                prog.wait("tensor", s_s, prog.cnt.get(f"a3_ev_{idx - 2}", 0))
                tv = prog.step("tensor", lambda e, pt=pt, ec=ec, uu=uu: e.matmul(
                    pt[:], dt_wh[:, ec * 128:(ec + 1) * 128],
                    dtlow_h[:, uu * LT:(uu + 1) * LT], start=True, stop=True), s_t, "t")
                prog.wait("scalar", s_t, tv)
                prog.wait("scalar", st_sems["lnt"], prog.val("st_lnt"))
                prog.step("scalar", lambda e, pt=pt, ec=ec: e.activation(
                    out=sig_t, in_=pt[:], func=AF.Sigmoid, scale=-1.0,
                    bias=dtb_neg[:, ec:ec + 1]), s_s, "s")
                sv = prog.step("scalar", lambda e, pp=pp: e.activation(
                    out=lnt_t[pp], in_=sig_t, func=AF.Ln), s_s, "s")
                prog.cnt[f"a3_ev_{idx}"] = sv
                prog.wait("gpsimd", s_s, sv)
                stv = prog.dma("gpsimd", lnt_f[b, ec, :, lb * LT:(lb + 1) * LT],
                               lnt_t[pp], st_sems["lnt"], "st_lnt")

        reset(0)
        dA = [carve(8 * LT), carve(8 * LT)]
        w_t = carve(N * LT)
        s_tl = carve(N * LT)
        bnb = carve(N * LT)
        ccb = carve(N * LT)
        lnts = [carve(LT), carve(LT)]
        xcvs2 = [carve(LT), carve(LT)]
        sgs = [carve(LT), carve(LT)]
        mdtu = carve(LT)
        yhist = carve(LT)
        yfin = [carve_h(LT), carve_h(LT)]
        sstate = carve(BC * EC * N)

        prog.run("vector", lambda e: e.memset(sstate, 0.0))
        prog.step("vector", lambda e: e.engine_nop(), s_v, "v")
        store_barrier("sync")
        prog.wait("sync", s_t, prog.val("t"))
        prog.wait("sync", s_v, prog.val("v"))
        prog.wait("sync", s_s, prog.val("s"))

        w3 = w_t.rearrange("p (n l) -> p n l", n=N)
        bnb3 = bnb.rearrange("p (n l) -> p n l", n=N)
        qred = w_t.rearrange("p (n l) -> p l n", n=N)

        for uu in range(BC * LB):
            b, lb = divmod(uu, LB)
            prog.wait("sync", s_v, prog.cnt.get(f"b_v_{uu - 1}", 0))
            prog.dma("sync", bnb3,
                     bn_r[b, :, lb * LT:(lb + 1) * LT].unsqueeze(0).broadcast_to([128, N, LT]),
                     s_ld2, "ld2")
            prog.dma("sync", ccb.rearrange("p (n l) -> p n l", n=N),
                     c_r[b, :, lb * LT:(lb + 1) * LT].unsqueeze(0).broadcast_to([128, N, LT]),
                     s_ld2, "ld2")
            bcv = prog.val("ld2")
            def phase_b_ec(ec, b=b, lb=lb, uu=uu, bcv=bcv):
                idx = uu * EC + ec
                pp = idx % 2
                prog.wait("sync", s_v, prog.cnt.get(f"b_vs_{idx - 2}", 0))
                prog.wait("sync", s_s, prog.cnt.get(f"b_ss_{idx - 2}", 0))
                prog.dma("sync", lnts[pp], lnt_f[b, ec, :, lb * LT:(lb + 1) * LT], s_b[pp], f"b{pp}")
                prog.dma("sync", xcvs2[pp], xcv_f[b, ec, :, lb * LT:(lb + 1) * LT], s_b[pp], f"b{pp}")
                lv3 = prog.dma("sync", sgs[pp], sg_f[b, ec, :, lb * LT:(lb + 1) * LT], s_b[pp], f"b{pp}")

                for hh in range(2):
                    prog.wait("scalar", s_b[pp], lv3)
                    prog.wait("scalar", s_v, prog.cnt.get(f"b_vh_{idx - 1}_{hh}", 0))
                    for nn in range(8):
                        n = hh * 8 + nn
                        fn = lambda e, hh=hh, nn=nn, n=n, ec=ec, pp=pp: e.activation(
                            out=dA[hh][:, nn * LT:(nn + 1) * LT], in_=lnts[pp],
                            func=AF.Exp, scale=eA[:, ec * N + n:ec * N + n + 1])
                        if debug and idx == 0 and hh == 0 and nn == 0:
                            e0 = prog.step("scalar", fn, s_s, "s")
                            prog.wait("gpsimd", s_s, e0)
                            prog.dma("gpsimd", dbg2[0], dA[0][:, 0:LT], st_sems["dbg"], "st_dbg")
                            prog.dma("gpsimd", dbg2[1], lnts[0], st_sems["dbg"], "st_dbg")
                            dmp2 = prog.dma("gpsimd", dbg2[2][:, 0:EC * N], eA, st_sems["dbg"], "st_dbg")
                            prog.wait("scalar", st_sems["dbg"], dmp2)
                        elif nn == 7:
                            sv = prog.step("scalar", fn, s_s, "s")
                            prog.cnt[f"b_sh_{idx}_{hh}"] = sv
                        else:
                            prog.on("scalar", fn)
                prog.cnt[f"b_ss_{idx}"] = prog.val("s")

                prog.wait("vector", s_b[pp], lv3)
                if ec == 0:
                    prog.wait("vector", s_ld2, bcv)
                prog.on("vector", lambda e, pp=pp: e.tensor_mul(mdtu, lnts[pp], xcvs2[pp]))
                prog.on("vector", lambda e: e.tensor_tensor(
                    out=w3, in0=mdtu.unsqueeze(1).broadcast_to([128, N, LT]),
                    in1=bnb3, op=ALU.mult))
                for hh in range(2):
                    prog.wait("vector", s_s, prog.cnt[f"b_sh_{idx}_{hh}"])
                    for nn in range(8):
                        n = hh * 8 + nn
                        st_col = sstate[:, (b * EC + ec) * N + n:(b * EC + ec) * N + n + 1]
                        ini = 0.0 if lb == 0 else st_col
                        fn = lambda e, hh=hh, nn=nn, n=n, ini=ini: e.tensor_tensor_scan(
                            out=s_tl[:, n * LT:(n + 1) * LT],
                            data0=dA[hh][:, nn * LT:(nn + 1) * LT],
                            data1=w_t[:, n * LT:(n + 1) * LT],
                            initial=ini, op0=ALU.mult, op1=ALU.add)
                        if nn == 7:
                            vv = prog.step("vector", fn, s_v, "v")
                            prog.cnt[f"b_vh_{idx}_{hh}"] = vv
                        else:
                            prog.on("vector", fn)
                if lb != LB - 1:
                    prog.on("vector", lambda e, b=b, ec=ec: e.tensor_copy(
                        sstate[:, (b * EC + ec) * N:(b * EC + ec + 1) * N],
                        s_tl[:, LT - 1::LT]))
                prog.on("vector", lambda e: e.tensor_mul(w_t, s_tl, ccb))
                prog.on("vector", lambda e: e.tensor_reduce(
                    out=yhist, in_=qred, axis=mybir.AxisListType.X, op=ALU.add))
                prog.on("vector", lambda e, pp=pp, ec=ec: e.scalar_tensor_tensor(
                    out=yhist, in0=xcvs2[pp], scalar=dD_sb[:, ec:ec + 1], in1=yhist,
                    op0=ALU.mult, op1=ALU.add))
                prog.wait("vector", st_sems["y"], prog.val("st_y"))
                vv = prog.step("vector", lambda e, pp=pp: e.tensor_mul(
                    yfin[pp], yhist, sgs[pp]), s_v, "v")
                if debug and idx == 0:
                    prog.wait("gpsimd", s_v, vv)
                    prog.dma("gpsimd", scan_d[0][:, 0:8 * LT], dA[0], st_sems["dbg"], "st_dbg")
                    prog.dma("gpsimd", scan_d[1][:, 0:8 * LT], dA[1], st_sems["dbg"], "st_dbg")
                    prog.dma("gpsimd", scan_d[2], s_tl, st_sems["dbg"], "st_dbg")
                    prog.dma("gpsimd", scan_d[3], w_t, st_sems["dbg"], "st_dbg")
                    prog.dma("gpsimd", scan_d[4][:, 0:LT], yhist, st_sems["dbg"], "st_dbg")
                    dmpv = prog.dma("gpsimd", scan_d[5], bnb, st_sems["dbg"], "st_dbg")
                    prog.wait("vector", st_sems["dbg"], dmpv)
                    prog.wait("scalar", st_sems["dbg"], dmpv)
                prog.cnt[f"b_vs_{idx}"] = vv
                prog.wait("gpsimd", s_v, vv)
                stv = prog.dma("gpsimd", y_hb[b, ec, :, lb * LT:(lb + 1) * LT], yfin[pp],
                               st_sems["y"], "st_y")
            for ec in range(EC):
                phase_b_ec(ec)
            prog.cnt[f"b_v_{uu}"] = prog.val("v")

        reset(0)
        ow_stage = [carve(1024), carve(1024)]
        ow_h = carve_h(EC * 1024)
        yblk = [carve_h(EC * LT), carve_h(EC * LT)]
        mo_t = carve(8 * LT)

        store_barrier("sync")
        prog.wait("sync", s_st, prog.val("st"))
        prog.wait("sync", s_v, prog.val("v"))
        prog.wait("sync", s_s, prog.val("s"))

        for half in range(2):
            for ec in range(EC):
                pp = ec % 2
                prog.wait("sync", s_v, prog.cnt.get(f"c1w_{half}_{ec - 2}", 0))
                if half > 0 and ec < 2:
                    prog.wait("sync", s_t, prog.cnt.get("c1_t_last", 0))
                lv = prog.dma("sync", ow_stage[pp],
                              w_out_t[ec * 128:(ec + 1) * 128, half * 1024:(half + 1) * 1024],
                              s_ld, "ld")
                prog.wait("vector", s_ld, lv)
                cv = prog.step("vector", lambda e, pp=pp, ec=ec: e.tensor_copy(
                    ow_h[:, ec * 1024:(ec + 1) * 1024], ow_stage[pp]), s_v, "v")
                prog.cnt[f"c1w_{half}_{ec}"] = cv
            wcast = prog.val("v")
            def phase_c1_unit(uu, half=half, wcast=wcast):
                b, lb = divmod(uu, LB)
                pp = uu % 2
                idx = half * BC * LB + uu
                prog.wait("sync", s_t, prog.cnt.get(f"c1_t_{idx - 2}", 0))
                lv = prog.dma("sync", yblk[pp].rearrange("p (ec l) -> p ec l", ec=EC),
                              y_hb[b, :, :, lb * LT:(lb + 1) * LT].rearrange("ec p l -> p ec l"),
                              s_ld, "ld")
                prog.wait("tensor", s_ld, lv)
                if uu == 0:
                    prog.wait("tensor", s_v, wcast)
                for ho in range(8):
                    pt = psum[ho % 2]
                    prog.wait("tensor", s_v, prog.cnt.get(f"c1_ev_{idx}_{ho - 2}", 0))
                    if ho == 0:
                        prog.wait("vector", s_st, prog.cnt.get("c1_most", 0))
                    for ec in range(EC):
                        fn = lambda e, pt=pt, pp=pp, ec=ec, ho=ho: e.matmul(
                            pt[:], ow_h[:, ec * 1024 + ho * 128:ec * 1024 + (ho + 1) * 128],
                            yblk[pp][:, ec * LT:(ec + 1) * LT],
                            start=(ec == 0), stop=(ec == EC - 1))
                        if ec == EC - 1:
                            tv = prog.step("tensor", fn, s_t, "t")
                            prog.cnt[f"c1_t_{idx}"] = tv
                        else:
                            prog.on("tensor", fn)
                    prog.wait("vector", s_t, tv)
                    ev = prog.step("vector", lambda e, ho=ho, pt=pt: e.tensor_copy(
                        mo_t[:, ho * LT:(ho + 1) * LT], pt[:]), s_v, "v")
                    prog.cnt[f"c1_ev_{idx}_{ho}"] = ev
                prog.wait("gpsimd", s_v, prog.val("v"))
                stv = prog.dma("gpsimd",
                               mo_f[b, half * 8:(half + 1) * 8, :, lb * LT:(lb + 1) * LT]
                               .rearrange("ho p l -> p ho l"),
                               mo_t.rearrange("p (ho l) -> p ho l", ho=8), s_st, "st")
                prog.cnt["c1_most"] = stv
            for uu in range(BC * LB):
                phase_c1_unit(uu)
            prog.cnt["c1_t_last"] = prog.val("t")

        reset(0)
        mo_b = carve(HC * LT)
        x_b = carve(HC * LT)
        sq_b = carve(HC * LT)
        x2_b = carve(HC * LT)
        x2l_b = carve_h(HC * LT)
        row_m2 = carve(512)[0:1, :]
        row_q2 = carve(512)[0:1, :]
        row_r2 = carve(512)[0:1, :]

        mo3 = mo_b.rearrange("p (hc l) -> p hc l", hc=HC)
        xb3 = x_b.rearrange("p (hc l) -> p hc l", hc=HC)
        sqb3 = sq_b.rearrange("p (hc l) -> p hc l", hc=HC)
        x2b3 = x2_b.rearrange("p (hc l) -> p hc l", hc=HC)
        x2l3 = x2l_b.rearrange("p (hc l) -> p hc l", hc=HC)

        prog.wait("sync", s_st, prog.val("st"))
        prog.wait("sync", s_v, prog.val("v"))
        prog.wait("sync", s_t, prog.val("t"))
        prev_v = prog.val("v")
        for uu in range(BC * LB):
            b, lb = divmod(uu, LB)
            prog.wait("sync", s_v, prog.cnt.get(f"c2_v_{uu - 1}", 0))
            prog.wait("sync", s_st, prog.cnt.get("c2_st", 0))
            prog.dma("sync", mo3,
                     mo_f[b, :, :, lb * LT:(lb + 1) * LT].rearrange("hc p l -> p hc l"),
                     s_ld, "ld")
            lv = prog.dma("sync", xb3,
                          x_fm[b, :, lb * LT:(lb + 1) * LT].rearrange("(hc p) l -> p hc l", p=128),
                          s_ld, "ld")
            prog.wait("scalar", s_ld, lv)
            prog.wait("tensor", s_ld, lv)
            ln_block(mo3, sqb3, row_m2, row_q2, row_r2, psum[0], psum[1], psum[2], psum[3],
                     prev_v, HC)
            tv = prog.val("t")
            prog.wait("vector", s_t, tv)
            prog.wait("vector", s_ld, lv)
            prog.on("vector", lambda e: e.tensor_tensor(
                out=sqb3, in0=mo3,
                in1=psum[0][:].unsqueeze(1).broadcast_to([128, HC, 512]), op=ALU.subtract))
            prog.on("vector", lambda e: e.tensor_tensor(
                out=sqb3, in0=sqb3,
                in1=psum[1][:].unsqueeze(1).broadcast_to([128, HC, 512]), op=ALU.mult))
            vv = prog.step("vector", lambda e: e.tensor_add(x2_b, sq_b, x_b), s_v, "v")
            prog.wait("scalar", s_v, vv)
            prog.wait("tensor", s_v, vv)
            ln_block(x2b3, sqb3, row_m2, row_q2, row_r2, psum[4], psum[5], psum[6], psum[7],
                     vv, HC)
            tv = prog.val("t")
            prog.wait("vector", s_t, tv)
            prog.on("vector", lambda e: e.tensor_tensor(
                out=sqb3, in0=x2b3,
                in1=psum[4][:].unsqueeze(1).broadcast_to([128, HC, 512]), op=ALU.subtract))
            vv = prog.step("vector", lambda e: e.tensor_tensor(
                out=x2l3, in0=sqb3,
                in1=psum[5][:].unsqueeze(1).broadcast_to([128, HC, 512]), op=ALU.mult), s_v, "v")
            prog.cnt[f"c2_v_{uu}"] = vv
            prev_v = vv
            prog.wait("gpsimd", s_v, vv)
            prog.dma("gpsimd",
                     x2_f[b, :, :, lb * LT:(lb + 1) * LT].rearrange("hc p l -> p hc l"),
                     x2b3, s_st, "st")
            stv = prog.dma("gpsimd",
                           x2ln_h[b, :, :, lb * LT:(lb + 1) * LT].rearrange("hc p l -> p hc l"),
                           x2l3, s_st, "st")
            prog.cnt["c2_st"] = stv

        reset(0)
        w1_stage = [carve(2048), carve(2048)]
        w1_h = carve_h(HC * 2048)
        w2_st = carve(HC * KF)
        w2_h = carve_h(HC * KF)
        x2l_s = [carve_h(HC * LT), carve_h(HC * LT)]
        h1_t = carve_h(HC * LT)
        ex_t = carve(LT)[0:KF, :]
        al_t = carve(LT)[0:KF, :]

        prog.wait("sync", s_st, prog.val("st"))
        prog.wait("sync", s_v, prog.val("v"))
        prog.wait("sync", s_t, prog.val("t"))
        prog.wait("sync", s_s, prog.val("s"))
        for dc in range(HC):
            pp = dc % 2
            prog.wait("sync", s_v, prog.cnt.get(f"c3w_{dc - 2}", 0))
            lv = prog.dma("sync", w1_stage[pp], w1[dc * 128:(dc + 1) * 128, :], s_ld, "ld")
            prog.wait("vector", s_ld, lv)
            cv = prog.step("vector", lambda e, pp=pp, dc=dc: e.tensor_copy(
                w1_h[:, dc * 2048:(dc + 1) * 2048], w1_stage[pp]), s_v, "v")
            prog.cnt[f"c3w_{dc}"] = cv
        lv = prog.dma("sync", w2_st.rearrange("p (dc k) -> p dc k", k=KF),
                      w2.rearrange("(dc p) k -> p dc k", p=128), s_ld, "ld")
        prog.wait("vector", s_ld, lv)
        wc = prog.step("vector", lambda e: e.tensor_copy(w2_h, w2_st), s_v, "v")

        def phase_c3a_unit(uu):
            b, lb = divmod(uu, LB)
            pp = uu % 2
            prog.wait("sync", s_t, prog.cnt.get(f"c3a_t_{uu - 2}", 0))
            lv = prog.dma("sync", x2l_s[pp].rearrange("p (hc l) -> p hc l", hc=HC),
                          x2ln_h[b, :, :, lb * LT:(lb + 1) * LT].rearrange("hc p l -> p hc l"),
                          s_ld, "ld")
            prog.wait("tensor", s_ld, lv)
            if uu == 0:
                prog.wait("tensor", s_v, wc)
            for eo in range(HC):
                pt = psum[eo % 2]
                prog.wait("tensor", s_s, prog.cnt.get(f"c3a_ev_{uu}_{eo - 2}", 0))
                for dc in range(HC):
                    fn = lambda e, pt=pt, pp=pp, dc=dc, eo=eo: e.matmul(
                        pt[:], w1_h[:, dc * 2048 + eo * 128:dc * 2048 + (eo + 1) * 128],
                        x2l_s[pp][:, dc * LT:(dc + 1) * LT],
                        start=(dc == 0), stop=(dc == HC - 1))
                    if dc == HC - 1:
                        tv = prog.step("tensor", fn, s_t, "t")
                    else:
                        prog.on("tensor", fn)
                prog.wait("scalar", s_t, tv)
                if eo == 0:
                    prog.wait("scalar", s_st, prog.cnt.get("c3a_h1st", 0))
                ev = prog.step("scalar", lambda e, eo=eo, pt=pt: e.activation(
                    out=h1_t[:, eo * LT:(eo + 1) * LT], in_=pt[:], func=AF.Relu,
                    bias=b1_sb[:, eo:eo + 1]), s_s, "s")
                prog.cnt[f"c3a_ev_{uu}_{eo}"] = ev
                prog.wait("tensor", s_s, ev)
                tv2 = prog.step("tensor", lambda e, eo=eo: e.matmul(
                    psum[2][0:KF, :], w2_h[:, eo * KF:(eo + 1) * KF],
                    h1_t[:, eo * LT:(eo + 1) * LT],
                    start=(eo == 0), stop=(eo == HC - 1), skip_group_check=True), s_t, "t")
                prog.cnt[f"c3a_t_{uu}"] = tv2
            tvl = prog.val("t")
            prog.wait("scalar", s_t, tvl)
            sv = prog.step("scalar", lambda e: e.activation(
                out=ex_t, in_=psum[2][0:KF, :], func=AF.Exp, bias=b2_col), s_s, "s")
            prog.wait("tensor", s_s, sv)
            tv3 = prog.step("tensor", lambda e: e.matmul(
                psum[3][0:1, :], one_col8, ex_t, start=True, stop=True), s_t, "t")
            prog.wait("vector", s_t, tv3)
            sv2 = prog.step("vector", lambda e: e.reciprocal(
                al_t[0:1, :], psum[3][0:1, :]), s_v, "v")
            prog.wait("tensor", s_v, sv2)
            tv4 = prog.step("tensor", lambda e: e.matmul(
                psum[4][0:KF, :], one_row1[0:1, 0:KF], al_t[0:1, :], start=True, stop=True), s_t, "t")
            prog.wait("vector", s_t, tv4)
            vv = prog.step("vector", lambda e: e.tensor_mul(al_t, ex_t, psum[4][0:KF, :]), s_v, "v")
            prog.wait("gpsimd", s_v, vv)
            prog.wait("gpsimd", s_s, prog.val("s"))
            stv_h = prog.dma("gpsimd",
                             h1_h[b, :, :, lb * LT:(lb + 1) * LT].rearrange("hc p l -> p hc l"),
                             h1_t.rearrange("p (hc l) -> p hc l", hc=HC), s_st, "st")
            prog.cnt["c3a_h1st"] = stv_h
            stv = prog.dma("gpsimd", al_f[b, :, lb * LT:(lb + 1) * LT], al_t, s_st, "st")
            prog.wait("scalar", s_st, stv)
        for uu in range(BC * LB):
            phase_c3a_unit(uu)

        reset(0)
        vt_st = [carve(KR), carve(KR)]
        vt_h = carve_h(HC * KR)
        ut_st = [carve(2048), carve(2048)]
        ut_h = carve_h(KRC * 2048)
        x2l_s2 = [carve_h(HC * LT), carve_h(HC * LT)]
        avx_h = carve_h(KRC * LT)
        alb = carve(KRC * LT)
        fo_t = carve(HC * LT)
        x2_s = carve(HC * LT)
        sq_c = carve(HC * LT)
        row_m3 = carve(512)[0:1, :]
        row_q3 = carve(512)[0:1, :]
        row_r3 = carve(512)[0:1, :]

        fo3 = fo_t.rearrange("p (hc l) -> p hc l", hc=HC)
        sqc3 = sq_c.rearrange("p (hc l) -> p hc l", hc=HC)
        x2s3 = x2_s.rearrange("p (hc l) -> p hc l", hc=HC)

        prog.wait("sync", s_v, prog.val("v"))
        prog.wait("sync", s_s, prog.val("s"))
        prog.wait("sync", s_t, prog.val("t"))
        for dc in range(HC):
            pp = dc % 2
            prog.wait("sync", s_v, prog.cnt.get(f"c3bw_{dc - 2}", 0))
            lv = prog.dma("sync", vt_st[pp], vt[dc * 128:(dc + 1) * 128, :], s_ld, "ld")
            prog.wait("vector", s_ld, lv)
            cv = prog.step("vector", lambda e, pp=pp, dc=dc: e.tensor_copy(
                vt_h[:, dc * KR:(dc + 1) * KR], vt_st[pp]), s_v, "v")
            prog.cnt[f"c3bw_{dc}"] = cv
        for kc in range(KRC):
            pp = kc % 2
            prog.wait("sync", s_v, prog.cnt.get(f"c3bu_{kc - 2}", 0))
            lv = prog.dma("sync", ut_st[pp], ut[kc * 128:(kc + 1) * 128, :], s_ld, "ld")
            prog.wait("vector", s_ld, lv)
            cv = prog.step("vector", lambda e, pp=pp, kc=kc: e.tensor_copy(
                ut_h[:, kc * 2048:(kc + 1) * 2048], ut_st[pp]), s_v, "v")
            prog.cnt[f"c3bu_{kc}"] = cv
        wc2 = prog.val("v")
        prog.wait("sync", s_st, prog.val("st"))

        prev_v_cell = [prog.val("v")]
        def phase_c3b_unit(uu):
            prev_v = prev_v_cell[0]
            b, lb = divmod(uu, LB)
            pp = uu % 2
            prog.wait("sync", s_t, prog.cnt.get(f"c3b_t_{uu - 2}", 0))
            prog.wait("sync", s_v, prog.cnt.get(f"c3b_v_{uu - 1}", 0))
            prog.wait("sync", s_st, prog.cnt.get("c3b_ost", 0))
            prog.dma("sync", x2l_s2[pp].rearrange("p (hc l) -> p hc l", hc=HC),
                     x2ln_h[b, :, :, lb * LT:(lb + 1) * LT].rearrange("hc p l -> p hc l"),
                     s_ld, "ld")
            prog.dma("sync", x2s3,
                     x2_f[b, :, :, lb * LT:(lb + 1) * LT].rearrange("hc p l -> p hc l"),
                     s_ld, "ld")
            for kc in range(KRC):
                for kk in range(2):
                    lva = prog.dma("sync",
                                   alb[kk * RF:(kk + 1) * RF, kc * LT:(kc + 1) * LT],
                                   al_f[b, 2 * kc + kk, lb * LT:(lb + 1) * LT]
                                   .unsqueeze(0).broadcast_to([RF, LT]),
                                   s_ld, "ld")
            prog.wait("tensor", s_ld, lva)
            if uu == 0:
                prog.wait("tensor", s_v, wc2)
            for kc in range(KRC):
                prog.wait("tensor", s_v, prog.cnt.get(f"c3b_av_{uu - 1}_{kc}", 0))
                if kc >= 2:
                    prog.wait("tensor", s_s, prog.cnt.get(f"c3b_lnsq_{uu - 1}", 0))
                for dc in range(HC):
                    fn = lambda e, kc=kc, dc=dc, pp=pp: e.matmul(
                        psum[kc][:], vt_h[:, dc * KR + kc * 128:dc * KR + (kc + 1) * 128],
                        x2l_s2[pp][:, dc * LT:(dc + 1) * LT],
                        start=(dc == 0), stop=(dc == HC - 1))
                    if dc == HC - 1:
                        tv = prog.step("tensor", fn, s_t, "t")
                    else:
                        prog.on("tensor", fn)
                prog.wait("vector", s_t, tv)
                av = prog.step("vector", lambda e, kc=kc: e.tensor_mul(
                    avx_h[:, kc * LT:(kc + 1) * LT], psum[kc][:],
                    alb[:, kc * LT:(kc + 1) * LT]), s_v, "v")
                prog.cnt[f"c3b_av_{uu}_{kc}"] = av
            avv = prog.val("v")
            prog.wait("tensor", s_v, avv)
            for dc in range(HC):
                pt = psum[4 + dc % 2]
                prog.wait("tensor", s_v, prog.cnt.get(f"c3b_fo_{uu}_{dc - 2}", 0))
                for kc in range(KRC):
                    fn = lambda e, pt=pt, dc=dc, kc=kc: e.matmul(
                        pt[:], ut_h[:, kc * 2048 + dc * 128:kc * 2048 + (dc + 1) * 128],
                        avx_h[:, kc * LT:(kc + 1) * LT],
                        start=(kc == 0), stop=(kc == KRC - 1))
                    if kc == KRC - 1:
                        tv = prog.step("tensor", fn, s_t, "t")
                        prog.cnt[f"c3b_t_{uu}"] = tv
                    else:
                        prog.on("tensor", fn)
                prog.wait("vector", s_t, tv)
                fv = prog.step("vector", lambda e, dc=dc, pt=pt: e.tensor_copy(
                    fo_t[:, dc * LT:(dc + 1) * LT], pt[:]), s_v, "v")
                prog.cnt[f"c3b_fo_{uu}_{dc}"] = fv
            fvl = prog.val("v")
            prog.wait("scalar", s_v, fvl)
            prog.wait("tensor", s_v, fvl)
            ln_block(fo3, sqc3, row_m3, row_q3, row_r3, psum[6], psum[7], psum[2], psum[3],
                     prev_v, HC)
            prog.cnt[f"c3b_lnsq_{uu}"] = prog.val("s")
            tv = prog.val("t")
            prog.wait("vector", s_t, tv)
            prog.wait("vector", s_ld, lva)
            prog.on("vector", lambda e: e.tensor_tensor(
                out=sqc3, in0=fo3,
                in1=psum[6][:].unsqueeze(1).broadcast_to([128, HC, 512]), op=ALU.subtract))
            prog.on("vector", lambda e: e.tensor_tensor(
                out=sqc3, in0=sqc3,
                in1=psum[7][:].unsqueeze(1).broadcast_to([128, HC, 512]), op=ALU.mult))
            vv = prog.step("vector", lambda e: e.tensor_add(x2_s, sq_c, x2_s), s_v, "v")
            prog.cnt[f"c3b_v_{uu}"] = vv
            prev_v_cell[0] = vv
            prog.wait("gpsimd", s_v, vv)
            stv = prog.dma("gpsimd",
                           out_fm[b, :, lb * LT:(lb + 1) * LT].rearrange("(hc p) l -> p hc l", p=128),
                           x2s3, s_st, "st")
            prog.cnt["c3b_ost"] = stv
        for uu in range(BC * LB):
            phase_c3b_unit(uu)

        prog.wait("sync", s_st, prog.val("st"))

        @block.sync
        def _(eng):
            for fn in prog.ops["sync"]:
                fn(eng)

        @block.tensor
        def _(eng):
            for fn in prog.ops["tensor"]:
                fn(eng)

        @block.vector
        def _(eng):
            for fn in prog.ops["vector"]:
                fn(eng)

        @block.scalar
        def _(eng):
            for fn in prog.ops["scalar"]:
                fn(eng)

        @block.gpsimd
        def _(eng):
            for fn in prog.ops["gpsimd"]:
                fn(eng)

    return nc


def _prep_core_inputs(inputs, core):
    d = inputs
    x = d["L0_attn_hidden_states"][2 * core:2 * core + 2]
    conv_w = d["L0_attn_conv_w"][:, 0, :]
    m = {
        "x_fm": x.transpose(0, 2, 1),
        "w_in_t": d["L0_attn_in_proj_w"].T,
        "conv_wt": conv_w.reshape(EC, 128, 4).transpose(1, 0, 2).reshape(128, EC * 4),
        "conv_bt": d["L0_attn_conv_b"].reshape(EC, 128).T,
        "w_xp_t": d["L0_attn_x_proj_w"].T,
        "w_dt_t": d["L0_attn_dt_proj_w"].T,
        "dt_bt": d["L0_attn_dt_proj_b"].reshape(EC, 128).T,
        "alog_t": d["L0_attn_A_log"].reshape(EC, 128, N).transpose(1, 0, 2).reshape(128, EC * N),
        "d_t": d["L0_attn_D"].reshape(EC, 128).T,
        "w_out_t": d["L0_attn_out_proj_w"].T,
        "w1": d["L0_ffn_W1"],
        "b1_t": d["L0_ffn_b1"].reshape(HC, 128).T,
        "w2": d["L0_ffn_W2"],
        "b2_t": d["L0_ffn_b2"].reshape(KF, 1),
        "vt": d["L0_ffn_V"].transpose(2, 0, 1).reshape(H, KR),
        "ut": d["L0_ffn_U"].transpose(0, 2, 1).reshape(KR, H),
    }
    return {k: np.ascontiguousarray(np.asarray(v, dtype=np.float32)) for k, v in m.items()}


def kernel(**inputs):
    nc = build_nc(debug=False)
    in_maps = [_prep_core_inputs(inputs, c) for c in range(NCORES)]
    res = run_bass_kernel_spmd(nc, in_maps, core_ids=list(range(NCORES)))
    outs = [r["out_fm"] for r in res.results]
    full = np.concatenate([o.transpose(0, 2, 1) for o in outs], axis=0)
    return np.ascontiguousarray(full.astype(np.float32))


# version 04
# speedup vs baseline: 1.1897x; 1.1897x over previous
import numpy as np
import concourse.bass as bass
import concourse.mybir as mybir
from concourse.bass_utils import run_bass_kernel_spmd

F32 = mybir.dt.float32
BF16 = mybir.dt.bfloat16
AF = mybir.ActivationFunctionType
ALU = mybir.AluOpType

NCORES = 8
BC = 2
L = 2048
H = 2048
HC = 16
E = 4096
EC = 32
EO = 64
N = 16
R = 128
LB = 4
LT = 512
KF = 8
RF = 64
KR = 512
KRC = 4

ENGINES = ("sync", "tensor", "vector", "scalar", "gpsimd")


class Prog:
    def __init__(self):
        self.ops = {e: [] for e in ENGINES}
        self.cnt = {}

    def on(self, eng, fn):
        self.ops[eng].append(fn)

    def bump(self, name, k=1):
        v = self.cnt.get(name, 0) + k
        self.cnt[name] = v
        return v

    def val(self, name):
        return self.cnt.get(name, 0)

    def wait(self, eng, sem_h, v):
        if v > 0:
            self.on(eng, lambda e, s=sem_h, vv=v: e.wait_ge(s, vv))

    def dma(self, eng, out, in_, sem_h, name):
        v = self.bump(name, 16)
        self.on(eng, lambda e, o=out, i=in_, s=sem_h: e.dma_start(out=o, in_=i).then_inc(s, 16))
        return v

    def step(self, eng, fn, sem_h, name):
        v = self.bump(name, 1)
        self.on(eng, lambda e, f=fn, s=sem_h: f(e).then_inc(s, 1))
        return v

    def run(self, eng, fn):
        self.on(eng, fn)


def build_nc(debug=False):
    nc = bass.Bass("TRN2", target_bir_lowering=False)

    x_fm = nc.dram_tensor("x_fm", [BC, H, L], F32, kind="ExternalInput").ap()
    w_in_t = nc.dram_tensor("w_in_t", [H, 2 * E], F32, kind="ExternalInput").ap()
    conv_wt = nc.dram_tensor("conv_wt", [128, EC * 4], F32, kind="ExternalInput").ap()
    conv_bt = nc.dram_tensor("conv_bt", [128, EC], F32, kind="ExternalInput").ap()
    w_xp_t = nc.dram_tensor("w_xp_t", [E, 160], F32, kind="ExternalInput").ap()
    w_dt_t = nc.dram_tensor("w_dt_t", [R, E], F32, kind="ExternalInput").ap()
    dt_bt = nc.dram_tensor("dt_bt", [128, EC], F32, kind="ExternalInput").ap()
    alog_t = nc.dram_tensor("alog_t", [128, EC * N], F32, kind="ExternalInput").ap()
    d_t = nc.dram_tensor("d_t", [128, EC], F32, kind="ExternalInput").ap()
    w_out_t = nc.dram_tensor("w_out_t", [E, H], F32, kind="ExternalInput").ap()
    w1 = nc.dram_tensor("w1", [H, H], F32, kind="ExternalInput").ap()
    b1_t = nc.dram_tensor("b1_t", [128, HC], F32, kind="ExternalInput").ap()
    w2 = nc.dram_tensor("w2", [H, KF], F32, kind="ExternalInput").ap()
    b2_t = nc.dram_tensor("b2_t", [KF, 1], F32, kind="ExternalInput").ap()
    vt = nc.dram_tensor("vt", [H, KR], F32, kind="ExternalInput").ap()
    ut = nc.dram_tensor("ut", [KR, H], F32, kind="ExternalInput").ap()
    ident_f = nc.dram_tensor("ident_f", [128, 128], F32, kind="ExternalInput").ap()

    out_fm = nc.dram_tensor("out_fm", [BC, H, L], F32, kind="ExternalOutput").ap()
    kind_s = "ExternalOutput" if debug else "Internal"
    xcv_f = nc.dram_tensor("xcv_f", [BC, EC, 128, L], F32, kind=kind_s).ap()
    xcv_h = nc.dram_tensor("xcv_h", [BC, EC, 128, L], BF16, kind=kind_s).ap()
    sg_f = nc.dram_tensor("sg_f", [BC, EC, 128, L], F32, kind=kind_s).ap()
    lnt_f = nc.dram_tensor("lnt_f", [BC, EC, 128, L], F32, kind=kind_s).ap()
    bn_rh = nc.dram_tensor("bn_rh", [BC, N, L], BF16, kind=kind_s).ap()
    c_rh = nc.dram_tensor("c_rh", [BC, N, L], BF16, kind=kind_s).ap()
    ya_f = nc.dram_tensor("ya_f", [EC, 128, L], BF16, kind=kind_s).ap()
    y_hb = nc.dram_tensor("y_hb", [BC, EC, 128, L], BF16, kind=kind_s).ap()
    mo_f = nc.dram_tensor("mo_f", [BC, HC, 128, L], F32, kind=kind_s).ap()
    x2_f = nc.dram_tensor("x2_f", [BC, HC, 128, L], F32, kind=kind_s).ap()
    x2ln_h = nc.dram_tensor("x2ln_h", [BC, HC, 128, L], BF16, kind=kind_s).ap()
    h1_h = nc.dram_tensor("h1_h", [BC, HC, 128, L], BF16, kind=kind_s).ap()
    al_f = nc.dram_tensor("al_f", [BC, KF, L], F32, kind=kind_s).ap()
    if debug:
        xln_d = nc.dram_tensor("xln_d", [128, HC * BC * L], BF16, kind="ExternalOutput").ap()
        u_d = nc.dram_tensor("u_d", [BC, EC, 128, L], F32, kind="ExternalOutput").ap()
        scan_d = nc.dram_tensor("scan_d", [6, 128, N * LT], F32, kind="ExternalOutput").ap()
        dbg2 = nc.dram_tensor("dbg2", [3, 128, 512], F32, kind="ExternalOutput").ap()

    prog = Prog()
    SB_F32 = 52000

    from contextlib import ExitStack
    with ExitStack() as _stack:
        arena_t = _stack.enter_context(nc.sbuf_tensor("arena", [128, SB_F32], F32))
        pers_t = _stack.enter_context(nc.sbuf_tensor("persist", [128, 1200], F32))
        psum = [_stack.enter_context(nc.psum_tensor(f"pb{i}", [128, 512], F32)) for i in range(8)]
        pb0, pb1, pb2, pb3, pb4, pb5, pb6, pb7 = psum
        s_c = _stack.enter_context(nc.semaphore("s_c"))
        s_ld2 = _stack.enter_context(nc.semaphore("s_ld2"))
        s_w = [_stack.enter_context(nc.semaphore(f"s_w{i}")) for i in range(2)]
        s_b = [_stack.enter_context(nc.semaphore(f"s_b{i}")) for i in range(2)]
        st_sems = {k: _stack.enter_context(nc.semaphore(f"st_{k}"))
                   for k in ("sg", "xcv", "lnt", "bc", "ya", "y", "mo", "x2", "h1", "out", "dbg")}
        s_ld = _stack.enter_context(nc.semaphore("s_ld"))
        s_st = _stack.enter_context(nc.semaphore("s_st"))
        s_t = _stack.enter_context(nc.semaphore("s_t"))
        s_v = _stack.enter_context(nc.semaphore("s_v"))
        s_s = _stack.enter_context(nc.semaphore("s_s"))
        block = _stack.enter_context(nc.Block())

        def store_barrier(eng):
            for k, sem in st_sems.items():
                prog.wait(eng, sem, prog.val("st_" + k))
        psum = [pb0, pb1, pb2, pb3, pb4, pb5, pb6, pb7]
        arena = arena_t.ap()
        pers = pers_t.ap()

        po = [0]

        def pcarve(n):
            a = po[0]
            po[0] += n
            assert po[0] <= 1200
            return pers[:, a:a + n]

        eA = pcarve(EC * N)
        cw_sb = pcarve(EC * 4)
        cb_sb = pcarve(EC)
        dtb_sb = pcarve(EC)
        dtb_neg = pcarve(EC)
        dD_sb = pcarve(EC)
        b1_sb = pcarve(HC)
        b2_col = pcarve(1)[0:KF, :]
        ones_col = pcarve(1)
        one_row1 = pcarve(128)[0:1, :]
        one_col8 = pcarve(1)[0:KF, :]
        eps_col = pcarve(1)

        ao = [0]

        def carve(n):
            a = ao[0]
            ao[0] += n
            assert ao[0] <= SB_F32, f"arena overflow {ao[0]}"
            return arena[:, a:a + n]

        def carve_h(n_bf16):
            assert n_bf16 % 2 == 0
            return carve(n_bf16 // 2).bitcast(BF16)

        def reset(keep=0):
            ao[0] = keep

        prog.dma("sync", eA, alog_t, s_c, "c")
        prog.dma("sync", cw_sb, conv_wt, s_c, "c")
        prog.dma("sync", cb_sb, conv_bt, s_c, "c")
        prog.dma("sync", dtb_sb, dt_bt, s_c, "c")
        prog.dma("sync", dD_sb, d_t, s_c, "c")
        prog.dma("sync", b1_sb, b1_t, s_c, "c")
        cldv = prog.dma("sync", b2_col, b2_t, s_c, "c")
        prog.wait("scalar", s_c, cldv)
        prog.step("scalar", lambda e: e.activation(out=eA, in_=eA, func=AF.Exp), s_s, "s")
        prog.step("scalar", lambda e: e.activation(out=dtb_neg, in_=dtb_sb, func=AF.Copy, scale=-1.0), s_s, "s")
        prog.run("vector", lambda e: e.memset(ones_col, 1.0))
        prog.run("vector", lambda e: e.memset(one_row1, 1.0))
        prog.run("vector", lambda e: e.memset(one_col8, 1.0))
        prog.run("vector", lambda e: e.memset(eps_col, 1e-5))
        consts_v = prog.step("vector", lambda e: e.engine_nop(), s_v, "v")

        def ln_block(xt3, sq3, row_m, row_q, row_r, pm, pr, pst_a, pst_b, prev_v, nch):
            prog.wait("scalar", s_v, prev_v)
            sv = prog.step("scalar", lambda e: e.activation(out=sq3, in_=xt3, func=AF.Square), s_s, "s")
            prog.wait("tensor", s_s, sv)
            for c in range(nch):
                fn = lambda e, c=c: e.matmul(pst_a[0:1, :], ones_col, xt3[:, c, :],
                                             start=(c == 0), stop=(c == nch - 1))
                prog.on("tensor", fn)
            for c in range(nch):
                fn = lambda e, c=c: e.matmul(pst_b[0:1, :], ones_col, sq3[:, c, :],
                                             start=(c == 0), stop=(c == nch - 1))
                if c == nch - 1:
                    prog.step("tensor", fn, s_t, "t")
                else:
                    prog.on("tensor", fn)
            tv = prog.val("t")
            prog.wait("scalar", s_t, tv)
            prog.step("scalar", lambda e: e.activation(out=row_m, in_=pst_a[0:1, :],
                                                       func=AF.Copy, scale=1.0 / (nch * 128)), s_s, "s")
            prog.step("scalar", lambda e: e.activation(out=row_q, in_=pst_b[0:1, :],
                                                       func=AF.Copy, scale=1.0 / (nch * 128)), s_s, "s")
            sv2 = prog.step("scalar", lambda e: e.activation(out=row_r, in_=row_m, func=AF.Square), s_s, "s")
            prog.wait("vector", s_s, sv2)
            vv = prog.step("vector", lambda e: e.tensor_sub(row_q, row_q, row_r), s_v, "v")
            prog.wait("scalar", s_v, vv)
            sv3 = prog.step("scalar", lambda e: e.activation(out=row_r, in_=row_q, func=AF.Sqrt, bias=eps_col[0:1, :]), s_s, "s")
            prog.wait("vector", s_s, sv3)
            vv2 = prog.step("vector", lambda e: e.reciprocal(row_r, row_r), s_v, "v")
            prog.wait("tensor", s_v, vv2)
            prog.wait("tensor", s_v, prev_v)
            prog.on("tensor", lambda e: e.matmul(pm[:], one_row1, row_m, start=True, stop=True))
            prog.step("tensor", lambda e: e.matmul(pr[:], one_row1, row_r, start=True, stop=True), s_t, "t")
            return prog.val("t")

        reset()
        xln_h = carve_h(HC * (BC * L))
        xt_a = carve(HC * LT)
        sq_a = carve(HC * LT)
        row_m = carve(512)[0:1, :]
        row_q = carve(512)[0:1, :]
        row_r = carve(512)[0:1, :]
        A0_KEEP = (HC * (BC * L)) // 2

        xt3 = xt_a.rearrange("p (hc l) -> p hc l", hc=HC)
        sq3 = sq_a.rearrange("p (hc l) -> p hc l", hc=HC)
        xln3 = xln_h.rearrange("p (hc t) -> p hc t", hc=HC)
        prev_v = consts_v
        for uu in range(BC * LB):
            b, lb = divmod(uu, LB)
            prog.wait("sync", s_v, prev_v)
            lv = prog.dma("sync", xt3,
                          x_fm[b, :, lb * LT:(lb + 1) * LT].rearrange("(hc p) l -> p hc l", p=128),
                          s_c, "c")
            prog.wait("scalar", s_c, lv)
            prog.wait("tensor", s_c, lv)
            ln_block(xt3, sq3, row_m, row_q, row_r, psum[0], psum[1], psum[2], psum[3],
                     prev_v, HC)
            tv = prog.val("t")
            prog.wait("vector", s_t, tv)
            prog.on("vector", lambda e: e.tensor_tensor(
                out=sq3, in0=xt3,
                in1=psum[0][:].unsqueeze(1).broadcast_to([128, HC, 512]),
                op=ALU.subtract))
            dst = xln3[:, :, uu * LT:(uu + 1) * LT]
            prev_v = prog.step("vector", lambda e, d=dst: e.tensor_tensor(
                out=d, in0=sq3,
                in1=psum[1][:].unsqueeze(1).broadcast_to([128, HC, 512]),
                op=ALU.mult), s_v, "v")
        ln1_done_v = prev_v
        if debug:
            prog.wait("gpsimd", s_v, ln1_done_v)
            prog.dma("gpsimd", xln_d, xln_h, st_sems["dbg"], "st_dbg")

        reset(A0_KEEP)
        wbuf = [carve(16 * 128), carve(16 * 128)]
        wbuf_h = [carve_h(16 * 128), carve_h(16 * 128)]
        sg_t = [carve(LT), carve(LT)]
        A1_GATE_END = ao[0]
        u_row = [carve(3 + L), carve(3 + L)]
        c0 = carve(L)
        c1 = carve(L)
        xcv_t = [carve(L)]
        xcvh_t = [carve_h(L)]

        prog.run("vector", lambda e: e.memset(u_row[0][:, 0:3], 0.0))
        prog.run("vector", lambda e: e.memset(u_row[1][:, 0:3], 0.0))

        wb3_src = w_in_t
        def phase_a1(eo):
            pp = eo % 2
            t_prev2 = prog.cnt.get(f"a1_t_{eo - 2}", 0)
            prog.wait("sync", s_t, t_prev2)
            if eo < 2:
                prog.wait("sync", s_v, ln1_done_v)
            lv = prog.dma("sync", wbuf[pp].rearrange("p (hc e) -> p hc e", e=128),
                          wb3_src[:, eo * 128:(eo + 1) * 128].rearrange("(hc p) e -> p hc e", p=128),
                          s_w[pp], f"w{pp}")
            prog.wait("scalar", s_w[pp], lv)
            prog.wait("scalar", s_t, t_prev2)
            cv = prog.step("scalar", lambda e, pp=pp: e.activation(
                out=wbuf_h[pp], in_=wbuf[pp], func=AF.Copy), s_s, "s")

            prog.wait("tensor", s_s, cv)
            if eo == 0:
                prog.wait("tensor", s_v, ln1_done_v)
            is_u = eo < EC
            ec = eo if is_u else eo - EC
            for b in range(BC):
                for lb in range(LB):
                    if is_u:
                        pt = psum[b * LB + lb]
                        w_ev = prog.cnt.get(f"a1_ev_{eo - 1}_{b}_{lb}", None)
                    else:
                        bank = 4 + (b * LB + lb) % 4
                        pt = psum[bank]
                        w_ev = prog.cnt.get(f"a1_gbank_{bank}", None)
                    if w_ev is not None:
                        prog.wait("tensor", s_v if w_ev[0] == "v" else s_s, w_ev[1])
                    for hc in range(HC):
                        fn = lambda e, pt=pt, pp=pp, hc=hc, b=b, lb=lb: e.matmul(
                            pt[:], wbuf_h[pp][:, hc * 128:(hc + 1) * 128],
                            xln3[:, hc, (b * LB + lb) * LT:(b * LB + lb + 1) * LT],
                            start=(hc == 0), stop=(hc == HC - 1))
                        if hc == HC - 1:
                            tv = prog.step("tensor", fn, s_t, "t")
                            prog.cnt[f"a1_t_{eo}"] = tv
                        else:
                            prog.on("tensor", fn)
                    if is_u:
                        prog.wait("vector", s_t, tv)
                        ev = prog.step("vector", lambda e, b=b, lb=lb, pt=pt: e.tensor_copy(
                            u_row[b][:, 3 + lb * LT: 3 + (lb + 1) * LT], pt[:]), s_v, "v")
                        prog.cnt[f"a1_ev_{eo}_{b}_{lb}"] = ("v", ev)
                    else:
                        gp = (eo * BC * LB + b * LB + lb) % 2
                        prog.wait("scalar", st_sems["sg"], prog.cnt.get(f"a1_sgst_{gp}", 0))
                        prog.wait("scalar", s_t, tv)
                        ev = prog.step("scalar", lambda e, gp=gp, pt=pt: e.activation(
                            out=sg_t[gp], in_=pt[:], func=AF.Silu), s_s, "s")
                        prog.cnt[f"a1_gbank_{bank}"] = ("s", ev)
                        prog.wait("gpsimd", s_s, ev)
                        stv = prog.dma("gpsimd", sg_f[b, ec, :, lb * LT:(lb + 1) * LT],
                                       sg_t[gp], st_sems["sg"], "st_sg")
                        prog.cnt[f"a1_sgst_{gp}"] = stv
                        prog.cnt[f"a1_sgst_ec_{ec}"] = stv
            if is_u:
                for b in range(BC):
                    if debug:
                        lastev = prog.cnt[f"a1_ev_{eo}_{b}_{LB - 1}"][1]
                        prog.wait("gpsimd", s_v, lastev)
                        prog.dma("gpsimd", u_d[b, ec], u_row[b][:, 3:3 + L], st_sems["dbg"], "st_dbg")
                    pp2 = 0
                    prog.wait("vector", st_sems["xcv"], prog.val("st_xcv"))
                    prog.on("vector", lambda e, b=b, ec=ec: e.tensor_scalar_mul(
                        c0, u_row[b][:, 0:L], cw_sb[:, ec * 4 + 0:ec * 4 + 1]))
                    prog.on("vector", lambda e, b=b, ec=ec: e.scalar_tensor_tensor(
                        out=c1, in0=u_row[b][:, 1:1 + L], scalar=cw_sb[:, ec * 4 + 1:ec * 4 + 2],
                        in1=c0, op0=ALU.mult, op1=ALU.add))
                    prog.on("vector", lambda e, b=b, ec=ec: e.scalar_tensor_tensor(
                        out=c0, in0=u_row[b][:, 2:2 + L], scalar=cw_sb[:, ec * 4 + 2:ec * 4 + 3],
                        in1=c1, op0=ALU.mult, op1=ALU.add))
                    vv = prog.step("vector", lambda e, b=b, ec=ec: e.scalar_tensor_tensor(
                        out=c1, in0=u_row[b][:, 3:3 + L], scalar=cw_sb[:, ec * 4 + 3:ec * 4 + 4],
                        in1=c0, op0=ALU.mult, op1=ALU.add), s_v, "v")
                    prog.wait("scalar", s_v, vv)
                    sv = prog.step("scalar", lambda e, pp2=pp2, ec=ec: e.activation(
                        out=xcv_t[pp2], in_=c1, func=AF.Silu, bias=cb_sb[:, ec:ec + 1]), s_s, "s")
                    prog.wait("vector", s_s, sv)
                    vv2 = prog.step("vector", lambda e, pp2=pp2: e.tensor_copy(
                        xcvh_t[pp2], xcv_t[pp2]), s_v, "v")
                    prog.wait("gpsimd", s_v, vv2)
                    prog.dma("gpsimd", xcv_f[b, ec], xcv_t[pp2], st_sems["xcv"], "st_xcv")
                    stv = prog.dma("gpsimd", xcv_h[b, ec], xcvh_t[pp2], st_sems["xcv"], "st_xcv")

        for eo in range(EC):
            phase_a1(eo)
        a1u_xcv = prog.val("st_xcv")
        a1u_v = prog.val("v")
        a1u_s = prog.val("s")

        reset(A1_GATE_END)
        wstage = carve(2560)
        xp_wh = carve_h(EC * 160)
        dtlow_h = carve_h(BC * L)
        xcvs = [carve_h(LT), carve_h(LT)]
        bc_ev = carve_h(LT)
        dt_wh = carve_h(EC * 128)
        lnt_t = [carve(LT), carve(LT)]
        sig_t = carve(LT)

        prog.wait("sync", s_v, a1u_v)
        prog.wait("sync", s_s, a1u_s)
        prog.wait("sync", st_sems["xcv"], a1u_xcv)
        for hh in range(2):
            lv = prog.dma("sync", wstage.rearrange("p (ec r) -> p ec r", r=160),
                          w_xp_t[hh * 2048:(hh + 1) * 2048, :]
                          .rearrange("(ec p) r -> p ec r", p=128), s_c, "c")
            prog.wait("scalar", s_c, lv)
            cv = prog.step("scalar", lambda e, hh=hh: e.activation(
                out=xp_wh[:, hh * 2560:(hh + 1) * 2560], in_=wstage,
                func=AF.Copy), s_s, "s")
            prog.wait("sync", s_s, cv)
        xpcv = cv
        for hh in range(2):
            lv = prog.dma("sync", wstage[:, 0:2048], w_dt_t[:, hh * 2048:(hh + 1) * 2048],
                          s_c, "c")
            prog.wait("scalar", s_c, lv)
            dtcv = prog.step("scalar", lambda e, hh=hh: e.activation(
                out=dt_wh[:, hh * 2048:(hh + 1) * 2048], in_=wstage[:, 0:2048],
                func=AF.Copy), s_s, "s")
            if hh == 0:
                prog.wait("sync", s_s, dtcv)

        def phase_a2_unit(uu):
            b, lb = divmod(uu, LB)
            for ec in range(EC):
                pp = ec % 2
                idx = uu * EC + ec
                prog.wait("sync", s_t, prog.cnt.get(f"a2_t_{idx - 2}", 0))
                lv = prog.dma("sync", xcvs[pp], xcv_h[b, ec, :, lb * LT:(lb + 1) * LT],
                              s_b[pp], f"b{pp}")
                prog.wait("tensor", s_b[pp], lv)
                if idx == 0:
                    prog.wait("tensor", s_s, xpcv)
                if ec == 0:
                    prog.wait("tensor", s_v, prog.cnt.get(f"a2_ev_{uu - 1}", 0))
                    prog.wait("tensor", s_s, prog.cnt.get(f"a2_es_{uu - 1}", 0))
                prog.on("tensor", lambda e, pp=pp, ec=ec: e.matmul(
                    psum[0][:], xp_wh[:, ec * 160:ec * 160 + 128], xcvs[pp],
                    start=(ec == 0), stop=(ec == EC - 1), skip_group_check=True))
                tv = prog.step("tensor", lambda e, pp=pp, ec=ec: e.matmul(
                    psum[1][0:32, :], xp_wh[:, ec * 160 + 128:(ec + 1) * 160], xcvs[pp],
                    start=(ec == 0), stop=(ec == EC - 1), skip_group_check=True), s_t, "t")
                prog.cnt[f"a2_t_{idx}"] = tv
            prog.wait("vector", s_t, tv)
            ev = prog.step("vector", lambda e, uu=uu: e.tensor_copy(
                dtlow_h[:, uu * LT:(uu + 1) * LT], psum[0][:]), s_v, "v")
            prog.cnt[f"a2_ev_{uu}"] = ev
            prog.wait("scalar", s_t, tv)
            prog.wait("scalar", st_sems["bc"], prog.val("st_bc"))
            prog.step("scalar", lambda e: e.activation(
                out=bc_ev[0:32, :], in_=psum[1][0:32, :], func=AF.Copy), s_s, "s")
            es = prog.step("scalar", lambda e: e.activation(
                out=bc_ev[0:16, :], in_=bc_ev[0:16, :], func=AF.Copy, scale=-1.0), s_s, "s")
            prog.cnt[f"a2_es_{uu}"] = es
            prog.wait("gpsimd", s_s, es)
            prog.dma("gpsimd", bn_rh[b, :, lb * LT:(lb + 1) * LT], bc_ev[0:16, :], st_sems["bc"], "st_bc")
            stv = prog.dma("gpsimd", c_rh[b, :, lb * LT:(lb + 1) * LT], bc_ev[16:32, :], st_sems["bc"], "st_bc")

        def phase_a3_unit(uu):
            b, lb = divmod(uu, LB)
            prog.wait("tensor", s_v, prog.cnt[f"a2_ev_{uu}"])
            if uu == 0:
                prog.wait("tensor", s_s, dtcv)
            for ec in range(EC):
                idx = uu * EC + ec
                pp = idx % 2
                pt = psum[2 + pp]
                prog.wait("tensor", s_s, prog.cnt.get(f"a3_ev_{idx - 2}", 0))
                tv = prog.step("tensor", lambda e, pt=pt, ec=ec, uu=uu: e.matmul(
                    pt[:], dt_wh[:, ec * 128:(ec + 1) * 128],
                    dtlow_h[:, uu * LT:(uu + 1) * LT], start=True, stop=True), s_t, "t")
                prog.wait("scalar", s_t, tv)
                prog.wait("scalar", st_sems["lnt"], prog.cnt.get(f"a3_lntst_{pp}", 0))
                prog.step("scalar", lambda e, pt=pt, ec=ec: e.activation(
                    out=sig_t, in_=pt[:], func=AF.Sigmoid, scale=-1.0,
                    bias=dtb_neg[:, ec:ec + 1]), s_s, "s")
                sv = prog.step("scalar", lambda e, pp=pp: e.activation(
                    out=lnt_t[pp], in_=sig_t, func=AF.Ln), s_s, "s")
                prog.cnt[f"a3_ev_{idx}"] = sv
                prog.wait("gpsimd", s_s, sv)
                stv = prog.dma("gpsimd", lnt_f[b, ec, :, lb * LT:(lb + 1) * LT],
                               lnt_t[pp], st_sems["lnt"], "st_lnt")
                prog.cnt[f"a3_lntst_{pp}"] = stv

        for s_i in range(EC):
            phase_a1(EC + s_i)
            if s_i % 4 == 3:
                uu = s_i // 4
                phase_a2_unit(uu)
                phase_a3_unit(uu)

        reset(0)
        bbc_h = carve_h(8 * L)
        cbc_h = carve_h(8 * L)
        ident_h = carve_h(128)
        lnt_em = [carve(L), carve(L)]
        xcv_em = [carve(L), carve(L)]
        sg_em = [carve(L), carve(L)]
        ya_em = [carve_h(L), carve_h(L)]
        mdtu_h = carve_h(L)
        dA_b = [carve(L), carve(L)]
        w_hb = carve_h(L)
        s_hb = carve_h(L)
        q_hb = [carve_h(L), carve_h(L)]
        t1_f = carve(L)
        yo_hb = [carve_h(L), carve_h(L)]
        ya_st = [carve_h(L), carve_h(L)]
        idsrc = carve(128)

        store_barrier("sync")
        prog.wait("sync", s_t, prog.val("t"))
        prog.wait("sync", s_v, prog.val("v"))
        prog.wait("sync", s_s, prog.val("s"))
        ilv = prog.dma("sync", idsrc, ident_f, s_c, "c")
        prog.wait("vector", s_c, ilv)
        prog.step("vector", lambda e: e.tensor_copy(ident_h, idsrc), s_v, "v")

        bbc3 = bbc_h.rearrange("p (n l) -> p n l", n=8)
        cbc3 = cbc_h.rearrange("p (n l) -> p n l", n=8)

        def phase_b_pass(b, g):
            prog.wait("sync", s_v, prog.val("v"))
            prog.wait("sync", s_s, prog.val("s"))
            prog.wait("sync", st_sems["ya"], prog.val("st_ya"))
            prog.dma("sync", bbc3,
                     bn_rh[b, g * 8:(g + 1) * 8, :].unsqueeze(0).broadcast_to([128, 8, L]),
                     s_ld2, "ld2")
            bcv = prog.dma("sync", cbc3,
                           c_rh[b, g * 8:(g + 1) * 8, :].unsqueeze(0).broadcast_to([128, 8, L]),
                           s_ld2, "ld2")
            prog.wait("vector", s_ld2, bcv)
            prog.wait("vector", s_t, prog.val("t"))

            for ec in range(EC):
                pp = ec % 2
                prog.wait("sync", s_s, prog.cnt.get(f"b_exp_{pp}", 0))
                prog.wait("sync", s_v, prog.cnt.get(f"b_vec_{pp}", 0))
                prog.dma("sync", lnt_em[pp], lnt_f[b, ec], s_b[pp], f"b{pp}")
                lv = prog.dma("sync", xcv_em[pp], xcv_f[b, ec], s_b[pp], f"b{pp}")
                if g == 1:
                    prog.dma("sync", sg_em[pp], sg_f[b, ec], s_b[pp], f"b{pp}")
                    lv = prog.dma("sync", ya_em[pp], ya_f[ec], s_b[pp], f"b{pp}")
                prog.wait("vector", s_b[pp], lv)
                prog.on("vector", lambda e, pp=pp: e.tensor_tensor(
                    out=mdtu_h, in0=lnt_em[pp], in1=xcv_em[pp], op=ALU.mult))
                prog.wait("scalar", s_b[pp], lv)
                for n8 in range(8):
                    n = g * 8 + n8
                    d = n8 % 2
                    k = prog.bump("b_nsq")
                    prog.wait("scalar", s_v, prog.cnt.get(f"b_scanv_{k - 2}", 0))
                    sv = prog.step("scalar", lambda e, d=d, pp=pp, ec=ec, n=n: e.activation(
                        out=dA_b[d], in_=lnt_em[pp], func=AF.Exp,
                        scale=eA[:, ec * N + n:ec * N + n + 1]), s_s, "s")
                    prog.on("vector", lambda e, n8=n8: e.tensor_tensor(
                        out=w_hb, in0=mdtu_h, in1=bbc3[:, n8, :], op=ALU.mult))
                    prog.wait("vector", s_s, sv)
                    vv = prog.step("vector", lambda e, d=d: e.tensor_tensor_scan(
                        out=s_hb, data0=dA_b[d], data1=w_hb,
                        initial=0.0, op0=ALU.mult, op1=ALU.add), s_v, "v")
                    prog.cnt[f"b_scanv_{k}"] = vv
                    prog.wait("vector", s_t, prog.cnt.get(f"b_mmt_{k - 2}", 0))
                    qv = prog.step("vector", lambda e, d=d, n8=n8: e.tensor_tensor(
                        out=q_hb[d], in0=s_hb, in1=cbc3[:, n8, :], op=ALU.mult), s_v, "v")
                    prog.wait("tensor", s_v, qv)
                    if n8 == 0:
                        pc = prog.cnt.get("b_pycons", None)
                        if pc is not None:
                            prog.wait("tensor", {"s": s_s, "v": s_v}[pc[0]], pc[1])
                        if g == 1:
                            prog.wait("tensor", s_b[pp], lv)
                            for lt in range(4):
                                prog.on("tensor", lambda e, lt=lt, pp=pp: e.matmul(
                                    psum[lt][:], ident_h,
                                    ya_em[pp][:, lt * 512:(lt + 1) * 512],
                                    start=True, stop=False))
                    for lt in range(4):
                        fn = lambda e, d=d, lt=lt, n8=n8, g=g: e.matmul(
                            psum[lt][:], ident_h, q_hb[d][:, lt * 512:(lt + 1) * 512],
                            start=(n8 == 0 and g == 0), stop=(n8 == 7))
                        if lt == 3:
                            tv = prog.step("tensor", fn, s_t, "t")
                            prog.cnt[f"b_mmt_{k}"] = tv
                        else:
                            prog.on("tensor", fn)
                prog.cnt[f"b_exp_{pp}"] = prog.val("s")
                mmv = prog.val("t")
                if g == 0:
                    gi = prog.bump("b_yagi")
                    prog.wait("scalar", s_t, mmv)
                    prog.wait("scalar", st_sems["ya"], prog.cnt.get(f"b_yasg_{gi - 2}", 0))
                    for lt in range(4):
                        fn = lambda e, lt=lt, pp=pp: e.activation(
                            out=ya_st[pp][:, lt * 512:(lt + 1) * 512], in_=psum[lt][:],
                            func=AF.Copy)
                        if lt == 3:
                            sv_e = prog.step("scalar", fn, s_s, "s")
                        else:
                            prog.on("scalar", fn)
                    prog.cnt["b_pycons"] = ("s", sv_e)
                    prog.wait("gpsimd", s_s, sv_e)
                    stv = prog.dma("gpsimd", ya_f[ec], ya_st[pp], st_sems["ya"], "st_ya")
                    prog.cnt[f"b_yasg_{gi}"] = stv
                else:
                    prog.wait("vector", s_t, mmv)
                    for lt in range(4):
                        fn = lambda e, lt=lt, pp=pp, ec=ec: e.scalar_tensor_tensor(
                            out=t1_f[:, lt * 512:(lt + 1) * 512],
                            in0=xcv_em[pp][:, lt * 512:(lt + 1) * 512],
                            scalar=dD_sb[:, ec:ec + 1], in1=psum[lt][:],
                            op0=ALU.mult, op1=ALU.add)
                        if lt == 3:
                            yfv = prog.step("vector", fn, s_v, "v")
                        else:
                            prog.on("vector", fn)
                    prog.cnt["b_pycons"] = ("v", yfv)
                    gi = prog.bump("b_ygi")
                    prog.wait("vector", st_sems["y"], prog.cnt.get(f"b_yst_{gi - 2}", 0))
                    vv = prog.step("vector", lambda e, pp=pp: e.tensor_tensor(
                        out=yo_hb[pp], in0=t1_f, in1=sg_em[pp], op=ALU.mult), s_v, "v")
                    prog.wait("gpsimd", s_v, vv)
                    stv = prog.dma("gpsimd", y_hb[b, ec], yo_hb[pp], st_sems["y"], "st_y")
                    prog.cnt[f"b_yst_{gi}"] = stv
                prog.cnt[f"b_vec_{pp}"] = prog.val("v")

        for b in range(BC):
            for g in range(2):
                phase_b_pass(b, g)

        reset(0)
        ow_stage = [carve(1024), carve(1024)]
        ow_h = carve_h(EC * 1024)
        yblk = [carve_h(EC * LT), carve_h(EC * LT)]
        mo_t = carve(8 * LT)

        store_barrier("sync")
        prog.wait("sync", s_st, prog.val("st"))
        prog.wait("sync", s_v, prog.val("v"))
        prog.wait("sync", s_s, prog.val("s"))

        for half in range(2):
            for ec in range(EC):
                pp = ec % 2
                prog.wait("sync", s_v, prog.cnt.get(f"c1w_{half}_{ec - 2}", 0))
                if half > 0 and ec < 2:
                    prog.wait("sync", s_t, prog.cnt.get("c1_t_last", 0))
                lv = prog.dma("sync", ow_stage[pp],
                              w_out_t[ec * 128:(ec + 1) * 128, half * 1024:(half + 1) * 1024],
                              s_ld, "ld")
                prog.wait("vector", s_ld, lv)
                cv = prog.step("vector", lambda e, pp=pp, ec=ec: e.tensor_copy(
                    ow_h[:, ec * 1024:(ec + 1) * 1024], ow_stage[pp]), s_v, "v")
                prog.cnt[f"c1w_{half}_{ec}"] = cv
            wcast = prog.val("v")
            def phase_c1_unit(uu, half=half, wcast=wcast):
                b, lb = divmod(uu, LB)
                pp = uu % 2
                idx = half * BC * LB + uu
                prog.wait("sync", s_t, prog.cnt.get(f"c1_t_{idx - 2}", 0))
                lv = prog.dma("sync", yblk[pp].rearrange("p (ec l) -> p ec l", ec=EC),
                              y_hb[b, :, :, lb * LT:(lb + 1) * LT].rearrange("ec p l -> p ec l"),
                              s_ld, "ld")
                prog.wait("tensor", s_ld, lv)
                if uu == 0:
                    prog.wait("tensor", s_v, wcast)
                for ho in range(8):
                    pt = psum[ho % 2]
                    prog.wait("tensor", s_v, prog.cnt.get(f"c1_ev_{idx}_{ho - 2}", 0))
                    if ho == 0:
                        prog.wait("vector", s_st, prog.cnt.get("c1_most", 0))
                    for ec in range(EC):
                        fn = lambda e, pt=pt, pp=pp, ec=ec, ho=ho: e.matmul(
                            pt[:], ow_h[:, ec * 1024 + ho * 128:ec * 1024 + (ho + 1) * 128],
                            yblk[pp][:, ec * LT:(ec + 1) * LT],
                            start=(ec == 0), stop=(ec == EC - 1))
                        if ec == EC - 1:
                            tv = prog.step("tensor", fn, s_t, "t")
                            prog.cnt[f"c1_t_{idx}"] = tv
                        else:
                            prog.on("tensor", fn)
                    prog.wait("vector", s_t, tv)
                    ev = prog.step("vector", lambda e, ho=ho, pt=pt: e.tensor_copy(
                        mo_t[:, ho * LT:(ho + 1) * LT], pt[:]), s_v, "v")
                    prog.cnt[f"c1_ev_{idx}_{ho}"] = ev
                prog.wait("gpsimd", s_v, prog.val("v"))
                stv = prog.dma("gpsimd",
                               mo_f[b, half * 8:(half + 1) * 8, :, lb * LT:(lb + 1) * LT]
                               .rearrange("ho p l -> p ho l"),
                               mo_t.rearrange("p (ho l) -> p ho l", ho=8), s_st, "st")
                prog.cnt["c1_most"] = stv
            for uu in range(BC * LB):
                phase_c1_unit(uu)
            prog.cnt["c1_t_last"] = prog.val("t")

        reset(0)
        mo_b = carve(HC * LT)
        x_b = carve(HC * LT)
        sq_b = carve(HC * LT)
        x2_b = carve(HC * LT)
        x2l_b = carve_h(HC * LT)
        row_m2 = carve(512)[0:1, :]
        row_q2 = carve(512)[0:1, :]
        row_r2 = carve(512)[0:1, :]

        mo3 = mo_b.rearrange("p (hc l) -> p hc l", hc=HC)
        xb3 = x_b.rearrange("p (hc l) -> p hc l", hc=HC)
        sqb3 = sq_b.rearrange("p (hc l) -> p hc l", hc=HC)
        x2b3 = x2_b.rearrange("p (hc l) -> p hc l", hc=HC)
        x2l3 = x2l_b.rearrange("p (hc l) -> p hc l", hc=HC)

        prog.wait("sync", s_st, prog.val("st"))
        prog.wait("sync", s_v, prog.val("v"))
        prog.wait("sync", s_t, prog.val("t"))
        prev_v = prog.val("v")
        for uu in range(BC * LB):
            b, lb = divmod(uu, LB)
            prog.wait("sync", s_v, prog.cnt.get(f"c2_v_{uu - 1}", 0))
            prog.wait("sync", s_st, prog.cnt.get("c2_st", 0))
            prog.dma("sync", mo3,
                     mo_f[b, :, :, lb * LT:(lb + 1) * LT].rearrange("hc p l -> p hc l"),
                     s_ld, "ld")
            lv = prog.dma("sync", xb3,
                          x_fm[b, :, lb * LT:(lb + 1) * LT].rearrange("(hc p) l -> p hc l", p=128),
                          s_ld, "ld")
            prog.wait("scalar", s_ld, lv)
            prog.wait("tensor", s_ld, lv)
            ln_block(mo3, sqb3, row_m2, row_q2, row_r2, psum[0], psum[1], psum[2], psum[3],
                     prev_v, HC)
            tv = prog.val("t")
            prog.wait("vector", s_t, tv)
            prog.wait("vector", s_ld, lv)
            prog.on("vector", lambda e: e.tensor_tensor(
                out=sqb3, in0=mo3,
                in1=psum[0][:].unsqueeze(1).broadcast_to([128, HC, 512]), op=ALU.subtract))
            prog.on("vector", lambda e: e.tensor_tensor(
                out=sqb3, in0=sqb3,
                in1=psum[1][:].unsqueeze(1).broadcast_to([128, HC, 512]), op=ALU.mult))
            vv = prog.step("vector", lambda e: e.tensor_add(x2_b, sq_b, x_b), s_v, "v")
            prog.wait("scalar", s_v, vv)
            prog.wait("tensor", s_v, vv)
            ln_block(x2b3, sqb3, row_m2, row_q2, row_r2, psum[4], psum[5], psum[6], psum[7],
                     vv, HC)
            tv = prog.val("t")
            prog.wait("vector", s_t, tv)
            prog.on("vector", lambda e: e.tensor_tensor(
                out=sqb3, in0=x2b3,
                in1=psum[4][:].unsqueeze(1).broadcast_to([128, HC, 512]), op=ALU.subtract))
            vv = prog.step("vector", lambda e: e.tensor_tensor(
                out=x2l3, in0=sqb3,
                in1=psum[5][:].unsqueeze(1).broadcast_to([128, HC, 512]), op=ALU.mult), s_v, "v")
            prog.cnt[f"c2_v_{uu}"] = vv
            prev_v = vv
            prog.wait("gpsimd", s_v, vv)
            prog.dma("gpsimd",
                     x2_f[b, :, :, lb * LT:(lb + 1) * LT].rearrange("hc p l -> p hc l"),
                     x2b3, s_st, "st")
            stv = prog.dma("gpsimd",
                           x2ln_h[b, :, :, lb * LT:(lb + 1) * LT].rearrange("hc p l -> p hc l"),
                           x2l3, s_st, "st")
            prog.cnt["c2_st"] = stv

        reset(0)
        w1_stage = [carve(2048), carve(2048)]
        w1_h = carve_h(HC * 2048)
        w2_st = carve(HC * KF)
        w2_h = carve_h(HC * KF)
        x2l_s = [carve_h(HC * LT), carve_h(HC * LT)]
        h1_t = carve_h(HC * LT)
        ex_t = carve(LT)[0:KF, :]
        al_t = carve(LT)[0:KF, :]

        prog.wait("sync", s_st, prog.val("st"))
        prog.wait("sync", s_v, prog.val("v"))
        prog.wait("sync", s_t, prog.val("t"))
        prog.wait("sync", s_s, prog.val("s"))
        for dc in range(HC):
            pp = dc % 2
            prog.wait("sync", s_v, prog.cnt.get(f"c3w_{dc - 2}", 0))
            lv = prog.dma("sync", w1_stage[pp], w1[dc * 128:(dc + 1) * 128, :], s_ld, "ld")
            prog.wait("vector", s_ld, lv)
            cv = prog.step("vector", lambda e, pp=pp, dc=dc: e.tensor_copy(
                w1_h[:, dc * 2048:(dc + 1) * 2048], w1_stage[pp]), s_v, "v")
            prog.cnt[f"c3w_{dc}"] = cv
        lv = prog.dma("sync", w2_st.rearrange("p (dc k) -> p dc k", k=KF),
                      w2.rearrange("(dc p) k -> p dc k", p=128), s_ld, "ld")
        prog.wait("vector", s_ld, lv)
        wc = prog.step("vector", lambda e: e.tensor_copy(w2_h, w2_st), s_v, "v")

        def phase_c3a_unit(uu):
            b, lb = divmod(uu, LB)
            pp = uu % 2
            prog.wait("sync", s_t, prog.cnt.get(f"c3a_t_{uu - 2}", 0))
            lv = prog.dma("sync", x2l_s[pp].rearrange("p (hc l) -> p hc l", hc=HC),
                          x2ln_h[b, :, :, lb * LT:(lb + 1) * LT].rearrange("hc p l -> p hc l"),
                          s_ld, "ld")
            prog.wait("tensor", s_ld, lv)
            if uu == 0:
                prog.wait("tensor", s_v, wc)
            for eo in range(HC):
                pt = psum[eo % 2]
                prog.wait("tensor", s_s, prog.cnt.get(f"c3a_ev_{uu}_{eo - 2}", 0))
                for dc in range(HC):
                    fn = lambda e, pt=pt, pp=pp, dc=dc, eo=eo: e.matmul(
                        pt[:], w1_h[:, dc * 2048 + eo * 128:dc * 2048 + (eo + 1) * 128],
                        x2l_s[pp][:, dc * LT:(dc + 1) * LT],
                        start=(dc == 0), stop=(dc == HC - 1))
                    if dc == HC - 1:
                        tv = prog.step("tensor", fn, s_t, "t")
                    else:
                        prog.on("tensor", fn)
                prog.wait("scalar", s_t, tv)
                if eo == 0:
                    prog.wait("scalar", s_st, prog.cnt.get("c3a_h1st", 0))
                ev = prog.step("scalar", lambda e, eo=eo, pt=pt: e.activation(
                    out=h1_t[:, eo * LT:(eo + 1) * LT], in_=pt[:], func=AF.Relu,
                    bias=b1_sb[:, eo:eo + 1]), s_s, "s")
                prog.cnt[f"c3a_ev_{uu}_{eo}"] = ev
                prog.wait("tensor", s_s, ev)
                tv2 = prog.step("tensor", lambda e, eo=eo: e.matmul(
                    psum[2][0:KF, :], w2_h[:, eo * KF:(eo + 1) * KF],
                    h1_t[:, eo * LT:(eo + 1) * LT],
                    start=(eo == 0), stop=(eo == HC - 1), skip_group_check=True), s_t, "t")
                prog.cnt[f"c3a_t_{uu}"] = tv2
            tvl = prog.val("t")
            prog.wait("scalar", s_t, tvl)
            sv = prog.step("scalar", lambda e: e.activation(
                out=ex_t, in_=psum[2][0:KF, :], func=AF.Exp, bias=b2_col), s_s, "s")
            prog.wait("tensor", s_s, sv)
            tv3 = prog.step("tensor", lambda e: e.matmul(
                psum[3][0:1, :], one_col8, ex_t, start=True, stop=True), s_t, "t")
            prog.wait("vector", s_t, tv3)
            sv2 = prog.step("vector", lambda e: e.reciprocal(
                al_t[0:1, :], psum[3][0:1, :]), s_v, "v")
            prog.wait("tensor", s_v, sv2)
            tv4 = prog.step("tensor", lambda e: e.matmul(
                psum[4][0:KF, :], one_row1[0:1, 0:KF], al_t[0:1, :], start=True, stop=True), s_t, "t")
            prog.wait("vector", s_t, tv4)
            vv = prog.step("vector", lambda e: e.tensor_mul(al_t, ex_t, psum[4][0:KF, :]), s_v, "v")
            prog.wait("gpsimd", s_v, vv)
            prog.wait("gpsimd", s_s, prog.val("s"))
            stv_h = prog.dma("gpsimd",
                             h1_h[b, :, :, lb * LT:(lb + 1) * LT].rearrange("hc p l -> p hc l"),
                             h1_t.rearrange("p (hc l) -> p hc l", hc=HC), s_st, "st")
            prog.cnt["c3a_h1st"] = stv_h
            stv = prog.dma("gpsimd", al_f[b, :, lb * LT:(lb + 1) * LT], al_t, s_st, "st")
            prog.wait("scalar", s_st, stv)
        for uu in range(BC * LB):
            phase_c3a_unit(uu)

        reset(0)
        vt_st = [carve(KR), carve(KR)]
        vt_h = carve_h(HC * KR)
        ut_st = [carve(2048), carve(2048)]
        ut_h = carve_h(KRC * 2048)
        x2l_s2 = [carve_h(HC * LT), carve_h(HC * LT)]
        avx_h = carve_h(KRC * LT)
        alb = carve(KRC * LT)
        fo_t = carve(HC * LT)
        x2_s = carve(HC * LT)
        sq_c = carve(HC * LT)
        row_m3 = carve(512)[0:1, :]
        row_q3 = carve(512)[0:1, :]
        row_r3 = carve(512)[0:1, :]

        fo3 = fo_t.rearrange("p (hc l) -> p hc l", hc=HC)
        sqc3 = sq_c.rearrange("p (hc l) -> p hc l", hc=HC)
        x2s3 = x2_s.rearrange("p (hc l) -> p hc l", hc=HC)

        prog.wait("sync", s_v, prog.val("v"))
        prog.wait("sync", s_s, prog.val("s"))
        prog.wait("sync", s_t, prog.val("t"))
        for dc in range(HC):
            pp = dc % 2
            prog.wait("sync", s_v, prog.cnt.get(f"c3bw_{dc - 2}", 0))
            lv = prog.dma("sync", vt_st[pp], vt[dc * 128:(dc + 1) * 128, :], s_ld, "ld")
            prog.wait("vector", s_ld, lv)
            cv = prog.step("vector", lambda e, pp=pp, dc=dc: e.tensor_copy(
                vt_h[:, dc * KR:(dc + 1) * KR], vt_st[pp]), s_v, "v")
            prog.cnt[f"c3bw_{dc}"] = cv
        for kc in range(KRC):
            pp = kc % 2
            prog.wait("sync", s_v, prog.cnt.get(f"c3bu_{kc - 2}", 0))
            lv = prog.dma("sync", ut_st[pp], ut[kc * 128:(kc + 1) * 128, :], s_ld, "ld")
            prog.wait("vector", s_ld, lv)
            cv = prog.step("vector", lambda e, pp=pp, kc=kc: e.tensor_copy(
                ut_h[:, kc * 2048:(kc + 1) * 2048], ut_st[pp]), s_v, "v")
            prog.cnt[f"c3bu_{kc}"] = cv
        wc2 = prog.val("v")
        prog.wait("sync", s_st, prog.val("st"))

        prev_v_cell = [prog.val("v")]
        def phase_c3b_unit(uu):
            prev_v = prev_v_cell[0]
            b, lb = divmod(uu, LB)
            pp = uu % 2
            prog.wait("sync", s_t, prog.cnt.get(f"c3b_t_{uu - 2}", 0))
            prog.wait("sync", s_v, prog.cnt.get(f"c3b_v_{uu - 1}", 0))
            prog.wait("sync", s_st, prog.cnt.get("c3b_ost", 0))
            prog.dma("sync", x2l_s2[pp].rearrange("p (hc l) -> p hc l", hc=HC),
                     x2ln_h[b, :, :, lb * LT:(lb + 1) * LT].rearrange("hc p l -> p hc l"),
                     s_ld, "ld")
            prog.dma("sync", x2s3,
                     x2_f[b, :, :, lb * LT:(lb + 1) * LT].rearrange("hc p l -> p hc l"),
                     s_ld, "ld")
            for kc in range(KRC):
                for kk in range(2):
                    lva = prog.dma("sync",
                                   alb[kk * RF:(kk + 1) * RF, kc * LT:(kc + 1) * LT],
                                   al_f[b, 2 * kc + kk, lb * LT:(lb + 1) * LT]
                                   .unsqueeze(0).broadcast_to([RF, LT]),
                                   s_ld, "ld")
            prog.wait("tensor", s_ld, lva)
            if uu == 0:
                prog.wait("tensor", s_v, wc2)
            for kc in range(KRC):
                prog.wait("tensor", s_v, prog.cnt.get(f"c3b_av_{uu - 1}_{kc}", 0))
                if kc >= 2:
                    prog.wait("tensor", s_s, prog.cnt.get(f"c3b_lnsq_{uu - 1}", 0))
                for dc in range(HC):
                    fn = lambda e, kc=kc, dc=dc, pp=pp: e.matmul(
                        psum[kc][:], vt_h[:, dc * KR + kc * 128:dc * KR + (kc + 1) * 128],
                        x2l_s2[pp][:, dc * LT:(dc + 1) * LT],
                        start=(dc == 0), stop=(dc == HC - 1))
                    if dc == HC - 1:
                        tv = prog.step("tensor", fn, s_t, "t")
                    else:
                        prog.on("tensor", fn)
                prog.wait("vector", s_t, tv)
                av = prog.step("vector", lambda e, kc=kc: e.tensor_mul(
                    avx_h[:, kc * LT:(kc + 1) * LT], psum[kc][:],
                    alb[:, kc * LT:(kc + 1) * LT]), s_v, "v")
                prog.cnt[f"c3b_av_{uu}_{kc}"] = av
            avv = prog.val("v")
            prog.wait("tensor", s_v, avv)
            for dc in range(HC):
                pt = psum[4 + dc % 2]
                prog.wait("tensor", s_v, prog.cnt.get(f"c3b_fo_{uu}_{dc - 2}", 0))
                for kc in range(KRC):
                    fn = lambda e, pt=pt, dc=dc, kc=kc: e.matmul(
                        pt[:], ut_h[:, kc * 2048 + dc * 128:kc * 2048 + (dc + 1) * 128],
                        avx_h[:, kc * LT:(kc + 1) * LT],
                        start=(kc == 0), stop=(kc == KRC - 1))
                    if kc == KRC - 1:
                        tv = prog.step("tensor", fn, s_t, "t")
                        prog.cnt[f"c3b_t_{uu}"] = tv
                    else:
                        prog.on("tensor", fn)
                prog.wait("vector", s_t, tv)
                fv = prog.step("vector", lambda e, dc=dc, pt=pt: e.tensor_copy(
                    fo_t[:, dc * LT:(dc + 1) * LT], pt[:]), s_v, "v")
                prog.cnt[f"c3b_fo_{uu}_{dc}"] = fv
            fvl = prog.val("v")
            prog.wait("scalar", s_v, fvl)
            prog.wait("tensor", s_v, fvl)
            ln_block(fo3, sqc3, row_m3, row_q3, row_r3, psum[6], psum[7], psum[2], psum[3],
                     prev_v, HC)
            prog.cnt[f"c3b_lnsq_{uu}"] = prog.val("s")
            tv = prog.val("t")
            prog.wait("vector", s_t, tv)
            prog.wait("vector", s_ld, lva)
            prog.on("vector", lambda e: e.tensor_tensor(
                out=sqc3, in0=fo3,
                in1=psum[6][:].unsqueeze(1).broadcast_to([128, HC, 512]), op=ALU.subtract))
            prog.on("vector", lambda e: e.tensor_tensor(
                out=sqc3, in0=sqc3,
                in1=psum[7][:].unsqueeze(1).broadcast_to([128, HC, 512]), op=ALU.mult))
            vv = prog.step("vector", lambda e: e.tensor_add(x2_s, sq_c, x2_s), s_v, "v")
            prog.cnt[f"c3b_v_{uu}"] = vv
            prev_v_cell[0] = vv
            prog.wait("gpsimd", s_v, vv)
            stv = prog.dma("gpsimd",
                           out_fm[b, :, lb * LT:(lb + 1) * LT].rearrange("(hc p) l -> p hc l", p=128),
                           x2s3, s_st, "st")
            prog.cnt["c3b_ost"] = stv
        for uu in range(BC * LB):
            phase_c3b_unit(uu)

        prog.wait("sync", s_st, prog.val("st"))

        @block.sync
        def _(eng):
            for fn in prog.ops["sync"]:
                fn(eng)

        @block.tensor
        def _(eng):
            for fn in prog.ops["tensor"]:
                fn(eng)

        @block.vector
        def _(eng):
            for fn in prog.ops["vector"]:
                fn(eng)

        @block.scalar
        def _(eng):
            for fn in prog.ops["scalar"]:
                fn(eng)

        @block.gpsimd
        def _(eng):
            for fn in prog.ops["gpsimd"]:
                fn(eng)

    return nc


def _prep_core_inputs(inputs, core):
    d = inputs
    x = d["L0_attn_hidden_states"][2 * core:2 * core + 2]
    conv_w = d["L0_attn_conv_w"][:, 0, :]
    m = {
        "x_fm": x.transpose(0, 2, 1),
        "w_in_t": d["L0_attn_in_proj_w"].T,
        "conv_wt": conv_w.reshape(EC, 128, 4).transpose(1, 0, 2).reshape(128, EC * 4),
        "conv_bt": d["L0_attn_conv_b"].reshape(EC, 128).T,
        "w_xp_t": d["L0_attn_x_proj_w"].T,
        "w_dt_t": d["L0_attn_dt_proj_w"].T,
        "dt_bt": d["L0_attn_dt_proj_b"].reshape(EC, 128).T,
        "alog_t": d["L0_attn_A_log"].reshape(EC, 128, N).transpose(1, 0, 2).reshape(128, EC * N),
        "d_t": d["L0_attn_D"].reshape(EC, 128).T,
        "w_out_t": d["L0_attn_out_proj_w"].T,
        "w1": d["L0_ffn_W1"],
        "b1_t": d["L0_ffn_b1"].reshape(HC, 128).T,
        "w2": d["L0_ffn_W2"],
        "b2_t": d["L0_ffn_b2"].reshape(KF, 1),
        "vt": d["L0_ffn_V"].transpose(2, 0, 1).reshape(H, KR),
        "ut": d["L0_ffn_U"].transpose(0, 2, 1).reshape(KR, H),
        "ident_f": np.eye(128, dtype=np.float32),
    }
    return {k: np.ascontiguousarray(np.asarray(v, dtype=np.float32)) for k, v in m.items()}


def kernel(**inputs):
    nc = build_nc(debug=False)
    in_maps = [_prep_core_inputs(inputs, c) for c in range(NCORES)]
    res = run_bass_kernel_spmd(nc, in_maps, core_ids=list(range(NCORES)))
    outs = [r["out_fm"] for r in res.results]
    full = np.concatenate([o.transpose(0, 2, 1) for o in outs], axis=0)
    return np.ascontiguousarray(full.astype(np.float32))


# version 05
# speedup vs baseline: 1.3840x; 1.1633x over previous
import numpy as np
import concourse.bass as bass
import concourse.mybir as mybir
from concourse.bass_utils import run_bass_kernel_spmd

F32 = mybir.dt.float32
BF16 = mybir.dt.bfloat16
AF = mybir.ActivationFunctionType
ALU = mybir.AluOpType

NCORES = 8
BC = 2
L = 2048
H = 2048
HC = 16
E = 4096
EC = 32
EO = 64
N = 16
R = 128
LB = 4
LT = 512
KF = 8
RF = 64
KR = 512
KRC = 4

ENGINES = ("sync", "tensor", "vector", "scalar", "gpsimd")


class Prog:
    def __init__(self):
        self.ops = {e: [] for e in ENGINES}
        self.cnt = {}

    def on(self, eng, fn):
        self.ops[eng].append(fn)

    def bump(self, name, k=1):
        v = self.cnt.get(name, 0) + k
        self.cnt[name] = v
        return v

    def val(self, name):
        return self.cnt.get(name, 0)

    def wait(self, eng, sem_h, v):
        if v > 0:
            self.on(eng, lambda e, s=sem_h, vv=v: e.wait_ge(s, vv))

    def dma(self, eng, out, in_, sem_h, name):
        v = self.bump(name, 16)
        self.on(eng, lambda e, o=out, i=in_, s=sem_h: e.dma_start(out=o, in_=i).then_inc(s, 16))
        return v

    def step(self, eng, fn, sem_h, name):
        v = self.bump(name, 1)
        self.on(eng, lambda e, f=fn, s=sem_h: f(e).then_inc(s, 1))
        return v

    def run(self, eng, fn):
        self.on(eng, fn)


def build_nc(debug=False):
    nc = bass.Bass("TRN2", target_bir_lowering=False)

    x_fm = nc.dram_tensor("x_fm", [BC, H, L], F32, kind="ExternalInput").ap()
    w_in_t = nc.dram_tensor("w_in_t", [H, 2 * E], F32, kind="ExternalInput").ap()
    conv_wt = nc.dram_tensor("conv_wt", [128, EC * 4], F32, kind="ExternalInput").ap()
    conv_bt = nc.dram_tensor("conv_bt", [128, EC], F32, kind="ExternalInput").ap()
    w_xp_t = nc.dram_tensor("w_xp_t", [E, 160], F32, kind="ExternalInput").ap()
    w_dt_t = nc.dram_tensor("w_dt_t", [R, E], F32, kind="ExternalInput").ap()
    dt_bt = nc.dram_tensor("dt_bt", [128, EC], F32, kind="ExternalInput").ap()
    alog_t = nc.dram_tensor("alog_t", [128, EC * N], F32, kind="ExternalInput").ap()
    d_t = nc.dram_tensor("d_t", [128, EC], F32, kind="ExternalInput").ap()
    w_out_t = nc.dram_tensor("w_out_t", [E, H], F32, kind="ExternalInput").ap()
    w1 = nc.dram_tensor("w1", [H, H], F32, kind="ExternalInput").ap()
    b1_t = nc.dram_tensor("b1_t", [128, HC], F32, kind="ExternalInput").ap()
    w2 = nc.dram_tensor("w2", [H, KF], F32, kind="ExternalInput").ap()
    b2_t = nc.dram_tensor("b2_t", [KF, 1], F32, kind="ExternalInput").ap()
    vt = nc.dram_tensor("vt", [H, KR], F32, kind="ExternalInput").ap()
    ut = nc.dram_tensor("ut", [KR, H], F32, kind="ExternalInput").ap()
    ident_f = nc.dram_tensor("ident_f", [128, 128], F32, kind="ExternalInput").ap()

    out_fm = nc.dram_tensor("out_fm", [BC, H, L], F32, kind="ExternalOutput").ap()
    kind_s = "ExternalOutput" if debug else "Internal"
    xcv_f = nc.dram_tensor("xcv_f", [BC, EC, 128, L], F32, kind=kind_s).ap()
    xcv_h = nc.dram_tensor("xcv_h", [BC, EC, 128, L], BF16, kind=kind_s).ap()
    sg_f = nc.dram_tensor("sg_f", [BC, EC, 128, L], F32, kind=kind_s).ap()
    lnt_f = nc.dram_tensor("lnt_f", [BC, EC, 128, L], F32, kind=kind_s).ap()
    bn_rh = nc.dram_tensor("bn_rh", [BC, N, L], BF16, kind=kind_s).ap()
    c_rh = nc.dram_tensor("c_rh", [BC, N, L], BF16, kind=kind_s).ap()
    ya_f = nc.dram_tensor("ya_f", [EC, 128, L], F32, kind=kind_s).ap()
    y_hb = nc.dram_tensor("y_hb", [BC, EC, 128, L], BF16, kind=kind_s).ap()
    mo_f = nc.dram_tensor("mo_f", [BC, HC, 128, L], F32, kind=kind_s).ap()
    x2_f = nc.dram_tensor("x2_f", [BC, HC, 128, L], F32, kind=kind_s).ap()
    x2ln_h = nc.dram_tensor("x2ln_h", [BC, HC, 128, L], BF16, kind=kind_s).ap()
    h1_h = nc.dram_tensor("h1_h", [BC, HC, 128, L], BF16, kind=kind_s).ap()
    al_f = nc.dram_tensor("al_f", [BC, KF, L], F32, kind=kind_s).ap()
    if debug:
        xln_d = nc.dram_tensor("xln_d", [128, HC * BC * L], BF16, kind="ExternalOutput").ap()
        u_d = nc.dram_tensor("u_d", [BC, EC, 128, L], F32, kind="ExternalOutput").ap()
        scan_d = nc.dram_tensor("scan_d", [6, 128, N * LT], F32, kind="ExternalOutput").ap()
        dbg2 = nc.dram_tensor("dbg2", [3, 128, 512], F32, kind="ExternalOutput").ap()

    prog = Prog()
    SB_F32 = 52000

    from contextlib import ExitStack
    with ExitStack() as _stack:
        arena_t = _stack.enter_context(nc.sbuf_tensor("arena", [128, SB_F32], F32))
        pers_t = _stack.enter_context(nc.sbuf_tensor("persist", [128, 1200], F32))
        psum = [_stack.enter_context(nc.psum_tensor(f"pb{i}", [128, 512], F32)) for i in range(8)]
        pb0, pb1, pb2, pb3, pb4, pb5, pb6, pb7 = psum
        s_c = _stack.enter_context(nc.semaphore("s_c"))
        s_ld2 = _stack.enter_context(nc.semaphore("s_ld2"))
        s_w = [_stack.enter_context(nc.semaphore(f"s_w{i}")) for i in range(2)]
        s_b = [_stack.enter_context(nc.semaphore(f"s_b{i}")) for i in range(2)]
        st_sems = {k: _stack.enter_context(nc.semaphore(f"st_{k}"))
                   for k in ("sg", "xcv", "lnt", "bc", "ya", "y", "mo", "x2", "h1", "out", "dbg")}
        s_ld = _stack.enter_context(nc.semaphore("s_ld"))
        s_st = _stack.enter_context(nc.semaphore("s_st"))
        s_t = _stack.enter_context(nc.semaphore("s_t"))
        s_v = _stack.enter_context(nc.semaphore("s_v"))
        s_s = _stack.enter_context(nc.semaphore("s_s"))
        block = _stack.enter_context(nc.Block())

        def store_barrier(eng):
            for k, sem in st_sems.items():
                prog.wait(eng, sem, prog.val("st_" + k))
        psum = [pb0, pb1, pb2, pb3, pb4, pb5, pb6, pb7]
        arena = arena_t.ap()
        pers = pers_t.ap()

        po = [0]

        def pcarve(n):
            a = po[0]
            po[0] += n
            assert po[0] <= 1200
            return pers[:, a:a + n]

        eA = pcarve(EC * N)
        cw_sb = pcarve(EC * 4)
        cb_sb = pcarve(EC)
        dtb_sb = pcarve(EC)
        dtb_neg = pcarve(EC)
        dD_sb = pcarve(EC)
        b1_sb = pcarve(HC)
        b2_col = pcarve(1)[0:KF, :]
        ones_col = pcarve(1)
        one_row1 = pcarve(128)[0:1, :]
        one_col8 = pcarve(1)[0:KF, :]
        eps_col = pcarve(1)

        ao = [0]

        def carve(n):
            a = ao[0]
            ao[0] += n
            assert ao[0] <= SB_F32, f"arena overflow {ao[0]}"
            return arena[:, a:a + n]

        def carve_h(n_bf16):
            assert n_bf16 % 2 == 0
            return carve(n_bf16 // 2).bitcast(BF16)

        def reset(keep=0):
            ao[0] = keep

        prog.dma("sync", eA, alog_t, s_c, "c")
        prog.dma("sync", cw_sb, conv_wt, s_c, "c")
        prog.dma("sync", cb_sb, conv_bt, s_c, "c")
        prog.dma("sync", dtb_sb, dt_bt, s_c, "c")
        prog.dma("sync", dD_sb, d_t, s_c, "c")
        prog.dma("sync", b1_sb, b1_t, s_c, "c")
        cldv = prog.dma("sync", b2_col, b2_t, s_c, "c")
        prog.wait("scalar", s_c, cldv)
        prog.step("scalar", lambda e: e.activation(out=eA, in_=eA, func=AF.Exp), s_s, "s")
        prog.step("scalar", lambda e: e.activation(out=dtb_neg, in_=dtb_sb, func=AF.Copy, scale=-1.0), s_s, "s")
        prog.run("vector", lambda e: e.memset(ones_col, 1.0))
        prog.run("vector", lambda e: e.memset(one_row1, 1.0))
        prog.run("vector", lambda e: e.memset(one_col8, 1.0))
        prog.run("vector", lambda e: e.memset(eps_col, 1e-5))
        consts_v = prog.step("vector", lambda e: e.engine_nop(), s_v, "v")

        def ln_block(xt3, sq3, row_m, row_q, row_r, pm, pr, pst_a, pst_b, prev_v, nch):
            prog.wait("scalar", s_v, prev_v)
            sv = prog.step("scalar", lambda e: e.activation(out=sq3, in_=xt3, func=AF.Square), s_s, "s")
            prog.wait("tensor", s_s, sv)
            for c in range(nch):
                fn = lambda e, c=c: e.matmul(pst_a[0:1, :], ones_col, xt3[:, c, :],
                                             start=(c == 0), stop=(c == nch - 1))
                prog.on("tensor", fn)
            for c in range(nch):
                fn = lambda e, c=c: e.matmul(pst_b[0:1, :], ones_col, sq3[:, c, :],
                                             start=(c == 0), stop=(c == nch - 1))
                if c == nch - 1:
                    prog.step("tensor", fn, s_t, "t")
                else:
                    prog.on("tensor", fn)
            tv = prog.val("t")
            prog.wait("scalar", s_t, tv)
            prog.step("scalar", lambda e: e.activation(out=row_m, in_=pst_a[0:1, :],
                                                       func=AF.Copy, scale=1.0 / (nch * 128)), s_s, "s")
            prog.step("scalar", lambda e: e.activation(out=row_q, in_=pst_b[0:1, :],
                                                       func=AF.Copy, scale=1.0 / (nch * 128)), s_s, "s")
            sv2 = prog.step("scalar", lambda e: e.activation(out=row_r, in_=row_m, func=AF.Square), s_s, "s")
            prog.wait("vector", s_s, sv2)
            vv = prog.step("vector", lambda e: e.tensor_sub(row_q, row_q, row_r), s_v, "v")
            prog.wait("scalar", s_v, vv)
            sv3 = prog.step("scalar", lambda e: e.activation(out=row_r, in_=row_q, func=AF.Sqrt, bias=eps_col[0:1, :]), s_s, "s")
            prog.wait("vector", s_s, sv3)
            vv2 = prog.step("vector", lambda e: e.reciprocal(row_r, row_r), s_v, "v")
            prog.wait("tensor", s_v, vv2)
            prog.wait("tensor", s_v, prev_v)
            prog.on("tensor", lambda e: e.matmul(pm[:], one_row1, row_m, start=True, stop=True))
            prog.step("tensor", lambda e: e.matmul(pr[:], one_row1, row_r, start=True, stop=True), s_t, "t")
            return prog.val("t")

        reset()
        xln_h = carve_h(HC * (BC * L))
        xt_a = carve(HC * LT)
        sq_a = carve(HC * LT)
        row_m = carve(512)[0:1, :]
        row_q = carve(512)[0:1, :]
        row_r = carve(512)[0:1, :]
        A0_KEEP = (HC * (BC * L)) // 2

        xt3 = xt_a.rearrange("p (hc l) -> p hc l", hc=HC)
        sq3 = sq_a.rearrange("p (hc l) -> p hc l", hc=HC)
        xln3 = xln_h.rearrange("p (hc t) -> p hc t", hc=HC)
        prev_v = consts_v
        for uu in range(BC * LB):
            b, lb = divmod(uu, LB)
            prog.wait("sync", s_v, prev_v)
            lv = prog.dma("sync", xt3,
                          x_fm[b, :, lb * LT:(lb + 1) * LT].rearrange("(hc p) l -> p hc l", p=128),
                          s_c, "c")
            prog.wait("scalar", s_c, lv)
            prog.wait("tensor", s_c, lv)
            ln_block(xt3, sq3, row_m, row_q, row_r, psum[0], psum[1], psum[2], psum[3],
                     prev_v, HC)
            tv = prog.val("t")
            prog.wait("vector", s_t, tv)
            prog.on("vector", lambda e: e.tensor_tensor(
                out=sq3, in0=xt3,
                in1=psum[0][:].unsqueeze(1).broadcast_to([128, HC, 512]),
                op=ALU.subtract))
            dst = xln3[:, :, uu * LT:(uu + 1) * LT]
            prev_v = prog.step("vector", lambda e, d=dst: e.tensor_tensor(
                out=d, in0=sq3,
                in1=psum[1][:].unsqueeze(1).broadcast_to([128, HC, 512]),
                op=ALU.mult), s_v, "v")
        ln1_done_v = prev_v
        if debug:
            prog.wait("gpsimd", s_v, ln1_done_v)
            prog.dma("gpsimd", xln_d, xln_h, st_sems["dbg"], "st_dbg")

        reset(A0_KEEP)
        wbuf = [carve(16 * 128), carve(16 * 128)]
        wbuf_h = [carve_h(16 * 128), carve_h(16 * 128)]
        sg_t = [carve(LT), carve(LT)]
        A1_GATE_END = ao[0]
        u_row = [carve(3 + L), carve(3 + L)]
        c0 = carve(L)
        c1 = carve(L)
        xcv_t = [carve(L)]
        xcvh_t = [carve_h(L)]

        prog.run("vector", lambda e: e.memset(u_row[0][:, 0:3], 0.0))
        prog.run("vector", lambda e: e.memset(u_row[1][:, 0:3], 0.0))

        wb3_src = w_in_t
        def phase_a1(eo):
            pp = eo % 2
            t_prev2 = prog.cnt.get(f"a1_t_{eo - 2}", 0)
            prog.wait("sync", s_t, t_prev2)
            if eo < 2:
                prog.wait("sync", s_v, ln1_done_v)
            lv = prog.dma("sync", wbuf[pp].rearrange("p (hc e) -> p hc e", e=128),
                          wb3_src[:, eo * 128:(eo + 1) * 128].rearrange("(hc p) e -> p hc e", p=128),
                          s_w[pp], f"w{pp}")
            prog.wait("scalar", s_w[pp], lv)
            prog.wait("scalar", s_t, t_prev2)
            cv = prog.step("scalar", lambda e, pp=pp: e.activation(
                out=wbuf_h[pp], in_=wbuf[pp], func=AF.Copy), s_s, "s")

            prog.wait("tensor", s_s, cv)
            if eo == 0:
                prog.wait("tensor", s_v, ln1_done_v)
            is_u = eo < EC
            ec = eo if is_u else eo - EC
            for b in range(BC):
                for lb in range(LB):
                    if is_u:
                        pt = psum[b * LB + lb]
                        w_ev = prog.cnt.get(f"a1_ev_{eo - 1}_{b}_{lb}", None)
                    else:
                        bank = 4 + (b * LB + lb) % 4
                        pt = psum[bank]
                        w_ev = prog.cnt.get(f"a1_gbank_{bank}", None)
                    if w_ev is not None:
                        prog.wait("tensor", s_v if w_ev[0] == "v" else s_s, w_ev[1])
                    for hc in range(HC):
                        fn = lambda e, pt=pt, pp=pp, hc=hc, b=b, lb=lb: e.matmul(
                            pt[:], wbuf_h[pp][:, hc * 128:(hc + 1) * 128],
                            xln3[:, hc, (b * LB + lb) * LT:(b * LB + lb + 1) * LT],
                            start=(hc == 0), stop=(hc == HC - 1))
                        if hc == HC - 1:
                            tv = prog.step("tensor", fn, s_t, "t")
                            prog.cnt[f"a1_t_{eo}"] = tv
                        else:
                            prog.on("tensor", fn)
                    if is_u:
                        prog.wait("vector", s_t, tv)
                        ev = prog.step("vector", lambda e, b=b, lb=lb, pt=pt: e.tensor_copy(
                            u_row[b][:, 3 + lb * LT: 3 + (lb + 1) * LT], pt[:]), s_v, "v")
                        prog.cnt[f"a1_ev_{eo}_{b}_{lb}"] = ("v", ev)
                    else:
                        gp = (eo * BC * LB + b * LB + lb) % 2
                        prog.wait("scalar", st_sems["sg"], prog.cnt.get(f"a1_sgst_{gp}", 0))
                        prog.wait("scalar", s_t, tv)
                        ev = prog.step("scalar", lambda e, gp=gp, pt=pt: e.activation(
                            out=sg_t[gp], in_=pt[:], func=AF.Silu), s_s, "s")
                        prog.cnt[f"a1_gbank_{bank}"] = ("s", ev)
                        prog.wait("gpsimd", s_s, ev)
                        stv = prog.dma("gpsimd", sg_f[b, ec, :, lb * LT:(lb + 1) * LT],
                                       sg_t[gp], st_sems["sg"], "st_sg")
                        prog.cnt[f"a1_sgst_{gp}"] = stv
                        prog.cnt[f"a1_sgst_ec_{ec}"] = stv
            if is_u:
                for b in range(BC):
                    if debug:
                        lastev = prog.cnt[f"a1_ev_{eo}_{b}_{LB - 1}"][1]
                        prog.wait("gpsimd", s_v, lastev)
                        prog.dma("gpsimd", u_d[b, ec], u_row[b][:, 3:3 + L], st_sems["dbg"], "st_dbg")
                    pp2 = 0
                    prog.wait("vector", st_sems["xcv"], prog.val("st_xcv"))
                    prog.on("vector", lambda e, b=b, ec=ec: e.tensor_scalar_mul(
                        c0, u_row[b][:, 0:L], cw_sb[:, ec * 4 + 0:ec * 4 + 1]))
                    prog.on("vector", lambda e, b=b, ec=ec: e.scalar_tensor_tensor(
                        out=c1, in0=u_row[b][:, 1:1 + L], scalar=cw_sb[:, ec * 4 + 1:ec * 4 + 2],
                        in1=c0, op0=ALU.mult, op1=ALU.add))
                    prog.on("vector", lambda e, b=b, ec=ec: e.scalar_tensor_tensor(
                        out=c0, in0=u_row[b][:, 2:2 + L], scalar=cw_sb[:, ec * 4 + 2:ec * 4 + 3],
                        in1=c1, op0=ALU.mult, op1=ALU.add))
                    vv = prog.step("vector", lambda e, b=b, ec=ec: e.scalar_tensor_tensor(
                        out=c1, in0=u_row[b][:, 3:3 + L], scalar=cw_sb[:, ec * 4 + 3:ec * 4 + 4],
                        in1=c0, op0=ALU.mult, op1=ALU.add), s_v, "v")
                    prog.wait("scalar", s_v, vv)
                    sv = prog.step("scalar", lambda e, pp2=pp2, ec=ec: e.activation(
                        out=xcv_t[pp2], in_=c1, func=AF.Silu, bias=cb_sb[:, ec:ec + 1]), s_s, "s")
                    prog.wait("vector", s_s, sv)
                    vv2 = prog.step("vector", lambda e, pp2=pp2: e.tensor_copy(
                        xcvh_t[pp2], xcv_t[pp2]), s_v, "v")
                    prog.wait("gpsimd", s_v, vv2)
                    prog.dma("gpsimd", xcv_f[b, ec], xcv_t[pp2], st_sems["xcv"], "st_xcv")
                    stv = prog.dma("gpsimd", xcv_h[b, ec], xcvh_t[pp2], st_sems["xcv"], "st_xcv")

        for eo in range(EC):
            phase_a1(eo)
        a1u_xcv = prog.val("st_xcv")
        a1u_v = prog.val("v")
        a1u_s = prog.val("s")

        reset(A1_GATE_END)
        wstage = carve(2560)
        xp_wh = carve_h(EC * 160)
        dtlow_h = carve_h(BC * L)
        xcvs = [carve_h(LT), carve_h(LT)]
        bc_ev = carve_h(LT)
        dt_wh = carve_h(EC * 128)
        lnt_t = [carve(LT), carve(LT)]
        sig_t = carve(LT)

        prog.wait("sync", s_v, a1u_v)
        prog.wait("sync", s_s, a1u_s)
        prog.wait("sync", st_sems["xcv"], a1u_xcv)
        for hh in range(2):
            lv = prog.dma("sync", wstage.rearrange("p (ec r) -> p ec r", r=160),
                          w_xp_t[hh * 2048:(hh + 1) * 2048, :]
                          .rearrange("(ec p) r -> p ec r", p=128), s_c, "c")
            prog.wait("scalar", s_c, lv)
            cv = prog.step("scalar", lambda e, hh=hh: e.activation(
                out=xp_wh[:, hh * 2560:(hh + 1) * 2560], in_=wstage,
                func=AF.Copy), s_s, "s")
            prog.wait("sync", s_s, cv)
        xpcv = cv
        for hh in range(2):
            lv = prog.dma("sync", wstage[:, 0:2048], w_dt_t[:, hh * 2048:(hh + 1) * 2048],
                          s_c, "c")
            prog.wait("scalar", s_c, lv)
            dtcv = prog.step("scalar", lambda e, hh=hh: e.activation(
                out=dt_wh[:, hh * 2048:(hh + 1) * 2048], in_=wstage[:, 0:2048],
                func=AF.Copy), s_s, "s")
            if hh == 0:
                prog.wait("sync", s_s, dtcv)

        def phase_a2_unit(uu):
            b, lb = divmod(uu, LB)
            for ec in range(EC):
                pp = ec % 2
                idx = uu * EC + ec
                prog.wait("sync", s_t, prog.cnt.get(f"a2_t_{idx - 2}", 0))
                lv = prog.dma("sync", xcvs[pp], xcv_h[b, ec, :, lb * LT:(lb + 1) * LT],
                              s_b[pp], f"b{pp}")
                prog.wait("tensor", s_b[pp], lv)
                if idx == 0:
                    prog.wait("tensor", s_s, xpcv)
                if ec == 0:
                    prog.wait("tensor", s_v, prog.cnt.get(f"a2_ev_{uu - 1}", 0))
                    prog.wait("tensor", s_s, prog.cnt.get(f"a2_es_{uu - 1}", 0))
                prog.on("tensor", lambda e, pp=pp, ec=ec: e.matmul(
                    psum[0][:], xp_wh[:, ec * 160:ec * 160 + 128], xcvs[pp],
                    start=(ec == 0), stop=(ec == EC - 1), skip_group_check=True))
                tv = prog.step("tensor", lambda e, pp=pp, ec=ec: e.matmul(
                    psum[1][0:32, :], xp_wh[:, ec * 160 + 128:(ec + 1) * 160], xcvs[pp],
                    start=(ec == 0), stop=(ec == EC - 1), skip_group_check=True), s_t, "t")
                prog.cnt[f"a2_t_{idx}"] = tv
            prog.wait("vector", s_t, tv)
            ev = prog.step("vector", lambda e, uu=uu: e.tensor_copy(
                dtlow_h[:, uu * LT:(uu + 1) * LT], psum[0][:]), s_v, "v")
            prog.cnt[f"a2_ev_{uu}"] = ev
            prog.wait("scalar", s_t, tv)
            prog.wait("scalar", st_sems["bc"], prog.val("st_bc"))
            prog.step("scalar", lambda e: e.activation(
                out=bc_ev[0:32, :], in_=psum[1][0:32, :], func=AF.Copy), s_s, "s")
            es = prog.step("scalar", lambda e: e.activation(
                out=bc_ev[0:16, :], in_=bc_ev[0:16, :], func=AF.Copy, scale=-1.0), s_s, "s")
            prog.cnt[f"a2_es_{uu}"] = es
            prog.wait("gpsimd", s_s, es)
            prog.dma("gpsimd", bn_rh[b, :, lb * LT:(lb + 1) * LT], bc_ev[0:16, :], st_sems["bc"], "st_bc")
            stv = prog.dma("gpsimd", c_rh[b, :, lb * LT:(lb + 1) * LT], bc_ev[16:32, :], st_sems["bc"], "st_bc")

        def phase_a3_unit(uu):
            b, lb = divmod(uu, LB)
            prog.wait("tensor", s_v, prog.cnt[f"a2_ev_{uu}"])
            if uu == 0:
                prog.wait("tensor", s_s, dtcv)
            for ec in range(EC):
                idx = uu * EC + ec
                pp = idx % 2
                pt = psum[2 + pp]
                prog.wait("tensor", s_s, prog.cnt.get(f"a3_ev_{idx - 2}", 0))
                tv = prog.step("tensor", lambda e, pt=pt, ec=ec, uu=uu: e.matmul(
                    pt[:], dt_wh[:, ec * 128:(ec + 1) * 128],
                    dtlow_h[:, uu * LT:(uu + 1) * LT], start=True, stop=True), s_t, "t")
                prog.wait("scalar", s_t, tv)
                prog.wait("scalar", st_sems["lnt"], prog.cnt.get(f"a3_lntst_{pp}", 0))
                prog.step("scalar", lambda e, pt=pt, ec=ec: e.activation(
                    out=sig_t, in_=pt[:], func=AF.Sigmoid, scale=-1.0,
                    bias=dtb_neg[:, ec:ec + 1]), s_s, "s")
                sv = prog.step("scalar", lambda e, pp=pp: e.activation(
                    out=lnt_t[pp], in_=sig_t, func=AF.Ln), s_s, "s")
                prog.cnt[f"a3_ev_{idx}"] = sv
                prog.wait("gpsimd", s_s, sv)
                stv = prog.dma("gpsimd", lnt_f[b, ec, :, lb * LT:(lb + 1) * LT],
                               lnt_t[pp], st_sems["lnt"], "st_lnt")
                prog.cnt[f"a3_lntst_{pp}"] = stv

        for s_i in range(EC):
            phase_a1(EC + s_i)
            if s_i % 4 == 3:
                uu = s_i // 4
                phase_a2_unit(uu)
                phase_a3_unit(uu)

        reset(0)
        bbc_h = carve_h(8 * L)
        cbc_h = carve_h(8 * L)
        ident_h = carve_h(128)
        lnt_em = [carve(L), carve(L)]
        xcv_em = [carve(L), carve(L)]
        sg_em = [carve(L), carve(L)]
        ya_em = [carve(L), carve(L)]
        mdtu_h = carve_h(L)
        dA_b = [carve(L), carve(L)]
        w_hb = carve_h(L)
        s_hb = carve_h(L)
        q_hb = [carve_h(L), carve_h(L)]
        t1_f = carve(L)
        yo_hb = [carve_h(L), carve_h(L)]
        ya_st = [carve(L), carve(L)]
        idsrc = carve(128)

        store_barrier("sync")
        prog.wait("sync", s_t, prog.val("t"))
        prog.wait("sync", s_v, prog.val("v"))
        prog.wait("sync", s_s, prog.val("s"))
        ilv = prog.dma("sync", idsrc, ident_f, s_c, "c")
        prog.wait("vector", s_c, ilv)
        prog.step("vector", lambda e: e.tensor_copy(ident_h, idsrc), s_v, "v")

        bbc3 = bbc_h.rearrange("p (n l) -> p n l", n=8)
        cbc3 = cbc_h.rearrange("p (n l) -> p n l", n=8)

        def phase_b_pass(b, g):
            prog.wait("sync", s_v, prog.val("v"))
            prog.wait("sync", s_s, prog.val("s"))
            prog.wait("sync", st_sems["ya"], prog.val("st_ya"))
            prog.dma("sync", bbc3,
                     bn_rh[b, g * 8:(g + 1) * 8, :].unsqueeze(0).broadcast_to([128, 8, L]),
                     s_ld2, "ld2")
            bcv = prog.dma("sync", cbc3,
                           c_rh[b, g * 8:(g + 1) * 8, :].unsqueeze(0).broadcast_to([128, 8, L]),
                           s_ld2, "ld2")
            prog.wait("vector", s_ld2, bcv)
            prog.wait("vector", s_t, prog.val("t"))

            for ec in range(EC):
                pp = ec % 2
                prog.wait("sync", s_s, prog.cnt.get(f"b_exp_{pp}", 0))
                prog.wait("sync", s_v, prog.cnt.get(f"b_vec_{pp}", 0))
                prog.dma("sync", lnt_em[pp], lnt_f[b, ec], s_b[pp], f"b{pp}")
                lv = prog.dma("sync", xcv_em[pp], xcv_f[b, ec], s_b[pp], f"b{pp}")
                if g == 1:
                    prog.dma("sync", sg_em[pp], sg_f[b, ec], s_b[pp], f"b{pp}")
                    lv = prog.dma("sync", ya_em[pp], ya_f[ec], s_b[pp], f"b{pp}")
                prog.wait("vector", s_b[pp], lv)
                prog.on("vector", lambda e, pp=pp: e.tensor_tensor(
                    out=mdtu_h, in0=lnt_em[pp], in1=xcv_em[pp], op=ALU.mult))
                prog.wait("scalar", s_b[pp], lv)
                for n8 in range(8):
                    n = g * 8 + n8
                    d = n8 % 2
                    k = prog.bump("b_nsq")
                    prog.wait("scalar", s_v, prog.cnt.get(f"b_scanv_{k - 2}", 0))
                    sv = prog.step("scalar", lambda e, d=d, pp=pp, ec=ec, n=n: e.activation(
                        out=dA_b[d], in_=lnt_em[pp], func=AF.Exp,
                        scale=eA[:, ec * N + n:ec * N + n + 1]), s_s, "s")
                    prog.on("vector", lambda e, n8=n8: e.tensor_tensor(
                        out=w_hb, in0=mdtu_h, in1=bbc3[:, n8, :], op=ALU.mult))
                    prog.wait("vector", s_s, sv)
                    vv = prog.step("vector", lambda e, d=d: e.tensor_tensor_scan(
                        out=s_hb, data0=dA_b[d], data1=w_hb,
                        initial=0.0, op0=ALU.mult, op1=ALU.add), s_v, "v")
                    prog.cnt[f"b_scanv_{k}"] = vv
                    prog.wait("vector", s_t, prog.cnt.get(f"b_mmt_{k - 2}", 0))
                    qv = prog.step("vector", lambda e, d=d, n8=n8: e.tensor_tensor(
                        out=q_hb[d], in0=s_hb, in1=cbc3[:, n8, :], op=ALU.mult), s_v, "v")
                    prog.wait("tensor", s_v, qv)
                    if n8 == 0:
                        pc = prog.cnt.get("b_pycons", None)
                        if pc is not None:
                            prog.wait("tensor", {"s": s_s, "v": s_v}[pc[0]], pc[1])
                    for lt in range(4):
                        fn = lambda e, d=d, lt=lt, n8=n8: e.matmul(
                            psum[lt][:], ident_h, q_hb[d][:, lt * 512:(lt + 1) * 512],
                            start=(n8 == 0), stop=(n8 == 7))
                        if lt == 3:
                            tv = prog.step("tensor", fn, s_t, "t")
                            prog.cnt[f"b_mmt_{k}"] = tv
                        else:
                            prog.on("tensor", fn)
                prog.cnt[f"b_exp_{pp}"] = prog.val("s")
                mmv = prog.val("t")
                if g == 0:
                    gi = prog.bump("b_yagi")
                    prog.wait("scalar", s_t, mmv)
                    prog.wait("scalar", st_sems["ya"], prog.cnt.get(f"b_yasg_{gi - 2}", 0))
                    for lt in range(4):
                        fn = lambda e, lt=lt, pp=pp: e.activation(
                            out=ya_st[pp][:, lt * 512:(lt + 1) * 512], in_=psum[lt][:],
                            func=AF.Copy)
                        if lt == 3:
                            sv_e = prog.step("scalar", fn, s_s, "s")
                        else:
                            prog.on("scalar", fn)
                    prog.cnt["b_pycons"] = ("s", sv_e)
                    prog.wait("gpsimd", s_s, sv_e)
                    stv = prog.dma("gpsimd", ya_f[ec], ya_st[pp], st_sems["ya"], "st_ya")
                    prog.cnt[f"b_yasg_{gi}"] = stv
                else:
                    prog.wait("vector", s_t, mmv)
                    for lt in range(4):
                        prog.on("vector", lambda e, lt=lt, pp=pp, ec=ec: e.scalar_tensor_tensor(
                            out=t1_f[:, lt * 512:(lt + 1) * 512],
                            in0=xcv_em[pp][:, lt * 512:(lt + 1) * 512],
                            scalar=dD_sb[:, ec:ec + 1], in1=psum[lt][:],
                            op0=ALU.mult, op1=ALU.add))
                    yfv = prog.step("vector", lambda e, pp=pp: e.tensor_add(
                        t1_f, t1_f, ya_em[pp]), s_v, "v")
                    prog.cnt["b_pycons"] = ("v", yfv)
                    gi = prog.bump("b_ygi")
                    prog.wait("vector", st_sems["y"], prog.cnt.get(f"b_yst_{gi - 2}", 0))
                    vv = prog.step("vector", lambda e, pp=pp: e.tensor_tensor(
                        out=yo_hb[pp], in0=t1_f, in1=sg_em[pp], op=ALU.mult), s_v, "v")
                    prog.wait("gpsimd", s_v, vv)
                    stv = prog.dma("gpsimd", y_hb[b, ec], yo_hb[pp], st_sems["y"], "st_y")
                    prog.cnt[f"b_yst_{gi}"] = stv
                prog.cnt[f"b_vec_{pp}"] = prog.val("v")

        for b in range(BC):
            for g in range(2):
                phase_b_pass(b, g)

        reset(0)
        ow_stage = [carve(1024), carve(1024)]
        ow_h = carve_h(EC * 1024)
        yblk = [carve_h(EC * LT), carve_h(EC * LT)]
        mo_t = carve(8 * LT)

        store_barrier("sync")
        prog.wait("sync", s_st, prog.val("st"))
        prog.wait("sync", s_v, prog.val("v"))
        prog.wait("sync", s_s, prog.val("s"))

        for half in range(2):
            for ec in range(EC):
                pp = ec % 2
                prog.wait("sync", s_v, prog.cnt.get(f"c1w_{half}_{ec - 2}", 0))
                if half > 0 and ec < 2:
                    prog.wait("sync", s_t, prog.cnt.get("c1_t_last", 0))
                lv = prog.dma("sync", ow_stage[pp],
                              w_out_t[ec * 128:(ec + 1) * 128, half * 1024:(half + 1) * 1024],
                              s_ld, "ld")
                prog.wait("vector", s_ld, lv)
                cv = prog.step("vector", lambda e, pp=pp, ec=ec: e.tensor_copy(
                    ow_h[:, ec * 1024:(ec + 1) * 1024], ow_stage[pp]), s_v, "v")
                prog.cnt[f"c1w_{half}_{ec}"] = cv
            wcast = prog.val("v")
            def phase_c1_unit(uu, half=half, wcast=wcast):
                b, lb = divmod(uu, LB)
                pp = uu % 2
                idx = half * BC * LB + uu
                prog.wait("sync", s_t, prog.cnt.get(f"c1_t_{idx - 2}", 0))
                lv = prog.dma("sync", yblk[pp].rearrange("p (ec l) -> p ec l", ec=EC),
                              y_hb[b, :, :, lb * LT:(lb + 1) * LT].rearrange("ec p l -> p ec l"),
                              s_ld, "ld")
                prog.wait("tensor", s_ld, lv)
                if uu == 0:
                    prog.wait("tensor", s_v, wcast)
                for ho in range(8):
                    pt = psum[ho % 2]
                    prog.wait("tensor", s_v, prog.cnt.get(f"c1_ev_{idx}_{ho - 2}", 0))
                    if ho == 0:
                        prog.wait("vector", s_st, prog.cnt.get("c1_most", 0))
                    for ec in range(EC):
                        fn = lambda e, pt=pt, pp=pp, ec=ec, ho=ho: e.matmul(
                            pt[:], ow_h[:, ec * 1024 + ho * 128:ec * 1024 + (ho + 1) * 128],
                            yblk[pp][:, ec * LT:(ec + 1) * LT],
                            start=(ec == 0), stop=(ec == EC - 1))
                        if ec == EC - 1:
                            tv = prog.step("tensor", fn, s_t, "t")
                            prog.cnt[f"c1_t_{idx}"] = tv
                        else:
                            prog.on("tensor", fn)
                    prog.wait("vector", s_t, tv)
                    ev = prog.step("vector", lambda e, ho=ho, pt=pt: e.tensor_copy(
                        mo_t[:, ho * LT:(ho + 1) * LT], pt[:]), s_v, "v")
                    prog.cnt[f"c1_ev_{idx}_{ho}"] = ev
                prog.wait("gpsimd", s_v, prog.val("v"))
                stv = prog.dma("gpsimd",
                               mo_f[b, half * 8:(half + 1) * 8, :, lb * LT:(lb + 1) * LT]
                               .rearrange("ho p l -> p ho l"),
                               mo_t.rearrange("p (ho l) -> p ho l", ho=8), s_st, "st")
                prog.cnt["c1_most"] = stv
            for uu in range(BC * LB):
                phase_c1_unit(uu)
            prog.cnt["c1_t_last"] = prog.val("t")

        reset(0)
        mo_b = carve(HC * LT)
        x_b = carve(HC * LT)
        sq_b = carve(HC * LT)
        x2_b = carve(HC * LT)
        x2l_b = carve_h(HC * LT)
        row_m2 = carve(512)[0:1, :]
        row_q2 = carve(512)[0:1, :]
        row_r2 = carve(512)[0:1, :]

        mo3 = mo_b.rearrange("p (hc l) -> p hc l", hc=HC)
        xb3 = x_b.rearrange("p (hc l) -> p hc l", hc=HC)
        sqb3 = sq_b.rearrange("p (hc l) -> p hc l", hc=HC)
        x2b3 = x2_b.rearrange("p (hc l) -> p hc l", hc=HC)
        x2l3 = x2l_b.rearrange("p (hc l) -> p hc l", hc=HC)

        prog.wait("sync", s_st, prog.val("st"))
        prog.wait("sync", s_v, prog.val("v"))
        prog.wait("sync", s_t, prog.val("t"))
        prev_v = prog.val("v")
        for uu in range(BC * LB):
            b, lb = divmod(uu, LB)
            prog.wait("sync", s_v, prog.cnt.get(f"c2_v_{uu - 1}", 0))
            prog.wait("sync", s_st, prog.cnt.get("c2_st", 0))
            prog.dma("sync", mo3,
                     mo_f[b, :, :, lb * LT:(lb + 1) * LT].rearrange("hc p l -> p hc l"),
                     s_ld, "ld")
            lv = prog.dma("sync", xb3,
                          x_fm[b, :, lb * LT:(lb + 1) * LT].rearrange("(hc p) l -> p hc l", p=128),
                          s_ld, "ld")
            prog.wait("scalar", s_ld, lv)
            prog.wait("tensor", s_ld, lv)
            ln_block(mo3, sqb3, row_m2, row_q2, row_r2, psum[0], psum[1], psum[2], psum[3],
                     prev_v, HC)
            tv = prog.val("t")
            prog.wait("vector", s_t, tv)
            prog.wait("vector", s_ld, lv)
            prog.on("vector", lambda e: e.tensor_tensor(
                out=sqb3, in0=mo3,
                in1=psum[0][:].unsqueeze(1).broadcast_to([128, HC, 512]), op=ALU.subtract))
            prog.on("vector", lambda e: e.tensor_tensor(
                out=sqb3, in0=sqb3,
                in1=psum[1][:].unsqueeze(1).broadcast_to([128, HC, 512]), op=ALU.mult))
            vv = prog.step("vector", lambda e: e.tensor_add(x2_b, sq_b, x_b), s_v, "v")
            prog.wait("scalar", s_v, vv)
            prog.wait("tensor", s_v, vv)
            ln_block(x2b3, sqb3, row_m2, row_q2, row_r2, psum[4], psum[5], psum[6], psum[7],
                     vv, HC)
            tv = prog.val("t")
            prog.wait("vector", s_t, tv)
            prog.on("vector", lambda e: e.tensor_tensor(
                out=sqb3, in0=x2b3,
                in1=psum[4][:].unsqueeze(1).broadcast_to([128, HC, 512]), op=ALU.subtract))
            vv = prog.step("vector", lambda e: e.tensor_tensor(
                out=x2l3, in0=sqb3,
                in1=psum[5][:].unsqueeze(1).broadcast_to([128, HC, 512]), op=ALU.mult), s_v, "v")
            prog.cnt[f"c2_v_{uu}"] = vv
            prev_v = vv
            prog.wait("gpsimd", s_v, vv)
            prog.dma("gpsimd",
                     x2_f[b, :, :, lb * LT:(lb + 1) * LT].rearrange("hc p l -> p hc l"),
                     x2b3, s_st, "st")
            stv = prog.dma("gpsimd",
                           x2ln_h[b, :, :, lb * LT:(lb + 1) * LT].rearrange("hc p l -> p hc l"),
                           x2l3, s_st, "st")
            prog.cnt["c2_st"] = stv

        reset(0)
        w1_stage = [carve(2048), carve(2048)]
        w1_h = carve_h(HC * 2048)
        w2_st = carve(HC * KF)
        w2_h = carve_h(HC * KF)
        x2l_s = [carve_h(HC * LT), carve_h(HC * LT)]
        h1_t = carve_h(HC * LT)
        ex_t = carve(LT)[0:KF, :]
        al_t = carve(LT)[0:KF, :]

        prog.wait("sync", s_st, prog.val("st"))
        prog.wait("sync", s_v, prog.val("v"))
        prog.wait("sync", s_t, prog.val("t"))
        prog.wait("sync", s_s, prog.val("s"))
        for dc in range(HC):
            pp = dc % 2
            prog.wait("sync", s_v, prog.cnt.get(f"c3w_{dc - 2}", 0))
            lv = prog.dma("sync", w1_stage[pp], w1[dc * 128:(dc + 1) * 128, :], s_ld, "ld")
            prog.wait("vector", s_ld, lv)
            cv = prog.step("vector", lambda e, pp=pp, dc=dc: e.tensor_copy(
                w1_h[:, dc * 2048:(dc + 1) * 2048], w1_stage[pp]), s_v, "v")
            prog.cnt[f"c3w_{dc}"] = cv
        lv = prog.dma("sync", w2_st.rearrange("p (dc k) -> p dc k", k=KF),
                      w2.rearrange("(dc p) k -> p dc k", p=128), s_ld, "ld")
        prog.wait("vector", s_ld, lv)
        wc = prog.step("vector", lambda e: e.tensor_copy(w2_h, w2_st), s_v, "v")

        def phase_c3a_unit(uu):
            b, lb = divmod(uu, LB)
            pp = uu % 2
            prog.wait("sync", s_t, prog.cnt.get(f"c3a_t_{uu - 2}", 0))
            lv = prog.dma("sync", x2l_s[pp].rearrange("p (hc l) -> p hc l", hc=HC),
                          x2ln_h[b, :, :, lb * LT:(lb + 1) * LT].rearrange("hc p l -> p hc l"),
                          s_ld, "ld")
            prog.wait("tensor", s_ld, lv)
            if uu == 0:
                prog.wait("tensor", s_v, wc)
            for eo in range(HC):
                pt = psum[eo % 2]
                prog.wait("tensor", s_s, prog.cnt.get(f"c3a_ev_{uu}_{eo - 2}", 0))
                for dc in range(HC):
                    fn = lambda e, pt=pt, pp=pp, dc=dc, eo=eo: e.matmul(
                        pt[:], w1_h[:, dc * 2048 + eo * 128:dc * 2048 + (eo + 1) * 128],
                        x2l_s[pp][:, dc * LT:(dc + 1) * LT],
                        start=(dc == 0), stop=(dc == HC - 1))
                    if dc == HC - 1:
                        tv = prog.step("tensor", fn, s_t, "t")
                    else:
                        prog.on("tensor", fn)
                prog.wait("scalar", s_t, tv)
                if eo == 0:
                    prog.wait("scalar", s_st, prog.cnt.get("c3a_h1st", 0))
                ev = prog.step("scalar", lambda e, eo=eo, pt=pt: e.activation(
                    out=h1_t[:, eo * LT:(eo + 1) * LT], in_=pt[:], func=AF.Relu,
                    bias=b1_sb[:, eo:eo + 1]), s_s, "s")
                prog.cnt[f"c3a_ev_{uu}_{eo}"] = ev
                prog.wait("tensor", s_s, ev)
                tv2 = prog.step("tensor", lambda e, eo=eo: e.matmul(
                    psum[2][0:KF, :], w2_h[:, eo * KF:(eo + 1) * KF],
                    h1_t[:, eo * LT:(eo + 1) * LT],
                    start=(eo == 0), stop=(eo == HC - 1), skip_group_check=True), s_t, "t")
                prog.cnt[f"c3a_t_{uu}"] = tv2
            tvl = prog.val("t")
            prog.wait("scalar", s_t, tvl)
            sv = prog.step("scalar", lambda e: e.activation(
                out=ex_t, in_=psum[2][0:KF, :], func=AF.Exp, bias=b2_col), s_s, "s")
            prog.wait("tensor", s_s, sv)
            tv3 = prog.step("tensor", lambda e: e.matmul(
                psum[3][0:1, :], one_col8, ex_t, start=True, stop=True), s_t, "t")
            prog.wait("vector", s_t, tv3)
            sv2 = prog.step("vector", lambda e: e.reciprocal(
                al_t[0:1, :], psum[3][0:1, :]), s_v, "v")
            prog.wait("tensor", s_v, sv2)
            tv4 = prog.step("tensor", lambda e: e.matmul(
                psum[4][0:KF, :], one_row1[0:1, 0:KF], al_t[0:1, :], start=True, stop=True), s_t, "t")
            prog.wait("vector", s_t, tv4)
            vv = prog.step("vector", lambda e: e.tensor_mul(al_t, ex_t, psum[4][0:KF, :]), s_v, "v")
            prog.wait("gpsimd", s_v, vv)
            prog.wait("gpsimd", s_s, prog.val("s"))
            stv_h = prog.dma("gpsimd",
                             h1_h[b, :, :, lb * LT:(lb + 1) * LT].rearrange("hc p l -> p hc l"),
                             h1_t.rearrange("p (hc l) -> p hc l", hc=HC), s_st, "st")
            prog.cnt["c3a_h1st"] = stv_h
            stv = prog.dma("gpsimd", al_f[b, :, lb * LT:(lb + 1) * LT], al_t, s_st, "st")
            prog.wait("scalar", s_st, stv)
        for uu in range(BC * LB):
            phase_c3a_unit(uu)

        reset(0)
        vt_st = [carve(KR), carve(KR)]
        vt_h = carve_h(HC * KR)
        ut_st = [carve(2048), carve(2048)]
        ut_h = carve_h(KRC * 2048)
        x2l_s2 = [carve_h(HC * LT), carve_h(HC * LT)]
        avx_h = carve_h(KRC * LT)
        alb = carve(KRC * LT)
        fo_t = carve(HC * LT)
        x2_s = carve(HC * LT)
        sq_c = carve(HC * LT)
        row_m3 = carve(512)[0:1, :]
        row_q3 = carve(512)[0:1, :]
        row_r3 = carve(512)[0:1, :]

        fo3 = fo_t.rearrange("p (hc l) -> p hc l", hc=HC)
        sqc3 = sq_c.rearrange("p (hc l) -> p hc l", hc=HC)
        x2s3 = x2_s.rearrange("p (hc l) -> p hc l", hc=HC)

        prog.wait("sync", s_v, prog.val("v"))
        prog.wait("sync", s_s, prog.val("s"))
        prog.wait("sync", s_t, prog.val("t"))
        for dc in range(HC):
            pp = dc % 2
            prog.wait("sync", s_v, prog.cnt.get(f"c3bw_{dc - 2}", 0))
            lv = prog.dma("sync", vt_st[pp], vt[dc * 128:(dc + 1) * 128, :], s_ld, "ld")
            prog.wait("vector", s_ld, lv)
            cv = prog.step("vector", lambda e, pp=pp, dc=dc: e.tensor_copy(
                vt_h[:, dc * KR:(dc + 1) * KR], vt_st[pp]), s_v, "v")
            prog.cnt[f"c3bw_{dc}"] = cv
        for kc in range(KRC):
            pp = kc % 2
            prog.wait("sync", s_v, prog.cnt.get(f"c3bu_{kc - 2}", 0))
            lv = prog.dma("sync", ut_st[pp], ut[kc * 128:(kc + 1) * 128, :], s_ld, "ld")
            prog.wait("vector", s_ld, lv)
            cv = prog.step("vector", lambda e, pp=pp, kc=kc: e.tensor_copy(
                ut_h[:, kc * 2048:(kc + 1) * 2048], ut_st[pp]), s_v, "v")
            prog.cnt[f"c3bu_{kc}"] = cv
        wc2 = prog.val("v")
        prog.wait("sync", s_st, prog.val("st"))

        prev_v_cell = [prog.val("v")]
        def phase_c3b_unit(uu):
            prev_v = prev_v_cell[0]
            b, lb = divmod(uu, LB)
            pp = uu % 2
            prog.wait("sync", s_t, prog.cnt.get(f"c3b_t_{uu - 2}", 0))
            prog.wait("sync", s_v, prog.cnt.get(f"c3b_v_{uu - 1}", 0))
            prog.wait("sync", s_st, prog.cnt.get("c3b_ost", 0))
            prog.dma("sync", x2l_s2[pp].rearrange("p (hc l) -> p hc l", hc=HC),
                     x2ln_h[b, :, :, lb * LT:(lb + 1) * LT].rearrange("hc p l -> p hc l"),
                     s_ld, "ld")
            prog.dma("sync", x2s3,
                     x2_f[b, :, :, lb * LT:(lb + 1) * LT].rearrange("hc p l -> p hc l"),
                     s_ld, "ld")
            for kc in range(KRC):
                for kk in range(2):
                    lva = prog.dma("sync",
                                   alb[kk * RF:(kk + 1) * RF, kc * LT:(kc + 1) * LT],
                                   al_f[b, 2 * kc + kk, lb * LT:(lb + 1) * LT]
                                   .unsqueeze(0).broadcast_to([RF, LT]),
                                   s_ld, "ld")
            prog.wait("tensor", s_ld, lva)
            if uu == 0:
                prog.wait("tensor", s_v, wc2)
            for kc in range(KRC):
                prog.wait("tensor", s_v, prog.cnt.get(f"c3b_av_{uu - 1}_{kc}", 0))
                if kc >= 2:
                    prog.wait("tensor", s_s, prog.cnt.get(f"c3b_lnsq_{uu - 1}", 0))
                for dc in range(HC):
                    fn = lambda e, kc=kc, dc=dc, pp=pp: e.matmul(
                        psum[kc][:], vt_h[:, dc * KR + kc * 128:dc * KR + (kc + 1) * 128],
                        x2l_s2[pp][:, dc * LT:(dc + 1) * LT],
                        start=(dc == 0), stop=(dc == HC - 1))
                    if dc == HC - 1:
                        tv = prog.step("tensor", fn, s_t, "t")
                    else:
                        prog.on("tensor", fn)
                prog.wait("vector", s_t, tv)
                av = prog.step("vector", lambda e, kc=kc: e.tensor_mul(
                    avx_h[:, kc * LT:(kc + 1) * LT], psum[kc][:],
                    alb[:, kc * LT:(kc + 1) * LT]), s_v, "v")
                prog.cnt[f"c3b_av_{uu}_{kc}"] = av
            avv = prog.val("v")
            prog.wait("tensor", s_v, avv)
            for dc in range(HC):
                pt = psum[4 + dc % 2]
                prog.wait("tensor", s_v, prog.cnt.get(f"c3b_fo_{uu}_{dc - 2}", 0))
                for kc in range(KRC):
                    fn = lambda e, pt=pt, dc=dc, kc=kc: e.matmul(
                        pt[:], ut_h[:, kc * 2048 + dc * 128:kc * 2048 + (dc + 1) * 128],
                        avx_h[:, kc * LT:(kc + 1) * LT],
                        start=(kc == 0), stop=(kc == KRC - 1))
                    if kc == KRC - 1:
                        tv = prog.step("tensor", fn, s_t, "t")
                        prog.cnt[f"c3b_t_{uu}"] = tv
                    else:
                        prog.on("tensor", fn)
                prog.wait("vector", s_t, tv)
                fv = prog.step("vector", lambda e, dc=dc, pt=pt: e.tensor_copy(
                    fo_t[:, dc * LT:(dc + 1) * LT], pt[:]), s_v, "v")
                prog.cnt[f"c3b_fo_{uu}_{dc}"] = fv
            fvl = prog.val("v")
            prog.wait("scalar", s_v, fvl)
            prog.wait("tensor", s_v, fvl)
            ln_block(fo3, sqc3, row_m3, row_q3, row_r3, psum[6], psum[7], psum[2], psum[3],
                     prev_v, HC)
            prog.cnt[f"c3b_lnsq_{uu}"] = prog.val("s")
            tv = prog.val("t")
            prog.wait("vector", s_t, tv)
            prog.wait("vector", s_ld, lva)
            prog.on("vector", lambda e: e.tensor_tensor(
                out=sqc3, in0=fo3,
                in1=psum[6][:].unsqueeze(1).broadcast_to([128, HC, 512]), op=ALU.subtract))
            prog.on("vector", lambda e: e.tensor_tensor(
                out=sqc3, in0=sqc3,
                in1=psum[7][:].unsqueeze(1).broadcast_to([128, HC, 512]), op=ALU.mult))
            vv = prog.step("vector", lambda e: e.tensor_add(x2_s, sq_c, x2_s), s_v, "v")
            prog.cnt[f"c3b_v_{uu}"] = vv
            prev_v_cell[0] = vv
            prog.wait("gpsimd", s_v, vv)
            stv = prog.dma("gpsimd",
                           out_fm[b, :, lb * LT:(lb + 1) * LT].rearrange("(hc p) l -> p hc l", p=128),
                           x2s3, s_st, "st")
            prog.cnt["c3b_ost"] = stv
        for uu in range(BC * LB):
            phase_c3b_unit(uu)

        prog.wait("sync", s_st, prog.val("st"))

        @block.sync
        def _(eng):
            for fn in prog.ops["sync"]:
                fn(eng)

        @block.tensor
        def _(eng):
            for fn in prog.ops["tensor"]:
                fn(eng)

        @block.vector
        def _(eng):
            for fn in prog.ops["vector"]:
                fn(eng)

        @block.scalar
        def _(eng):
            for fn in prog.ops["scalar"]:
                fn(eng)

        @block.gpsimd
        def _(eng):
            for fn in prog.ops["gpsimd"]:
                fn(eng)

    return nc


def _prep_core_inputs(inputs, core):
    d = inputs
    x = d["L0_attn_hidden_states"][2 * core:2 * core + 2]
    conv_w = d["L0_attn_conv_w"][:, 0, :]
    m = {
        "x_fm": x.transpose(0, 2, 1),
        "w_in_t": d["L0_attn_in_proj_w"].T,
        "conv_wt": conv_w.reshape(EC, 128, 4).transpose(1, 0, 2).reshape(128, EC * 4),
        "conv_bt": d["L0_attn_conv_b"].reshape(EC, 128).T,
        "w_xp_t": d["L0_attn_x_proj_w"].T,
        "w_dt_t": d["L0_attn_dt_proj_w"].T,
        "dt_bt": d["L0_attn_dt_proj_b"].reshape(EC, 128).T,
        "alog_t": d["L0_attn_A_log"].reshape(EC, 128, N).transpose(1, 0, 2).reshape(128, EC * N),
        "d_t": d["L0_attn_D"].reshape(EC, 128).T,
        "w_out_t": d["L0_attn_out_proj_w"].T,
        "w1": d["L0_ffn_W1"],
        "b1_t": d["L0_ffn_b1"].reshape(HC, 128).T,
        "w2": d["L0_ffn_W2"],
        "b2_t": d["L0_ffn_b2"].reshape(KF, 1),
        "vt": d["L0_ffn_V"].transpose(2, 0, 1).reshape(H, KR),
        "ut": d["L0_ffn_U"].transpose(0, 2, 1).reshape(KR, H),
        "ident_f": np.eye(128, dtype=np.float32),
    }
    return {k: np.ascontiguousarray(np.asarray(v, dtype=np.float32)) for k, v in m.items()}


def kernel(**inputs):
    nc = build_nc(debug=False)
    in_maps = [_prep_core_inputs(inputs, c) for c in range(NCORES)]
    res = run_bass_kernel_spmd(nc, in_maps, core_ids=list(range(NCORES)))
    outs = [r["out_fm"] for r in res.results]
    full = np.concatenate([o.transpose(0, 2, 1) for o in outs], axis=0)
    return np.ascontiguousarray(full.astype(np.float32))


# version 09
# speedup vs baseline: 1.4429x; 1.0426x over previous
import numpy as np
import concourse.bass as bass
import concourse.mybir as mybir
from concourse.bass_utils import run_bass_kernel_spmd

F32 = mybir.dt.float32
BF16 = mybir.dt.bfloat16
AF = mybir.ActivationFunctionType
ALU = mybir.AluOpType

NCORES = 8
BC = 2
L = 2048
H = 2048
HC = 16
E = 4096
EC = 32
EO = 64
N = 16
R = 128
LB = 4
LT = 512
KF = 8
RF = 64
KR = 512
KRC = 4

ENGINES = ("sync", "tensor", "vector", "scalar", "gpsimd")


class Prog:
    def __init__(self):
        self.ops = {e: [] for e in ENGINES}
        self.cnt = {}

    def on(self, eng, fn):
        self.ops[eng].append(fn)

    def bump(self, name, k=1):
        v = self.cnt.get(name, 0) + k
        self.cnt[name] = v
        return v

    def val(self, name):
        return self.cnt.get(name, 0)

    def wait(self, eng, sem_h, v):
        if v > 0:
            self.on(eng, lambda e, s=sem_h, vv=v: e.wait_ge(s, vv))

    def dma(self, eng, out, in_, sem_h, name):
        v = self.bump(name, 16)
        self.on(eng, lambda e, o=out, i=in_, s=sem_h: e.dma_start(out=o, in_=i).then_inc(s, 16))
        return v

    def step(self, eng, fn, sem_h, name):
        v = self.bump(name, 1)
        self.on(eng, lambda e, f=fn, s=sem_h: f(e).then_inc(s, 1))
        return v

    def run(self, eng, fn):
        self.on(eng, fn)


def build_nc(debug=False):
    nc = bass.Bass("TRN2", target_bir_lowering=False)

    x_fm = nc.dram_tensor("x_fm", [BC, H, L], F32, kind="ExternalInput").ap()
    w_in_t = nc.dram_tensor("w_in_t", [H, 2 * E], F32, kind="ExternalInput").ap()
    conv_wt = nc.dram_tensor("conv_wt", [128, EC * 4], F32, kind="ExternalInput").ap()
    conv_bt = nc.dram_tensor("conv_bt", [128, EC], F32, kind="ExternalInput").ap()
    w_xp_t = nc.dram_tensor("w_xp_t", [E, 160], F32, kind="ExternalInput").ap()
    w_dt_t = nc.dram_tensor("w_dt_t", [R, E], F32, kind="ExternalInput").ap()
    dt_bt = nc.dram_tensor("dt_bt", [128, EC], F32, kind="ExternalInput").ap()
    alog_t = nc.dram_tensor("alog_t", [128, EC * N], F32, kind="ExternalInput").ap()
    d_t = nc.dram_tensor("d_t", [128, EC], F32, kind="ExternalInput").ap()
    w_out_t = nc.dram_tensor("w_out_t", [E, H], F32, kind="ExternalInput").ap()
    w1 = nc.dram_tensor("w1", [H, H], F32, kind="ExternalInput").ap()
    b1_t = nc.dram_tensor("b1_t", [128, HC], F32, kind="ExternalInput").ap()
    w2 = nc.dram_tensor("w2", [H, KF], F32, kind="ExternalInput").ap()
    b2_t = nc.dram_tensor("b2_t", [KF, 1], F32, kind="ExternalInput").ap()
    vt = nc.dram_tensor("vt", [H, KR], F32, kind="ExternalInput").ap()
    ut = nc.dram_tensor("ut", [KR, H], F32, kind="ExternalInput").ap()
    ident_f = nc.dram_tensor("ident_f", [128, 128], F32, kind="ExternalInput").ap()

    out_fm = nc.dram_tensor("out_fm", [BC, H, L], F32, kind="ExternalOutput").ap()
    kind_s = "ExternalOutput" if debug else "Internal"
    xcv_f = nc.dram_tensor("xcv_f", [BC, EC, 128, L], F32, kind=kind_s).ap()
    xcv_h = nc.dram_tensor("xcv_h", [BC, EC, 128, L], BF16, kind=kind_s).ap()
    sg_f = nc.dram_tensor("sg_f", [BC, EC, 128, L], F32, kind=kind_s).ap()
    lnt_f = nc.dram_tensor("lnt_f", [BC, EC, 128, L], F32, kind=kind_s).ap()
    bn_rh = nc.dram_tensor("bn_rh", [BC, N, L], BF16, kind=kind_s).ap()
    c_rh = nc.dram_tensor("c_rh", [BC, N, L], BF16, kind=kind_s).ap()
    ya_f = nc.dram_tensor("ya_f", [EC, 128, L], F32, kind=kind_s).ap()
    y_hb = nc.dram_tensor("y_hb", [BC, EC, 128, L], BF16, kind=kind_s).ap()
    mo_f = nc.dram_tensor("mo_f", [BC, HC, 128, L], F32, kind=kind_s).ap()
    x2_f = nc.dram_tensor("x2_f", [BC, HC, 128, L], F32, kind=kind_s).ap()
    x2ln_h = nc.dram_tensor("x2ln_h", [BC, HC, 128, L], BF16, kind=kind_s).ap()
    h1_h = nc.dram_tensor("h1_h", [BC, HC, 128, L], BF16, kind=kind_s).ap()
    al_f = nc.dram_tensor("al_f", [BC, KF, L], F32, kind=kind_s).ap()
    if debug:
        xln_d = nc.dram_tensor("xln_d", [128, HC * BC * L], BF16, kind="ExternalOutput").ap()
        u_d = nc.dram_tensor("u_d", [BC, EC, 128, L], F32, kind="ExternalOutput").ap()
        scan_d = nc.dram_tensor("scan_d", [6, 128, N * LT], F32, kind="ExternalOutput").ap()
        dbg2 = nc.dram_tensor("dbg2", [3, 128, 512], F32, kind="ExternalOutput").ap()

    prog = Prog()
    SB_F32 = 52000

    from contextlib import ExitStack
    with ExitStack() as _stack:
        arena_t = _stack.enter_context(nc.sbuf_tensor("arena", [128, SB_F32], F32))
        pers_t = _stack.enter_context(nc.sbuf_tensor("persist", [128, 1200], F32))
        psum = [_stack.enter_context(nc.psum_tensor(f"pb{i}", [128, 512], F32)) for i in range(8)]
        pb0, pb1, pb2, pb3, pb4, pb5, pb6, pb7 = psum
        s_c = _stack.enter_context(nc.semaphore("s_c"))
        s_ld2 = _stack.enter_context(nc.semaphore("s_ld2"))
        s_w = [_stack.enter_context(nc.semaphore(f"s_w{i}")) for i in range(2)]
        s_b = [_stack.enter_context(nc.semaphore(f"s_b{i}")) for i in range(2)]
        st_sems = {k: _stack.enter_context(nc.semaphore(f"st_{k}"))
                   for k in ("sg", "xcv", "lnt", "bc", "ya", "y", "mo", "x2", "h1", "out", "dbg")}
        s_ld = _stack.enter_context(nc.semaphore("s_ld"))
        s_st = _stack.enter_context(nc.semaphore("s_st"))
        s_t = _stack.enter_context(nc.semaphore("s_t"))
        s_v = _stack.enter_context(nc.semaphore("s_v"))
        s_s = _stack.enter_context(nc.semaphore("s_s"))
        block = _stack.enter_context(nc.Block())

        def store_barrier(eng):
            for k, sem in st_sems.items():
                prog.wait(eng, sem, prog.val("st_" + k))
        psum = [pb0, pb1, pb2, pb3, pb4, pb5, pb6, pb7]
        arena = arena_t.ap()
        pers = pers_t.ap()

        po = [0]

        def pcarve(n):
            a = po[0]
            po[0] += n
            assert po[0] <= 1200
            return pers[:, a:a + n]

        eA = pcarve(EC * N)
        cw_sb = pcarve(EC * 4)
        cb_sb = pcarve(EC)
        dtb_sb = pcarve(EC)
        dtb_neg = pcarve(EC)
        dD_sb = pcarve(EC)
        b1_sb = pcarve(HC)
        b2_col = pcarve(1)[0:KF, :]
        ones_col = pcarve(1)
        one_row1 = pcarve(128)[0:1, :]
        one_col8 = pcarve(1)[0:KF, :]
        eps_col = pcarve(1)

        ao = [0]

        def carve(n):
            a = ao[0]
            ao[0] += n
            assert ao[0] <= SB_F32, f"arena overflow {ao[0]}"
            return arena[:, a:a + n]

        def carve_h(n_bf16):
            assert n_bf16 % 2 == 0
            return carve(n_bf16 // 2).bitcast(BF16)

        def reset(keep=0):
            ao[0] = keep

        prog.dma("sync", eA, alog_t, s_c, "c")
        prog.dma("sync", cw_sb, conv_wt, s_c, "c")
        prog.dma("sync", cb_sb, conv_bt, s_c, "c")
        prog.dma("sync", dtb_sb, dt_bt, s_c, "c")
        prog.dma("sync", dD_sb, d_t, s_c, "c")
        prog.dma("sync", b1_sb, b1_t, s_c, "c")
        cldv = prog.dma("sync", b2_col, b2_t, s_c, "c")
        prog.wait("scalar", s_c, cldv)
        prog.step("scalar", lambda e: e.activation(out=eA, in_=eA, func=AF.Exp), s_s, "s")
        prog.step("scalar", lambda e: e.activation(out=dtb_neg, in_=dtb_sb, func=AF.Copy, scale=-1.0), s_s, "s")
        prog.run("vector", lambda e: e.memset(ones_col, 1.0))
        prog.run("vector", lambda e: e.memset(one_row1, 1.0))
        prog.run("vector", lambda e: e.memset(one_col8, 1.0))
        prog.run("vector", lambda e: e.memset(eps_col, 1e-5))
        consts_v = prog.step("vector", lambda e: e.engine_nop(), s_v, "v")

        def ln_block(xt3, sq3, row_m, row_q, row_r, pm, pr, pst_a, pst_b, prev_v, nch):
            prog.wait("scalar", s_v, prev_v)
            sv = prog.step("scalar", lambda e: e.activation(out=sq3, in_=xt3, func=AF.Square), s_s, "s")
            prog.wait("tensor", s_s, sv)
            for c in range(nch):
                fn = lambda e, c=c: e.matmul(pst_a[0:1, :], ones_col, xt3[:, c, :],
                                             start=(c == 0), stop=(c == nch - 1))
                prog.on("tensor", fn)
            for c in range(nch):
                fn = lambda e, c=c: e.matmul(pst_b[0:1, :], ones_col, sq3[:, c, :],
                                             start=(c == 0), stop=(c == nch - 1))
                if c == nch - 1:
                    prog.step("tensor", fn, s_t, "t")
                else:
                    prog.on("tensor", fn)
            tv = prog.val("t")
            prog.wait("scalar", s_t, tv)
            prog.step("scalar", lambda e: e.activation(out=row_m, in_=pst_a[0:1, :],
                                                       func=AF.Copy, scale=1.0 / (nch * 128)), s_s, "s")
            prog.step("scalar", lambda e: e.activation(out=row_q, in_=pst_b[0:1, :],
                                                       func=AF.Copy, scale=1.0 / (nch * 128)), s_s, "s")
            sv2 = prog.step("scalar", lambda e: e.activation(out=row_r, in_=row_m, func=AF.Square), s_s, "s")
            prog.wait("vector", s_s, sv2)
            vv = prog.step("vector", lambda e: e.tensor_sub(row_q, row_q, row_r), s_v, "v")
            prog.wait("scalar", s_v, vv)
            sv3 = prog.step("scalar", lambda e: e.activation(out=row_r, in_=row_q, func=AF.Sqrt, bias=eps_col[0:1, :]), s_s, "s")
            prog.wait("vector", s_s, sv3)
            vv2 = prog.step("vector", lambda e: e.reciprocal(row_r, row_r), s_v, "v")
            prog.wait("tensor", s_v, vv2)
            prog.wait("tensor", s_v, prev_v)
            prog.on("tensor", lambda e: e.matmul(pm[:], one_row1, row_m, start=True, stop=True))
            prog.step("tensor", lambda e: e.matmul(pr[:], one_row1, row_r, start=True, stop=True), s_t, "t")
            return prog.val("t")

        reset()
        xln_h = carve_h(HC * (BC * L))
        xt_a = carve(HC * LT)
        sq_a = carve(HC * LT)
        row_m = carve(512)[0:1, :]
        row_q = carve(512)[0:1, :]
        row_r = carve(512)[0:1, :]
        A0_KEEP = (HC * (BC * L)) // 2

        xt3 = xt_a.rearrange("p (hc l) -> p hc l", hc=HC)
        sq3 = sq_a.rearrange("p (hc l) -> p hc l", hc=HC)
        xln3 = xln_h.rearrange("p (hc t) -> p hc t", hc=HC)
        prev_v = consts_v
        for uu in range(BC * LB):
            b, lb = divmod(uu, LB)
            prog.wait("sync", s_v, prev_v)
            lv = prog.dma("sync", xt3,
                          x_fm[b, :, lb * LT:(lb + 1) * LT].rearrange("(hc p) l -> p hc l", p=128),
                          s_c, "c")
            prog.wait("scalar", s_c, lv)
            prog.wait("tensor", s_c, lv)
            ln_block(xt3, sq3, row_m, row_q, row_r, psum[0], psum[1], psum[2], psum[3],
                     prev_v, HC)
            tv = prog.val("t")
            prog.wait("vector", s_t, tv)
            prog.on("vector", lambda e: e.tensor_tensor(
                out=sq3, in0=xt3,
                in1=psum[0][:].unsqueeze(1).broadcast_to([128, HC, 512]),
                op=ALU.subtract))
            dst = xln3[:, :, uu * LT:(uu + 1) * LT]
            prev_v = prog.step("vector", lambda e, d=dst: e.tensor_tensor(
                out=d, in0=sq3,
                in1=psum[1][:].unsqueeze(1).broadcast_to([128, HC, 512]),
                op=ALU.mult), s_v, "v")
        ln1_done_v = prev_v
        if debug:
            prog.wait("gpsimd", s_v, ln1_done_v)
            prog.dma("gpsimd", xln_d, xln_h, st_sems["dbg"], "st_dbg")

        reset(A0_KEEP)
        wbuf = [carve(16 * 128), carve(16 * 128)]
        wbuf_h = [carve_h(16 * 128), carve_h(16 * 128)]
        sg_t = [carve(LT), carve(LT)]
        A1_GATE_END = ao[0]
        u_row = [carve(3 + L), carve(3 + L)]
        c0 = carve(L)
        c1 = carve(L)
        xcv_t = [carve(L)]
        xcvh_t = [carve_h(L)]

        prog.run("vector", lambda e: e.memset(u_row[0][:, 0:3], 0.0))
        prog.run("vector", lambda e: e.memset(u_row[1][:, 0:3], 0.0))

        wb3_src = w_in_t
        def phase_a1(eo):
            pp = eo % 2
            t_prev2 = prog.cnt.get(f"a1_t_{eo - 2}", 0)
            prog.wait("sync", s_t, t_prev2)
            if eo < 2:
                prog.wait("sync", s_v, ln1_done_v)
            lv = prog.dma("sync", wbuf[pp].rearrange("p (hc e) -> p hc e", e=128),
                          wb3_src[:, eo * 128:(eo + 1) * 128].rearrange("(hc p) e -> p hc e", p=128),
                          s_w[pp], f"w{pp}")
            prog.wait("scalar", s_w[pp], lv)
            prog.wait("scalar", s_t, t_prev2)
            cv = prog.step("scalar", lambda e, pp=pp: e.activation(
                out=wbuf_h[pp], in_=wbuf[pp], func=AF.Copy), s_s, "s")

            prog.wait("tensor", s_s, cv)
            if eo == 0:
                prog.wait("tensor", s_v, ln1_done_v)
            is_u = eo < EC
            ec = eo if is_u else eo - EC
            for b in range(BC):
                for lb in range(LB):
                    if is_u:
                        pt = psum[b * LB + lb]
                        w_ev = prog.cnt.get(f"a1_ev_{eo - 1}_{b}_{lb}", None)
                    else:
                        bank = 4 + (b * LB + lb) % 4
                        pt = psum[bank]
                        w_ev = prog.cnt.get(f"a1_gbank_{bank}", None)
                    if w_ev is not None:
                        prog.wait("tensor", s_v if w_ev[0] == "v" else s_s, w_ev[1])
                    for hc in range(HC):
                        fn = lambda e, pt=pt, pp=pp, hc=hc, b=b, lb=lb: e.matmul(
                            pt[:], wbuf_h[pp][:, hc * 128:(hc + 1) * 128],
                            xln3[:, hc, (b * LB + lb) * LT:(b * LB + lb + 1) * LT],
                            start=(hc == 0), stop=(hc == HC - 1))
                        if hc == HC - 1:
                            tv = prog.step("tensor", fn, s_t, "t")
                            prog.cnt[f"a1_t_{eo}"] = tv
                        else:
                            prog.on("tensor", fn)
                    if is_u:
                        prog.wait("vector", s_t, tv)
                        ev = prog.step("vector", lambda e, b=b, lb=lb, pt=pt: e.tensor_copy(
                            u_row[b][:, 3 + lb * LT: 3 + (lb + 1) * LT], pt[:]), s_v, "v")
                        prog.cnt[f"a1_ev_{eo}_{b}_{lb}"] = ("v", ev)
                    else:
                        gp = (eo * BC * LB + b * LB + lb) % 2
                        prog.wait("scalar", st_sems["sg"], prog.cnt.get(f"a1_sgst_{gp}", 0))
                        prog.wait("scalar", s_t, tv)
                        ev = prog.step("scalar", lambda e, gp=gp, pt=pt: e.activation(
                            out=sg_t[gp], in_=pt[:], func=AF.Silu), s_s, "s")
                        prog.cnt[f"a1_gbank_{bank}"] = ("s", ev)
                        prog.wait("gpsimd", s_s, ev)
                        stv = prog.dma("gpsimd", sg_f[b, ec, :, lb * LT:(lb + 1) * LT],
                                       sg_t[gp], st_sems["sg"], "st_sg")
                        prog.cnt[f"a1_sgst_{gp}"] = stv
                        prog.cnt[f"a1_sgst_ec_{ec}"] = stv
            if is_u:
                for b in range(BC):
                    if debug:
                        lastev = prog.cnt[f"a1_ev_{eo}_{b}_{LB - 1}"][1]
                        prog.wait("gpsimd", s_v, lastev)
                        prog.dma("gpsimd", u_d[b, ec], u_row[b][:, 3:3 + L], st_sems["dbg"], "st_dbg")
                    prog.wait("vector", s_s, prog.cnt.get("a1_silu", 0))
                    prog.on("vector", lambda e, b=b, ec=ec: e.tensor_scalar_mul(
                        c0, u_row[b][:, 0:L], cw_sb[:, ec * 4 + 0:ec * 4 + 1]))
                    prog.on("vector", lambda e, b=b, ec=ec: e.scalar_tensor_tensor(
                        out=c1, in0=u_row[b][:, 1:1 + L], scalar=cw_sb[:, ec * 4 + 1:ec * 4 + 2],
                        in1=c0, op0=ALU.mult, op1=ALU.add))
                    prog.on("vector", lambda e, b=b, ec=ec: e.scalar_tensor_tensor(
                        out=c0, in0=u_row[b][:, 2:2 + L], scalar=cw_sb[:, ec * 4 + 2:ec * 4 + 3],
                        in1=c1, op0=ALU.mult, op1=ALU.add))
                    vv = prog.step("vector", lambda e, b=b, ec=ec: e.scalar_tensor_tensor(
                        out=c1, in0=u_row[b][:, 3:3 + L], scalar=cw_sb[:, ec * 4 + 3:ec * 4 + 4],
                        in1=c0, op0=ALU.mult, op1=ALU.add), s_v, "v")
                    prog.wait("scalar", s_v, vv)
                    prog.wait("scalar", st_sems["xcv"], prog.cnt.get("a1_xcvst", 0))
                    sv = prog.step("scalar", lambda e, ec=ec: e.activation(
                        out=xcv_t[0], in_=c1, func=AF.Silu, bias=cb_sb[:, ec:ec + 1]), s_s, "s")
                    prog.cnt["a1_silu"] = sv
                    sv2 = prog.step("scalar", lambda e: e.activation(
                        out=xcvh_t[0], in_=xcv_t[0], func=AF.Copy), s_s, "s")
                    prog.wait("gpsimd", s_s, sv2)
                    prog.dma("gpsimd", xcv_f[b, ec], xcv_t[0], st_sems["xcv"], "st_xcv")
                    stv = prog.dma("gpsimd", xcv_h[b, ec], xcvh_t[0], st_sems["xcv"], "st_xcv")
                    prog.cnt["a1_xcvst"] = stv

        for eo in range(EC):
            phase_a1(eo)
        a1u_xcv = prog.val("st_xcv")
        a1u_v = prog.val("v")
        a1u_s = prog.val("s")
        for bank in range(4, 8):
            prog.cnt[f"a1_gbank_{bank}"] = ("v", a1u_v)

        reset(A1_GATE_END)
        wstage = carve(2560)
        xp_wh = carve_h(EC * 160)
        dtlow_h = carve_h(BC * L)
        xcvs = [carve_h(LT), carve_h(LT)]
        bc_ev = carve_h(LT)
        dt_wh = carve_h(EC * 128)
        lnt_t = [carve(LT), carve(LT)]
        sig_t = carve(LT)

        prog.wait("sync", s_v, a1u_v)
        prog.wait("sync", s_s, a1u_s)
        prog.wait("sync", st_sems["xcv"], a1u_xcv)
        for hh in range(2):
            lv = prog.dma("sync", wstage.rearrange("p (ec r) -> p ec r", r=160),
                          w_xp_t[hh * 2048:(hh + 1) * 2048, :]
                          .rearrange("(ec p) r -> p ec r", p=128), s_c, "c")
            prog.wait("scalar", s_c, lv)
            cv = prog.step("scalar", lambda e, hh=hh: e.activation(
                out=xp_wh[:, hh * 2560:(hh + 1) * 2560], in_=wstage,
                func=AF.Copy), s_s, "s")
            prog.wait("sync", s_s, cv)
        xpcv = cv
        for hh in range(2):
            lv = prog.dma("sync", wstage[:, 0:2048], w_dt_t[:, hh * 2048:(hh + 1) * 2048],
                          s_c, "c")
            prog.wait("scalar", s_c, lv)
            dtcv = prog.step("scalar", lambda e, hh=hh: e.activation(
                out=dt_wh[:, hh * 2048:(hh + 1) * 2048], in_=wstage[:, 0:2048],
                func=AF.Copy), s_s, "s")
            if hh == 0:
                prog.wait("sync", s_s, dtcv)

        def phase_a2_unit(uu):
            b, lb = divmod(uu, LB)
            for ec in range(EC):
                pp = ec % 2
                idx = uu * EC + ec
                prog.wait("sync", s_t, prog.cnt.get(f"a2_t_{idx - 2}", 0))
                lv = prog.dma("sync", xcvs[pp], xcv_h[b, ec, :, lb * LT:(lb + 1) * LT],
                              s_b[pp], f"b{pp}")
                prog.wait("tensor", s_b[pp], lv)
                if idx == 0:
                    prog.wait("tensor", s_s, xpcv)
                    prog.wait("tensor", s_v, a1u_v)
                if ec == 0:
                    prog.wait("tensor", s_v, prog.cnt.get(f"a2_ev_{uu - 1}", 0))
                    prog.wait("tensor", s_s, prog.cnt.get(f"a2_es_{uu - 1}", 0))
                prog.on("tensor", lambda e, pp=pp, ec=ec: e.matmul(
                    psum[0][:], xp_wh[:, ec * 160:ec * 160 + 128], xcvs[pp],
                    start=(ec == 0), stop=(ec == EC - 1), skip_group_check=True))
                tv = prog.step("tensor", lambda e, pp=pp, ec=ec: e.matmul(
                    psum[1][0:32, :], xp_wh[:, ec * 160 + 128:(ec + 1) * 160], xcvs[pp],
                    start=(ec == 0), stop=(ec == EC - 1), skip_group_check=True), s_t, "t")
                prog.cnt[f"a2_t_{idx}"] = tv
            prog.wait("vector", s_t, tv)
            ev = prog.step("vector", lambda e, uu=uu: e.tensor_copy(
                dtlow_h[:, uu * LT:(uu + 1) * LT], psum[0][:]), s_v, "v")
            prog.cnt[f"a2_ev_{uu}"] = ev
            prog.wait("scalar", s_t, tv)
            prog.wait("scalar", st_sems["bc"], prog.val("st_bc"))
            prog.step("scalar", lambda e: e.activation(
                out=bc_ev[0:32, :], in_=psum[1][0:32, :], func=AF.Copy), s_s, "s")
            es = prog.step("scalar", lambda e: e.activation(
                out=bc_ev[0:16, :], in_=bc_ev[0:16, :], func=AF.Copy, scale=-1.0), s_s, "s")
            prog.cnt[f"a2_es_{uu}"] = es
            prog.wait("gpsimd", s_s, es)
            prog.dma("gpsimd", bn_rh[b, :, lb * LT:(lb + 1) * LT], bc_ev[0:16, :], st_sems["bc"], "st_bc")
            stv = prog.dma("gpsimd", c_rh[b, :, lb * LT:(lb + 1) * LT], bc_ev[16:32, :], st_sems["bc"], "st_bc")

        def phase_a3_unit(uu):
            b, lb = divmod(uu, LB)
            prog.wait("tensor", s_v, prog.cnt[f"a2_ev_{uu}"])
            if uu == 0:
                prog.wait("tensor", s_s, dtcv)
            for ec in range(EC):
                idx = uu * EC + ec
                pp = idx % 2
                pt = psum[2 + pp]
                prog.wait("tensor", s_s, prog.cnt.get(f"a3_ev_{idx - 2}", 0))
                tv = prog.step("tensor", lambda e, pt=pt, ec=ec, uu=uu: e.matmul(
                    pt[:], dt_wh[:, ec * 128:(ec + 1) * 128],
                    dtlow_h[:, uu * LT:(uu + 1) * LT], start=True, stop=True), s_t, "t")
                prog.wait("scalar", s_t, tv)
                prog.wait("scalar", st_sems["lnt"], prog.cnt.get(f"a3_lntst_{pp}", 0))
                prog.step("scalar", lambda e, pt=pt, ec=ec: e.activation(
                    out=sig_t, in_=pt[:], func=AF.Sigmoid, scale=-1.0,
                    bias=dtb_neg[:, ec:ec + 1]), s_s, "s")
                sv = prog.step("scalar", lambda e, pp=pp: e.activation(
                    out=lnt_t[pp], in_=sig_t, func=AF.Ln), s_s, "s")
                prog.cnt[f"a3_ev_{idx}"] = sv
                prog.wait("gpsimd", s_s, sv)
                stv = prog.dma("gpsimd", lnt_f[b, ec, :, lb * LT:(lb + 1) * LT],
                               lnt_t[pp], st_sems["lnt"], "st_lnt")
                prog.cnt[f"a3_lntst_{pp}"] = stv

        for s_i in range(EC):
            phase_a1(EC + s_i)
            if s_i % 4 == 3:
                uu = s_i // 4
                phase_a2_unit(uu)
                phase_a3_unit(uu)

        reset(0)
        bbc_h = carve_h(8 * L)
        cbc_h = carve_h(8 * L)
        ident_h = carve_h(128)
        lnt_em = [carve(L), carve(L)]
        xcv_em = [carve(L), carve(L)]
        sg_em = [carve(L), carve(L)]
        ya_em = [carve(L), carve(L)]
        mdtu_h = carve_h(L)
        dA_b = [carve(L), carve(L)]
        w_hb = carve_h(L)
        s_hb = carve_h(L)
        q_hb = [carve_h(L), carve_h(L)]
        t1_f = carve(L)
        yo_hb = [carve_h(L), carve_h(L)]
        ya_st = [carve(L), carve(L)]
        idsrc = carve(128)

        store_barrier("sync")
        prog.wait("sync", s_t, prog.val("t"))
        prog.wait("sync", s_v, prog.val("v"))
        prog.wait("sync", s_s, prog.val("s"))
        ilv = prog.dma("sync", idsrc, ident_f, s_c, "c")
        prog.wait("vector", s_c, ilv)
        prog.step("vector", lambda e: e.tensor_copy(ident_h, idsrc), s_v, "v")

        bbc3 = bbc_h.rearrange("p (n l) -> p n l", n=8)
        cbc3 = cbc_h.rearrange("p (n l) -> p n l", n=8)

        def phase_b_pass(b, g):
            prog.wait("sync", s_v, prog.val("v"))
            prog.wait("sync", s_s, prog.val("s"))
            prog.wait("sync", st_sems["ya"], prog.val("st_ya"))
            prog.dma("sync", bbc3,
                     bn_rh[b, g * 8:(g + 1) * 8, :].unsqueeze(0).broadcast_to([128, 8, L]),
                     s_ld2, "ld2")
            bcv = prog.dma("sync", cbc3,
                           c_rh[b, g * 8:(g + 1) * 8, :].unsqueeze(0).broadcast_to([128, 8, L]),
                           s_ld2, "ld2")
            prog.wait("vector", s_ld2, bcv)
            prog.wait("vector", s_t, prog.val("t"))

            for ec in range(EC):
                pp = ec % 2
                prog.wait("sync", s_s, prog.cnt.get(f"b_exp_{pp}", 0))
                prog.wait("sync", s_v, prog.cnt.get(f"b_vec_{pp}", 0))
                prog.dma("sync", lnt_em[pp], lnt_f[b, ec], s_b[pp], f"b{pp}")
                lv = prog.dma("sync", xcv_em[pp], xcv_f[b, ec], s_b[pp], f"b{pp}")
                if g == 1:
                    prog.dma("sync", sg_em[pp], sg_f[b, ec], s_b[pp], f"b{pp}")
                    lv = prog.dma("sync", ya_em[pp], ya_f[ec], s_b[pp], f"b{pp}")
                prog.wait("vector", s_b[pp], lv)
                prog.on("vector", lambda e, pp=pp: e.tensor_tensor(
                    out=mdtu_h, in0=lnt_em[pp], in1=xcv_em[pp], op=ALU.mult))
                prog.wait("scalar", s_b[pp], lv)
                for n8 in range(8):
                    n = g * 8 + n8
                    d = n8 % 2
                    k = prog.bump("b_nsq")
                    prog.wait("scalar", s_v, prog.cnt.get(f"b_scanv_{k - 2}", 0))
                    sv = prog.step("scalar", lambda e, d=d, pp=pp, ec=ec, n=n: e.activation(
                        out=dA_b[d], in_=lnt_em[pp], func=AF.Exp,
                        scale=eA[:, ec * N + n:ec * N + n + 1]), s_s, "s")
                    prog.on("vector", lambda e, n8=n8: e.tensor_tensor(
                        out=w_hb, in0=mdtu_h, in1=bbc3[:, n8, :], op=ALU.mult))
                    prog.wait("vector", s_s, sv)
                    vv = prog.step("vector", lambda e, d=d: e.tensor_tensor_scan(
                        out=s_hb, data0=dA_b[d], data1=w_hb,
                        initial=0.0, op0=ALU.mult, op1=ALU.add), s_v, "v")
                    prog.cnt[f"b_scanv_{k}"] = vv
                    prog.wait("vector", s_t, prog.cnt.get(f"b_mmt_{k - 2}", 0))
                    qv = prog.step("vector", lambda e, d=d, n8=n8: e.tensor_tensor(
                        out=q_hb[d], in0=s_hb, in1=cbc3[:, n8, :], op=ALU.mult), s_v, "v")
                    prog.wait("tensor", s_v, qv)
                    if n8 == 0:
                        pc = prog.cnt.get("b_pycons", None)
                        if pc is not None:
                            prog.wait("tensor", {"s": s_s, "v": s_v}[pc[0]], pc[1])
                    for lt in range(4):
                        fn = lambda e, d=d, lt=lt, n8=n8: e.matmul(
                            psum[lt][:], ident_h, q_hb[d][:, lt * 512:(lt + 1) * 512],
                            start=(n8 == 0), stop=(n8 == 7))
                        if lt == 3:
                            tv = prog.step("tensor", fn, s_t, "t")
                            prog.cnt[f"b_mmt_{k}"] = tv
                        else:
                            prog.on("tensor", fn)
                prog.cnt[f"b_exp_{pp}"] = prog.val("s")
                mmv = prog.val("t")
                if g == 0:
                    gi = prog.bump("b_yagi")
                    prog.wait("scalar", s_t, mmv)
                    prog.wait("scalar", st_sems["ya"], prog.cnt.get(f"b_yasg_{gi - 2}", 0))
                    for lt in range(4):
                        fn = lambda e, lt=lt, pp=pp: e.activation(
                            out=ya_st[pp][:, lt * 512:(lt + 1) * 512], in_=psum[lt][:],
                            func=AF.Copy)
                        if lt == 3:
                            sv_e = prog.step("scalar", fn, s_s, "s")
                        else:
                            prog.on("scalar", fn)
                    prog.cnt["b_pycons"] = ("s", sv_e)
                    prog.wait("gpsimd", s_s, sv_e)
                    stv = prog.dma("gpsimd", ya_f[ec], ya_st[pp], st_sems["ya"], "st_ya")
                    prog.cnt[f"b_yasg_{gi}"] = stv
                else:
                    prog.wait("vector", s_t, mmv)
                    for lt in range(4):
                        prog.on("vector", lambda e, lt=lt, pp=pp, ec=ec: e.scalar_tensor_tensor(
                            out=t1_f[:, lt * 512:(lt + 1) * 512],
                            in0=xcv_em[pp][:, lt * 512:(lt + 1) * 512],
                            scalar=dD_sb[:, ec:ec + 1], in1=psum[lt][:],
                            op0=ALU.mult, op1=ALU.add))
                    yfv = prog.step("vector", lambda e, pp=pp: e.tensor_add(
                        t1_f, t1_f, ya_em[pp]), s_v, "v")
                    prog.cnt["b_pycons"] = ("v", yfv)
                    gi = prog.bump("b_ygi")
                    prog.wait("vector", st_sems["y"], prog.cnt.get(f"b_yst_{gi - 2}", 0))
                    vv = prog.step("vector", lambda e, pp=pp: e.tensor_tensor(
                        out=yo_hb[pp], in0=t1_f, in1=sg_em[pp], op=ALU.mult), s_v, "v")
                    prog.wait("gpsimd", s_v, vv)
                    stv = prog.dma("gpsimd", y_hb[b, ec], yo_hb[pp], st_sems["y"], "st_y")
                    prog.cnt[f"b_yst_{gi}"] = stv
                prog.cnt[f"b_vec_{pp}"] = prog.val("v")

        for b in range(BC):
            for g in range(2):
                phase_b_pass(b, g)

        reset(0)
        ow_stage = [carve(1024), carve(1024)]
        ow_h = carve_h(EC * 1024)
        yblk = [carve_h(EC * LT), carve_h(EC * LT)]
        mo_t = carve(8 * LT)

        store_barrier("sync")
        prog.wait("sync", s_st, prog.val("st"))
        prog.wait("sync", s_v, prog.val("v"))
        prog.wait("sync", s_s, prog.val("s"))

        for half in range(2):
            for ec in range(EC):
                pp = ec % 2
                prog.wait("sync", s_v, prog.cnt.get(f"c1w_{half}_{ec - 2}", 0))
                if half > 0 and ec < 2:
                    prog.wait("sync", s_t, prog.cnt.get("c1_t_last", 0))
                lv = prog.dma("sync", ow_stage[pp],
                              w_out_t[ec * 128:(ec + 1) * 128, half * 1024:(half + 1) * 1024],
                              s_ld, "ld")
                prog.wait("vector", s_ld, lv)
                cv = prog.step("vector", lambda e, pp=pp, ec=ec: e.tensor_copy(
                    ow_h[:, ec * 1024:(ec + 1) * 1024], ow_stage[pp]), s_v, "v")
                prog.cnt[f"c1w_{half}_{ec}"] = cv
            wcast = prog.val("v")
            def phase_c1_unit(uu, half=half, wcast=wcast):
                b, lb = divmod(uu, LB)
                pp = uu % 2
                idx = half * BC * LB + uu
                prog.wait("sync", s_t, prog.cnt.get(f"c1_t_{idx - 2}", 0))
                lv = prog.dma("sync", yblk[pp].rearrange("p (ec l) -> p ec l", ec=EC),
                              y_hb[b, :, :, lb * LT:(lb + 1) * LT].rearrange("ec p l -> p ec l"),
                              s_ld, "ld")
                prog.wait("tensor", s_ld, lv)
                if uu == 0:
                    prog.wait("tensor", s_v, wcast)
                for ho in range(8):
                    pt = psum[ho % 2]
                    prog.wait("tensor", s_v, prog.cnt.get(f"c1_ev_{idx}_{ho - 2}", 0))
                    if ho == 0:
                        prog.wait("vector", s_st, prog.cnt.get("c1_most", 0))
                    for ec in range(EC):
                        fn = lambda e, pt=pt, pp=pp, ec=ec, ho=ho: e.matmul(
                            pt[:], ow_h[:, ec * 1024 + ho * 128:ec * 1024 + (ho + 1) * 128],
                            yblk[pp][:, ec * LT:(ec + 1) * LT],
                            start=(ec == 0), stop=(ec == EC - 1))
                        if ec == EC - 1:
                            tv = prog.step("tensor", fn, s_t, "t")
                            prog.cnt[f"c1_t_{idx}"] = tv
                        else:
                            prog.on("tensor", fn)
                    prog.wait("vector", s_t, tv)
                    ev = prog.step("vector", lambda e, ho=ho, pt=pt: e.tensor_copy(
                        mo_t[:, ho * LT:(ho + 1) * LT], pt[:]), s_v, "v")
                    prog.cnt[f"c1_ev_{idx}_{ho}"] = ev
                prog.wait("gpsimd", s_v, prog.val("v"))
                stv = prog.dma("gpsimd",
                               mo_f[b, half * 8:(half + 1) * 8, :, lb * LT:(lb + 1) * LT]
                               .rearrange("ho p l -> p ho l"),
                               mo_t.rearrange("p (ho l) -> p ho l", ho=8), s_st, "st")
                prog.cnt["c1_most"] = stv
            for uu in range(BC * LB):
                phase_c1_unit(uu)
            prog.cnt["c1_t_last"] = prog.val("t")

        reset(0)
        mo_b = carve(HC * LT)
        x_b = [carve(HC * LT), carve(HC * LT)]
        sq_b = [carve(HC * LT), carve(HC * LT)]
        x2l_one = carve_h(HC * LT)
        x2l_b = [x2l_one, x2l_one]
        rows2 = [carve(1536), carve(1536)]

        mo3 = mo_b.rearrange("p (hc l) -> p hc l", hc=HC)
        xb3 = [x.rearrange("p (hc l) -> p hc l", hc=HC) for x in x_b]
        sqb3 = [x.rearrange("p (hc l) -> p hc l", hc=HC) for x in sq_b]
        x2l3 = [x.rearrange("p (hc l) -> p hc l", hc=HC) for x in x2l_b]

        prog.wait("sync", s_st, prog.val("st"))
        prog.wait("sync", s_v, prog.val("v"))
        prog.wait("sync", s_t, prog.val("t"))
        for uu in range(BC * LB):
            b, lb = divmod(uu, LB)
            d = uu % 2
            psA2, psB2 = (psum[0], psum[1]) if d == 0 else (psum[4], psum[5])
            psA3, psB3 = (psum[2], psum[3]) if d == 0 else (psum[6], psum[7])
            rm = rows2[d][0:1, 0:512]
            rq = rows2[d][0:1, 512:1024]
            rr = rows2[d][0:1, 1024:1536]
            prog.wait("sync", s_v, prog.cnt.get("c2_mocons", 0))
            prog.wait("sync", s_v, prog.cnt.get(f"c2_vfin_{d}", 0))
            prog.wait("sync", s_st, prog.cnt.get(f"c2_st_{d}", 0))
            prog.dma("sync", mo3,
                     mo_f[b, :, :, lb * LT:(lb + 1) * LT].rearrange("hc p l -> p hc l"),
                     s_ld, "ld")
            lv = prog.dma("sync", xb3[d],
                          x_fm[b, :, lb * LT:(lb + 1) * LT].rearrange("(hc p) l -> p hc l", p=128),
                          s_ld, "ld")
            prog.wait("scalar", s_ld, lv)
            prog.wait("tensor", s_ld, lv)
            ln_block(mo3, sqb3[d], rm, rq, rr, psA2, psB2, psA2, psB2,
                     prog.cnt.get(f"c2_vfin_{d}", 0), HC)
            tv = prog.val("t")
            prog.wait("vector", s_t, tv)
            prog.wait("vector", s_ld, lv)
            prog.on("vector", lambda e, psA2=psA2: e.tensor_tensor(
                out=mo3, in0=mo3,
                in1=psA2[:].unsqueeze(1).broadcast_to([128, HC, 512]), op=ALU.subtract))
            prog.on("vector", lambda e, psB2=psB2: e.tensor_tensor(
                out=mo3, in0=mo3,
                in1=psB2[:].unsqueeze(1).broadcast_to([128, HC, 512]), op=ALU.mult))
            vv1 = prog.step("vector", lambda e, d=d: e.tensor_add(x_b[d], x_b[d], mo_b),
                            s_v, "v")
            prog.cnt["c2_mocons"] = vv1
            prog.wait("scalar", s_v, vv1)
            prog.wait("tensor", s_v, vv1)
            ln_block(xb3[d], sqb3[d], rm, rq, rr, psA3, psB3, psA3, psB3, vv1, HC)
            tv = prog.val("t")
            prog.wait("vector", s_t, tv)
            prog.on("vector", lambda e, d=d, psA3=psA3: e.tensor_tensor(
                out=sqb3[d], in0=xb3[d],
                in1=psA3[:].unsqueeze(1).broadcast_to([128, HC, 512]), op=ALU.subtract))
            prog.wait("vector", s_st, prog.cnt.get("c2_st", 0))
            vv = prog.step("vector", lambda e, d=d, psB3=psB3: e.tensor_tensor(
                out=x2l3[d], in0=sqb3[d],
                in1=psB3[:].unsqueeze(1).broadcast_to([128, HC, 512]), op=ALU.mult), s_v, "v")
            prog.cnt[f"c2_vfin_{d}"] = vv
            prog.cnt[f"c2_v_{uu}"] = vv
            prog.wait("gpsimd", s_v, vv)
            prog.dma("gpsimd",
                     x2_f[b, :, :, lb * LT:(lb + 1) * LT].rearrange("hc p l -> p hc l"),
                     xb3[d], s_st, "st")
            stv = prog.dma("gpsimd",
                           x2ln_h[b, :, :, lb * LT:(lb + 1) * LT].rearrange("hc p l -> p hc l"),
                           x2l3[d], s_st, "st")
            prog.cnt[f"c2_st_{d}"] = stv
            prog.cnt["c2_st"] = stv

        reset(0)
        w1_stage = [carve(2048), carve(2048)]
        w1_h = carve_h(HC * 2048)
        w2_st = carve(HC * KF)
        w2_h = carve_h(HC * KF)
        x2l_s = [carve_h(HC * LT), carve_h(HC * LT)]
        h1_t = carve_h(HC * LT)
        ex_t = carve(LT)[0:KF, :]
        al_t = carve(LT)[0:KF, :]

        prog.wait("sync", s_st, prog.val("st"))
        prog.wait("sync", s_v, prog.val("v"))
        prog.wait("sync", s_t, prog.val("t"))
        prog.wait("sync", s_s, prog.val("s"))
        for dc in range(HC):
            pp = dc % 2
            prog.wait("sync", s_v, prog.cnt.get(f"c3w_{dc - 2}", 0))
            lv = prog.dma("sync", w1_stage[pp], w1[dc * 128:(dc + 1) * 128, :], s_ld, "ld")
            prog.wait("vector", s_ld, lv)
            cv = prog.step("vector", lambda e, pp=pp, dc=dc: e.tensor_copy(
                w1_h[:, dc * 2048:(dc + 1) * 2048], w1_stage[pp]), s_v, "v")
            prog.cnt[f"c3w_{dc}"] = cv
        lv = prog.dma("sync", w2_st.rearrange("p (dc k) -> p dc k", k=KF),
                      w2.rearrange("(dc p) k -> p dc k", p=128), s_ld, "ld")
        prog.wait("vector", s_ld, lv)
        wc = prog.step("vector", lambda e: e.tensor_copy(w2_h, w2_st), s_v, "v")

        def phase_c3a_unit(uu):
            b, lb = divmod(uu, LB)
            pp = uu % 2
            prog.wait("sync", s_t, prog.cnt.get(f"c3a_t_{uu - 2}", 0))
            lv = prog.dma("sync", x2l_s[pp].rearrange("p (hc l) -> p hc l", hc=HC),
                          x2ln_h[b, :, :, lb * LT:(lb + 1) * LT].rearrange("hc p l -> p hc l"),
                          s_ld, "ld")
            prog.wait("tensor", s_ld, lv)
            if uu == 0:
                prog.wait("tensor", s_v, wc)
            for eo in range(HC):
                pt = psum[eo % 2]
                prog.wait("tensor", s_s, prog.cnt.get(f"c3a_ev_{uu}_{eo - 2}", 0))
                for dc in range(HC):
                    fn = lambda e, pt=pt, pp=pp, dc=dc, eo=eo: e.matmul(
                        pt[:], w1_h[:, dc * 2048 + eo * 128:dc * 2048 + (eo + 1) * 128],
                        x2l_s[pp][:, dc * LT:(dc + 1) * LT],
                        start=(dc == 0), stop=(dc == HC - 1))
                    if dc == HC - 1:
                        tv = prog.step("tensor", fn, s_t, "t")
                    else:
                        prog.on("tensor", fn)
                prog.wait("scalar", s_t, tv)
                if eo == 0:
                    prog.wait("scalar", s_st, prog.cnt.get("c3a_h1st", 0))
                ev = prog.step("scalar", lambda e, eo=eo, pt=pt: e.activation(
                    out=h1_t[:, eo * LT:(eo + 1) * LT], in_=pt[:], func=AF.Relu,
                    bias=b1_sb[:, eo:eo + 1]), s_s, "s")
                prog.cnt[f"c3a_ev_{uu}_{eo}"] = ev
                prog.wait("tensor", s_s, ev)
                tv2 = prog.step("tensor", lambda e, eo=eo: e.matmul(
                    psum[2][0:KF, :], w2_h[:, eo * KF:(eo + 1) * KF],
                    h1_t[:, eo * LT:(eo + 1) * LT],
                    start=(eo == 0), stop=(eo == HC - 1), skip_group_check=True), s_t, "t")
                prog.cnt[f"c3a_t_{uu}"] = tv2
            tvl = prog.val("t")
            prog.wait("scalar", s_t, tvl)
            sv = prog.step("scalar", lambda e: e.activation(
                out=ex_t, in_=psum[2][0:KF, :], func=AF.Exp, bias=b2_col), s_s, "s")
            prog.wait("tensor", s_s, sv)
            tv3 = prog.step("tensor", lambda e: e.matmul(
                psum[3][0:1, :], one_col8, ex_t, start=True, stop=True), s_t, "t")
            prog.wait("vector", s_t, tv3)
            sv2 = prog.step("vector", lambda e: e.reciprocal(
                al_t[0:1, :], psum[3][0:1, :]), s_v, "v")
            prog.wait("tensor", s_v, sv2)
            tv4 = prog.step("tensor", lambda e: e.matmul(
                psum[4][0:KF, :], one_row1[0:1, 0:KF], al_t[0:1, :], start=True, stop=True), s_t, "t")
            prog.wait("vector", s_t, tv4)
            vv = prog.step("vector", lambda e: e.tensor_mul(al_t, ex_t, psum[4][0:KF, :]), s_v, "v")
            prog.wait("gpsimd", s_v, vv)
            prog.wait("gpsimd", s_s, prog.val("s"))
            stv_h = prog.dma("gpsimd",
                             h1_h[b, :, :, lb * LT:(lb + 1) * LT].rearrange("hc p l -> p hc l"),
                             h1_t.rearrange("p (hc l) -> p hc l", hc=HC), s_st, "st")
            prog.cnt["c3a_h1st"] = stv_h
            stv = prog.dma("gpsimd", al_f[b, :, lb * LT:(lb + 1) * LT], al_t, s_st, "st")
            prog.wait("scalar", s_st, stv)
        for uu in range(BC * LB):
            phase_c3a_unit(uu)

        reset(0)
        vt_st = [carve(KR), carve(KR)]
        vt_h = carve_h(HC * KR)
        ut_st = [carve(2048), carve(2048)]
        ut_h = carve_h(KRC * 2048)
        x2l_s2 = [carve_h(HC * LT), carve_h(HC * LT)]
        avx_h = carve_h(KRC * LT)
        alb = carve(KRC * LT)
        fo_t = carve(HC * LT)
        x2_s = carve(HC * LT)
        sq_c = carve(HC * LT)
        row_m3 = carve(512)[0:1, :]
        row_q3 = carve(512)[0:1, :]
        row_r3 = carve(512)[0:1, :]

        fo3 = fo_t.rearrange("p (hc l) -> p hc l", hc=HC)
        sqc3 = sq_c.rearrange("p (hc l) -> p hc l", hc=HC)
        x2s3 = x2_s.rearrange("p (hc l) -> p hc l", hc=HC)

        prog.wait("sync", s_v, prog.val("v"))
        prog.wait("sync", s_s, prog.val("s"))
        prog.wait("sync", s_t, prog.val("t"))
        for dc in range(HC):
            pp = dc % 2
            prog.wait("sync", s_v, prog.cnt.get(f"c3bw_{dc - 2}", 0))
            lv = prog.dma("sync", vt_st[pp], vt[dc * 128:(dc + 1) * 128, :], s_ld, "ld")
            prog.wait("vector", s_ld, lv)
            cv = prog.step("vector", lambda e, pp=pp, dc=dc: e.tensor_copy(
                vt_h[:, dc * KR:(dc + 1) * KR], vt_st[pp]), s_v, "v")
            prog.cnt[f"c3bw_{dc}"] = cv
        for kc in range(KRC):
            pp = kc % 2
            prog.wait("sync", s_v, prog.cnt.get(f"c3bu_{kc - 2}", 0))
            lv = prog.dma("sync", ut_st[pp], ut[kc * 128:(kc + 1) * 128, :], s_ld, "ld")
            prog.wait("vector", s_ld, lv)
            cv = prog.step("vector", lambda e, pp=pp, kc=kc: e.tensor_copy(
                ut_h[:, kc * 2048:(kc + 1) * 2048], ut_st[pp]), s_v, "v")
            prog.cnt[f"c3bu_{kc}"] = cv
        wc2 = prog.val("v")
        prog.wait("sync", s_st, prog.val("st"))

        prev_v_cell = [prog.val("v")]
        def phase_c3b_unit(uu):
            prev_v = prev_v_cell[0]
            b, lb = divmod(uu, LB)
            pp = uu % 2
            prog.wait("sync", s_t, prog.cnt.get(f"c3b_t_{uu - 2}", 0))
            prog.wait("sync", s_v, prog.cnt.get(f"c3b_v_{uu - 1}", 0))
            prog.wait("sync", s_st, prog.cnt.get("c3b_ost", 0))
            prog.dma("sync", x2l_s2[pp].rearrange("p (hc l) -> p hc l", hc=HC),
                     x2ln_h[b, :, :, lb * LT:(lb + 1) * LT].rearrange("hc p l -> p hc l"),
                     s_ld, "ld")
            prog.dma("sync", x2s3,
                     x2_f[b, :, :, lb * LT:(lb + 1) * LT].rearrange("hc p l -> p hc l"),
                     s_ld, "ld")
            for kc in range(KRC):
                for kk in range(2):
                    lva = prog.dma("sync",
                                   alb[kk * RF:(kk + 1) * RF, kc * LT:(kc + 1) * LT],
                                   al_f[b, 2 * kc + kk, lb * LT:(lb + 1) * LT]
                                   .unsqueeze(0).broadcast_to([RF, LT]),
                                   s_ld, "ld")
            prog.wait("tensor", s_ld, lva)
            if uu == 0:
                prog.wait("tensor", s_v, wc2)
            for kc in range(KRC):
                prog.wait("tensor", s_v, prog.cnt.get(f"c3b_av_{uu - 1}_{kc}", 0))
                if kc >= 2:
                    prog.wait("tensor", s_s, prog.cnt.get(f"c3b_lnsq_{uu - 1}", 0))
                for dc in range(HC):
                    fn = lambda e, kc=kc, dc=dc, pp=pp: e.matmul(
                        psum[kc][:], vt_h[:, dc * KR + kc * 128:dc * KR + (kc + 1) * 128],
                        x2l_s2[pp][:, dc * LT:(dc + 1) * LT],
                        start=(dc == 0), stop=(dc == HC - 1))
                    if dc == HC - 1:
                        tv = prog.step("tensor", fn, s_t, "t")
                    else:
                        prog.on("tensor", fn)
                prog.wait("vector", s_t, tv)
                av = prog.step("vector", lambda e, kc=kc: e.tensor_mul(
                    avx_h[:, kc * LT:(kc + 1) * LT], psum[kc][:],
                    alb[:, kc * LT:(kc + 1) * LT]), s_v, "v")
                prog.cnt[f"c3b_av_{uu}_{kc}"] = av
            avv = prog.val("v")
            prog.wait("tensor", s_v, avv)
            for dc in range(HC):
                pt = psum[4 + dc % 2]
                prog.wait("tensor", s_v, prog.cnt.get(f"c3b_fo_{uu}_{dc - 2}", 0))
                for kc in range(KRC):
                    fn = lambda e, pt=pt, dc=dc, kc=kc: e.matmul(
                        pt[:], ut_h[:, kc * 2048 + dc * 128:kc * 2048 + (dc + 1) * 128],
                        avx_h[:, kc * LT:(kc + 1) * LT],
                        start=(kc == 0), stop=(kc == KRC - 1))
                    if kc == KRC - 1:
                        tv = prog.step("tensor", fn, s_t, "t")
                        prog.cnt[f"c3b_t_{uu}"] = tv
                    else:
                        prog.on("tensor", fn)
                prog.wait("vector", s_t, tv)
                fv = prog.step("vector", lambda e, dc=dc, pt=pt: e.tensor_copy(
                    fo_t[:, dc * LT:(dc + 1) * LT], pt[:]), s_v, "v")
                prog.cnt[f"c3b_fo_{uu}_{dc}"] = fv
            fvl = prog.val("v")
            prog.wait("scalar", s_v, fvl)
            prog.wait("tensor", s_v, fvl)
            ln_block(fo3, sqc3, row_m3, row_q3, row_r3, psum[6], psum[7], psum[2], psum[3],
                     prev_v, HC)
            prog.cnt[f"c3b_lnsq_{uu}"] = prog.val("s")
            tv = prog.val("t")
            prog.wait("vector", s_t, tv)
            prog.wait("vector", s_ld, lva)
            prog.on("vector", lambda e: e.tensor_tensor(
                out=sqc3, in0=fo3,
                in1=psum[6][:].unsqueeze(1).broadcast_to([128, HC, 512]), op=ALU.subtract))
            prog.on("vector", lambda e: e.tensor_tensor(
                out=sqc3, in0=sqc3,
                in1=psum[7][:].unsqueeze(1).broadcast_to([128, HC, 512]), op=ALU.mult))
            vv = prog.step("vector", lambda e: e.tensor_add(x2_s, sq_c, x2_s), s_v, "v")
            prog.cnt[f"c3b_v_{uu}"] = vv
            prev_v_cell[0] = vv
            prog.wait("gpsimd", s_v, vv)
            stv = prog.dma("gpsimd",
                           out_fm[b, :, lb * LT:(lb + 1) * LT].rearrange("(hc p) l -> p hc l", p=128),
                           x2s3, s_st, "st")
            prog.cnt["c3b_ost"] = stv
        for uu in range(BC * LB):
            phase_c3b_unit(uu)

        prog.wait("sync", s_st, prog.val("st"))

        @block.sync
        def _(eng):
            for fn in prog.ops["sync"]:
                fn(eng)

        @block.tensor
        def _(eng):
            for fn in prog.ops["tensor"]:
                fn(eng)

        @block.vector
        def _(eng):
            for fn in prog.ops["vector"]:
                fn(eng)

        @block.scalar
        def _(eng):
            for fn in prog.ops["scalar"]:
                fn(eng)

        @block.gpsimd
        def _(eng):
            for fn in prog.ops["gpsimd"]:
                fn(eng)

    return nc


def _prep_core_inputs(inputs, core):
    d = inputs
    x = d["L0_attn_hidden_states"][2 * core:2 * core + 2]
    conv_w = d["L0_attn_conv_w"][:, 0, :]
    m = {
        "x_fm": x.transpose(0, 2, 1),
        "w_in_t": d["L0_attn_in_proj_w"].T,
        "conv_wt": conv_w.reshape(EC, 128, 4).transpose(1, 0, 2).reshape(128, EC * 4),
        "conv_bt": d["L0_attn_conv_b"].reshape(EC, 128).T,
        "w_xp_t": d["L0_attn_x_proj_w"].T,
        "w_dt_t": d["L0_attn_dt_proj_w"].T,
        "dt_bt": d["L0_attn_dt_proj_b"].reshape(EC, 128).T,
        "alog_t": d["L0_attn_A_log"].reshape(EC, 128, N).transpose(1, 0, 2).reshape(128, EC * N),
        "d_t": d["L0_attn_D"].reshape(EC, 128).T,
        "w_out_t": d["L0_attn_out_proj_w"].T,
        "w1": d["L0_ffn_W1"],
        "b1_t": d["L0_ffn_b1"].reshape(HC, 128).T,
        "w2": d["L0_ffn_W2"],
        "b2_t": d["L0_ffn_b2"].reshape(KF, 1),
        "vt": d["L0_ffn_V"].transpose(2, 0, 1).reshape(H, KR),
        "ut": d["L0_ffn_U"].transpose(0, 2, 1).reshape(KR, H),
        "ident_f": np.eye(128, dtype=np.float32),
    }
    return {k: np.ascontiguousarray(np.asarray(v, dtype=np.float32)) for k, v in m.items()}


def kernel(**inputs):
    nc = build_nc(debug=False)
    in_maps = [_prep_core_inputs(inputs, c) for c in range(NCORES)]
    res = run_bass_kernel_spmd(nc, in_maps, core_ids=list(range(NCORES)))
    outs = [r["out_fm"] for r in res.results]
    full = np.concatenate([o.transpose(0, 2, 1) for o in outs], axis=0)
    return np.ascontiguousarray(full.astype(np.float32))


# version 10
# speedup vs baseline: 1.4454x; 1.0017x over previous
import numpy as np
import concourse.bass as bass
import concourse.mybir as mybir
from concourse.bass_utils import run_bass_kernel_spmd

F32 = mybir.dt.float32
BF16 = mybir.dt.bfloat16
AF = mybir.ActivationFunctionType
ALU = mybir.AluOpType

NCORES = 8
BC = 2
L = 2048
H = 2048
HC = 16
E = 4096
EC = 32
EO = 64
N = 16
R = 128
LB = 4
LT = 512
KF = 8
RF = 64
KR = 512
KRC = 4

ENGINES = ("sync", "tensor", "vector", "scalar", "gpsimd")


class Prog:
    def __init__(self):
        self.ops = {e: [] for e in ENGINES}
        self.cnt = {}

    def on(self, eng, fn):
        self.ops[eng].append(fn)

    def bump(self, name, k=1):
        v = self.cnt.get(name, 0) + k
        self.cnt[name] = v
        return v

    def val(self, name):
        return self.cnt.get(name, 0)

    def wait(self, eng, sem_h, v):
        if v > 0:
            self.on(eng, lambda e, s=sem_h, vv=v: e.wait_ge(s, vv))

    def dma(self, eng, out, in_, sem_h, name):
        v = self.bump(name, 16)
        self.on(eng, lambda e, o=out, i=in_, s=sem_h: e.dma_start(out=o, in_=i).then_inc(s, 16))
        return v

    def step(self, eng, fn, sem_h, name):
        v = self.bump(name, 1)
        self.on(eng, lambda e, f=fn, s=sem_h: f(e).then_inc(s, 1))
        return v

    def run(self, eng, fn):
        self.on(eng, fn)


def build_nc(debug=False):
    nc = bass.Bass("TRN2", target_bir_lowering=False)

    x_fm = nc.dram_tensor("x_fm", [BC, H, L], F32, kind="ExternalInput").ap()
    w_in_t = nc.dram_tensor("w_in_t", [H, 2 * E], F32, kind="ExternalInput").ap()
    conv_wt = nc.dram_tensor("conv_wt", [128, EC * 4], F32, kind="ExternalInput").ap()
    conv_bt = nc.dram_tensor("conv_bt", [128, EC], F32, kind="ExternalInput").ap()
    w_xp_t = nc.dram_tensor("w_xp_t", [E, 160], F32, kind="ExternalInput").ap()
    w_dt_t = nc.dram_tensor("w_dt_t", [R, E], F32, kind="ExternalInput").ap()
    dt_bt = nc.dram_tensor("dt_bt", [128, EC], F32, kind="ExternalInput").ap()
    alog_t = nc.dram_tensor("alog_t", [128, EC * N], F32, kind="ExternalInput").ap()
    d_t = nc.dram_tensor("d_t", [128, EC], F32, kind="ExternalInput").ap()
    w_out_t = nc.dram_tensor("w_out_t", [E, H], F32, kind="ExternalInput").ap()
    w1 = nc.dram_tensor("w1", [H, H], F32, kind="ExternalInput").ap()
    b1_t = nc.dram_tensor("b1_t", [128, HC], F32, kind="ExternalInput").ap()
    w2 = nc.dram_tensor("w2", [H, KF], F32, kind="ExternalInput").ap()
    b2_t = nc.dram_tensor("b2_t", [KF, 1], F32, kind="ExternalInput").ap()
    vt = nc.dram_tensor("vt", [H, KR], F32, kind="ExternalInput").ap()
    ut = nc.dram_tensor("ut", [KR, H], F32, kind="ExternalInput").ap()
    ident_f = nc.dram_tensor("ident_f", [128, 128], F32, kind="ExternalInput").ap()

    out_fm = nc.dram_tensor("out_fm", [BC, H, L], F32, kind="ExternalOutput").ap()
    kind_s = "ExternalOutput" if debug else "Internal"
    xcv_f = nc.dram_tensor("xcv_f", [BC, EC, 128, L], F32, kind=kind_s).ap()
    xcv_h = nc.dram_tensor("xcv_h", [BC, EC, 128, L], BF16, kind=kind_s).ap()
    sg_f = nc.dram_tensor("sg_f", [BC, EC, 128, L], F32, kind=kind_s).ap()
    lnt_f = nc.dram_tensor("lnt_f", [BC, EC, 128, L], F32, kind=kind_s).ap()
    bn_rh = nc.dram_tensor("bn_rh", [BC, N, L], BF16, kind=kind_s).ap()
    c_rh = nc.dram_tensor("c_rh", [BC, N, L], BF16, kind=kind_s).ap()
    ya_f = nc.dram_tensor("ya_f", [EC, 128, L], F32, kind=kind_s).ap()
    y_hb = nc.dram_tensor("y_hb", [BC, EC, 128, L], BF16, kind=kind_s).ap()
    mo_f = nc.dram_tensor("mo_f", [BC, HC, 128, L], F32, kind=kind_s).ap()
    x2_f = nc.dram_tensor("x2_f", [BC, HC, 128, L], F32, kind=kind_s).ap()
    x2ln_h = nc.dram_tensor("x2ln_h", [BC, HC, 128, L], BF16, kind=kind_s).ap()
    h1_h = nc.dram_tensor("h1_h", [BC, HC, 128, L], BF16, kind=kind_s).ap()
    al_f = nc.dram_tensor("al_f", [BC, KF, L], F32, kind=kind_s).ap()
    if debug:
        xln_d = nc.dram_tensor("xln_d", [128, HC * BC * L], BF16, kind="ExternalOutput").ap()
        u_d = nc.dram_tensor("u_d", [BC, EC, 128, L], F32, kind="ExternalOutput").ap()
        scan_d = nc.dram_tensor("scan_d", [6, 128, N * LT], F32, kind="ExternalOutput").ap()
        dbg2 = nc.dram_tensor("dbg2", [3, 128, 512], F32, kind="ExternalOutput").ap()

    prog = Prog()
    SB_F32 = 52000

    from contextlib import ExitStack
    with ExitStack() as _stack:
        arena_t = _stack.enter_context(nc.sbuf_tensor("arena", [128, SB_F32], F32))
        pers_t = _stack.enter_context(nc.sbuf_tensor("persist", [128, 1200], F32))
        psum = [_stack.enter_context(nc.psum_tensor(f"pb{i}", [128, 512], F32)) for i in range(8)]
        pb0, pb1, pb2, pb3, pb4, pb5, pb6, pb7 = psum
        s_c = _stack.enter_context(nc.semaphore("s_c"))
        s_ld2 = _stack.enter_context(nc.semaphore("s_ld2"))
        s_w = [_stack.enter_context(nc.semaphore(f"s_w{i}")) for i in range(2)]
        s_b = [_stack.enter_context(nc.semaphore(f"s_b{i}")) for i in range(2)]
        st_sems = {k: _stack.enter_context(nc.semaphore(f"st_{k}"))
                   for k in ("sg", "xcv", "lnt", "bc", "ya", "y", "mo", "x2", "h1", "out", "dbg")}
        s_ld = _stack.enter_context(nc.semaphore("s_ld"))
        s_st = _stack.enter_context(nc.semaphore("s_st"))
        s_t = _stack.enter_context(nc.semaphore("s_t"))
        s_v = _stack.enter_context(nc.semaphore("s_v"))
        s_s = _stack.enter_context(nc.semaphore("s_s"))
        block = _stack.enter_context(nc.Block())

        def store_barrier(eng):
            for k, sem in st_sems.items():
                prog.wait(eng, sem, prog.val("st_" + k))
        psum = [pb0, pb1, pb2, pb3, pb4, pb5, pb6, pb7]
        arena = arena_t.ap()
        pers = pers_t.ap()

        po = [0]

        def pcarve(n):
            a = po[0]
            po[0] += n
            assert po[0] <= 1200
            return pers[:, a:a + n]

        eA = pcarve(EC * N)
        cw_sb = pcarve(EC * 4)
        cb_sb = pcarve(EC)
        dtb_sb = pcarve(EC)
        dtb_neg = pcarve(EC)
        dD_sb = pcarve(EC)
        b1_sb = pcarve(HC)
        b2_col = pcarve(1)[0:KF, :]
        ones_col = pcarve(1)
        one_row1 = pcarve(128)[0:1, :]
        one_col8 = pcarve(1)[0:KF, :]
        eps_col = pcarve(1)

        ao = [0]

        def carve(n):
            a = ao[0]
            ao[0] += n
            assert ao[0] <= SB_F32, f"arena overflow {ao[0]}"
            return arena[:, a:a + n]

        def carve_h(n_bf16):
            assert n_bf16 % 2 == 0
            return carve(n_bf16 // 2).bitcast(BF16)

        def reset(keep=0):
            ao[0] = keep

        prog.dma("sync", eA, alog_t, s_c, "c")
        prog.dma("sync", cw_sb, conv_wt, s_c, "c")
        prog.dma("sync", cb_sb, conv_bt, s_c, "c")
        prog.dma("sync", dtb_sb, dt_bt, s_c, "c")
        prog.dma("sync", dD_sb, d_t, s_c, "c")
        prog.dma("sync", b1_sb, b1_t, s_c, "c")
        cldv = prog.dma("sync", b2_col, b2_t, s_c, "c")
        prog.wait("scalar", s_c, cldv)
        prog.step("scalar", lambda e: e.activation(out=eA, in_=eA, func=AF.Exp), s_s, "s")
        prog.step("scalar", lambda e: e.activation(out=dtb_neg, in_=dtb_sb, func=AF.Copy, scale=-1.0), s_s, "s")
        prog.run("vector", lambda e: e.memset(ones_col, 1.0))
        prog.run("vector", lambda e: e.memset(one_row1, 1.0))
        prog.run("vector", lambda e: e.memset(one_col8, 1.0))
        prog.run("vector", lambda e: e.memset(eps_col, 1e-5))
        consts_v = prog.step("vector", lambda e: e.engine_nop(), s_v, "v")

        def ln_block(xt3, sq3, row_m, row_q, row_r, pm, pr, pst_a, pst_b, prev_v, nch):
            prog.wait("scalar", s_v, prev_v)
            sv = prog.step("scalar", lambda e: e.activation(out=sq3, in_=xt3, func=AF.Square), s_s, "s")
            prog.wait("tensor", s_s, sv)
            for c in range(nch):
                fn = lambda e, c=c: e.matmul(pst_a[0:1, :], ones_col, xt3[:, c, :],
                                             start=(c == 0), stop=(c == nch - 1))
                prog.on("tensor", fn)
            for c in range(nch):
                fn = lambda e, c=c: e.matmul(pst_b[0:1, :], ones_col, sq3[:, c, :],
                                             start=(c == 0), stop=(c == nch - 1))
                if c == nch - 1:
                    prog.step("tensor", fn, s_t, "t")
                else:
                    prog.on("tensor", fn)
            tv = prog.val("t")
            prog.wait("scalar", s_t, tv)
            prog.step("scalar", lambda e: e.activation(out=row_m, in_=pst_a[0:1, :],
                                                       func=AF.Copy, scale=1.0 / (nch * 128)), s_s, "s")
            prog.step("scalar", lambda e: e.activation(out=row_q, in_=pst_b[0:1, :],
                                                       func=AF.Copy, scale=1.0 / (nch * 128)), s_s, "s")
            sv2 = prog.step("scalar", lambda e: e.activation(out=row_r, in_=row_m, func=AF.Square), s_s, "s")
            prog.wait("vector", s_s, sv2)
            vv = prog.step("vector", lambda e: e.tensor_sub(row_q, row_q, row_r), s_v, "v")
            prog.wait("scalar", s_v, vv)
            sv3 = prog.step("scalar", lambda e: e.activation(out=row_r, in_=row_q, func=AF.Sqrt, bias=eps_col[0:1, :]), s_s, "s")
            prog.wait("vector", s_s, sv3)
            vv2 = prog.step("vector", lambda e: e.reciprocal(row_r, row_r), s_v, "v")
            prog.wait("tensor", s_v, vv2)
            prog.wait("tensor", s_v, prev_v)
            prog.on("tensor", lambda e: e.matmul(pm[:], one_row1, row_m, start=True, stop=True))
            prog.step("tensor", lambda e: e.matmul(pr[:], one_row1, row_r, start=True, stop=True), s_t, "t")
            return prog.val("t")

        reset()
        xln_h = carve_h(HC * (BC * L))
        xt_a = carve(HC * LT)
        sq_a = carve(HC * LT)
        row_m = carve(512)[0:1, :]
        row_q = carve(512)[0:1, :]
        row_r = carve(512)[0:1, :]
        A0_KEEP = (HC * (BC * L)) // 2

        xt3 = xt_a.rearrange("p (hc l) -> p hc l", hc=HC)
        sq3 = sq_a.rearrange("p (hc l) -> p hc l", hc=HC)
        xln3 = xln_h.rearrange("p (hc t) -> p hc t", hc=HC)
        prev_v = consts_v
        for uu in range(BC * LB):
            b, lb = divmod(uu, LB)
            prog.wait("sync", s_v, prev_v)
            lv = prog.dma("sync", xt3,
                          x_fm[b, :, lb * LT:(lb + 1) * LT].rearrange("(hc p) l -> p hc l", p=128),
                          s_c, "c")
            prog.wait("scalar", s_c, lv)
            prog.wait("tensor", s_c, lv)
            ln_block(xt3, sq3, row_m, row_q, row_r, psum[0], psum[1], psum[2], psum[3],
                     prev_v, HC)
            tv = prog.val("t")
            prog.wait("vector", s_t, tv)
            prog.on("vector", lambda e: e.tensor_tensor(
                out=sq3, in0=xt3,
                in1=psum[0][:].unsqueeze(1).broadcast_to([128, HC, 512]),
                op=ALU.subtract))
            dst = xln3[:, :, uu * LT:(uu + 1) * LT]
            prev_v = prog.step("vector", lambda e, d=dst: e.tensor_tensor(
                out=d, in0=sq3,
                in1=psum[1][:].unsqueeze(1).broadcast_to([128, HC, 512]),
                op=ALU.mult), s_v, "v")
        ln1_done_v = prev_v
        if debug:
            prog.wait("gpsimd", s_v, ln1_done_v)
            prog.dma("gpsimd", xln_d, xln_h, st_sems["dbg"], "st_dbg")

        reset(A0_KEEP)
        wbuf = [carve(16 * 128), carve(16 * 128)]
        wbuf_h = [carve_h(16 * 128), carve_h(16 * 128)]
        sg_t = [carve(LT), carve(LT)]
        A1_GATE_END = ao[0]
        u_row = [carve(3 + L), carve(3 + L)]
        c0 = carve(L)
        c1 = carve(L)
        xcv_t = [carve(L)]
        xcvh_t = [carve_h(L)]

        prog.run("vector", lambda e: e.memset(u_row[0][:, 0:3], 0.0))
        prog.run("vector", lambda e: e.memset(u_row[1][:, 0:3], 0.0))

        wb3_src = w_in_t
        def phase_a1(eo):
            pp = eo % 2
            t_prev2 = prog.cnt.get(f"a1_t_{eo - 2}", 0)
            prog.wait("sync", s_t, t_prev2)
            if eo < 2:
                prog.wait("sync", s_v, ln1_done_v)
            lv = prog.dma("sync", wbuf[pp].rearrange("p (hc e) -> p hc e", e=128),
                          wb3_src[:, eo * 128:(eo + 1) * 128].rearrange("(hc p) e -> p hc e", p=128),
                          s_w[pp], f"w{pp}")
            prog.wait("scalar", s_w[pp], lv)
            prog.wait("scalar", s_t, t_prev2)
            cv = prog.step("scalar", lambda e, pp=pp: e.activation(
                out=wbuf_h[pp], in_=wbuf[pp], func=AF.Copy), s_s, "s")

            prog.wait("tensor", s_s, cv)
            if eo == 0:
                prog.wait("tensor", s_v, ln1_done_v)
            is_u = eo < EC
            ec = eo if is_u else eo - EC
            for b in range(BC):
                for lb in range(LB):
                    if is_u:
                        pt = psum[b * LB + lb]
                        w_ev = prog.cnt.get(f"a1_ev_{eo - 1}_{b}_{lb}", None)
                    else:
                        bank = 4 + (b * LB + lb) % 4
                        pt = psum[bank]
                        w_ev = prog.cnt.get(f"a1_gbank_{bank}", None)
                    if w_ev is not None:
                        prog.wait("tensor", s_v if w_ev[0] == "v" else s_s, w_ev[1])
                    for hc in range(HC):
                        fn = lambda e, pt=pt, pp=pp, hc=hc, b=b, lb=lb: e.matmul(
                            pt[:], wbuf_h[pp][:, hc * 128:(hc + 1) * 128],
                            xln3[:, hc, (b * LB + lb) * LT:(b * LB + lb + 1) * LT],
                            start=(hc == 0), stop=(hc == HC - 1))
                        if hc == HC - 1:
                            tv = prog.step("tensor", fn, s_t, "t")
                            prog.cnt[f"a1_t_{eo}"] = tv
                        else:
                            prog.on("tensor", fn)
                    if is_u:
                        prog.wait("vector", s_t, tv)
                        ev = prog.step("vector", lambda e, b=b, lb=lb, pt=pt: e.tensor_copy(
                            u_row[b][:, 3 + lb * LT: 3 + (lb + 1) * LT], pt[:]), s_v, "v")
                        prog.cnt[f"a1_ev_{eo}_{b}_{lb}"] = ("v", ev)
                    else:
                        gp = (eo * BC * LB + b * LB + lb) % 2
                        prog.wait("scalar", st_sems["sg"], prog.cnt.get(f"a1_sgst_{gp}", 0))
                        prog.wait("scalar", s_t, tv)
                        ev = prog.step("scalar", lambda e, gp=gp, pt=pt: e.activation(
                            out=sg_t[gp], in_=pt[:], func=AF.Silu), s_s, "s")
                        prog.cnt[f"a1_gbank_{bank}"] = ("s", ev)
                        prog.wait("gpsimd", s_s, ev)
                        stv = prog.dma("gpsimd", sg_f[b, ec, :, lb * LT:(lb + 1) * LT],
                                       sg_t[gp], st_sems["sg"], "st_sg")
                        prog.cnt[f"a1_sgst_{gp}"] = stv
                        prog.cnt[f"a1_sgst_ec_{ec}"] = stv
            if is_u:
                for b in range(BC):
                    if debug:
                        lastev = prog.cnt[f"a1_ev_{eo}_{b}_{LB - 1}"][1]
                        prog.wait("gpsimd", s_v, lastev)
                        prog.dma("gpsimd", u_d[b, ec], u_row[b][:, 3:3 + L], st_sems["dbg"], "st_dbg")
                    prog.wait("vector", s_s, prog.cnt.get("a1_silu", 0))
                    prog.on("vector", lambda e, b=b, ec=ec: e.tensor_scalar_mul(
                        c0, u_row[b][:, 0:L], cw_sb[:, ec * 4 + 0:ec * 4 + 1]))
                    prog.on("vector", lambda e, b=b, ec=ec: e.scalar_tensor_tensor(
                        out=c1, in0=u_row[b][:, 1:1 + L], scalar=cw_sb[:, ec * 4 + 1:ec * 4 + 2],
                        in1=c0, op0=ALU.mult, op1=ALU.add))
                    prog.on("vector", lambda e, b=b, ec=ec: e.scalar_tensor_tensor(
                        out=c0, in0=u_row[b][:, 2:2 + L], scalar=cw_sb[:, ec * 4 + 2:ec * 4 + 3],
                        in1=c1, op0=ALU.mult, op1=ALU.add))
                    vv = prog.step("vector", lambda e, b=b, ec=ec: e.scalar_tensor_tensor(
                        out=c1, in0=u_row[b][:, 3:3 + L], scalar=cw_sb[:, ec * 4 + 3:ec * 4 + 4],
                        in1=c0, op0=ALU.mult, op1=ALU.add), s_v, "v")
                    prog.wait("scalar", s_v, vv)
                    prog.wait("scalar", st_sems["xcv"], prog.cnt.get("a1_xcvst", 0))
                    sv = prog.step("scalar", lambda e, ec=ec: e.activation(
                        out=xcv_t[0], in_=c1, func=AF.Silu, bias=cb_sb[:, ec:ec + 1]), s_s, "s")
                    prog.cnt["a1_silu"] = sv
                    sv2 = prog.step("scalar", lambda e: e.activation(
                        out=xcvh_t[0], in_=xcv_t[0], func=AF.Copy), s_s, "s")
                    prog.wait("gpsimd", s_s, sv2)
                    prog.dma("gpsimd", xcv_f[b, ec], xcv_t[0], st_sems["xcv"], "st_xcv")
                    stv = prog.dma("gpsimd", xcv_h[b, ec], xcvh_t[0], st_sems["xcv"], "st_xcv")
                    prog.cnt["a1_xcvst"] = stv

        for eo in range(EC):
            phase_a1(eo)
        a1u_xcv = prog.val("st_xcv")
        a1u_v = prog.val("v")
        a1u_s = prog.val("s")
        for bank in range(4, 8):
            prog.cnt[f"a1_gbank_{bank}"] = ("v", a1u_v)

        reset(A1_GATE_END)
        wstage = carve(2560)
        xp_wh = carve_h(EC * 160)
        dtlow_h = carve_h(BC * L)
        xcvs = [carve_h(LT), carve_h(LT)]
        bc_ev = carve_h(LT)
        dt_wh = carve_h(EC * 128)
        lnt_t = [carve(LT), carve(LT)]
        sig_t = carve(LT)

        prog.wait("sync", s_v, a1u_v)
        prog.wait("sync", s_s, a1u_s)
        prog.wait("sync", st_sems["xcv"], a1u_xcv)
        for hh in range(2):
            lv = prog.dma("sync", wstage.rearrange("p (ec r) -> p ec r", r=160),
                          w_xp_t[hh * 2048:(hh + 1) * 2048, :]
                          .rearrange("(ec p) r -> p ec r", p=128), s_c, "c")
            prog.wait("scalar", s_c, lv)
            cv = prog.step("scalar", lambda e, hh=hh: e.activation(
                out=xp_wh[:, hh * 2560:(hh + 1) * 2560], in_=wstage,
                func=AF.Copy), s_s, "s")
            prog.wait("sync", s_s, cv)
        xpcv = cv
        for hh in range(2):
            lv = prog.dma("sync", wstage[:, 0:2048], w_dt_t[:, hh * 2048:(hh + 1) * 2048],
                          s_c, "c")
            prog.wait("scalar", s_c, lv)
            dtcv = prog.step("scalar", lambda e, hh=hh: e.activation(
                out=dt_wh[:, hh * 2048:(hh + 1) * 2048], in_=wstage[:, 0:2048],
                func=AF.Copy), s_s, "s")
            if hh == 0:
                prog.wait("sync", s_s, dtcv)

        def phase_a2_unit(uu):
            b, lb = divmod(uu, LB)
            for ec in range(EC):
                pp = ec % 2
                idx = uu * EC + ec
                prog.wait("sync", s_t, prog.cnt.get(f"a2_t_{idx - 2}", 0))
                lv = prog.dma("sync", xcvs[pp], xcv_h[b, ec, :, lb * LT:(lb + 1) * LT],
                              s_b[pp], f"b{pp}")
                prog.wait("tensor", s_b[pp], lv)
                if idx == 0:
                    prog.wait("tensor", s_s, xpcv)
                    prog.wait("tensor", s_v, a1u_v)
                if ec == 0:
                    prog.wait("tensor", s_v, prog.cnt.get(f"a2_ev_{uu - 1}", 0))
                    prog.wait("tensor", s_s, prog.cnt.get(f"a2_es_{uu - 1}", 0))
                prog.on("tensor", lambda e, pp=pp, ec=ec: e.matmul(
                    psum[0][:], xp_wh[:, ec * 160:ec * 160 + 128], xcvs[pp],
                    start=(ec == 0), stop=(ec == EC - 1), skip_group_check=True))
                tv = prog.step("tensor", lambda e, pp=pp, ec=ec: e.matmul(
                    psum[1][0:32, :], xp_wh[:, ec * 160 + 128:(ec + 1) * 160], xcvs[pp],
                    start=(ec == 0), stop=(ec == EC - 1), skip_group_check=True), s_t, "t")
                prog.cnt[f"a2_t_{idx}"] = tv
            prog.wait("vector", s_t, tv)
            ev = prog.step("vector", lambda e, uu=uu: e.tensor_copy(
                dtlow_h[:, uu * LT:(uu + 1) * LT], psum[0][:]), s_v, "v")
            prog.cnt[f"a2_ev_{uu}"] = ev
            prog.wait("scalar", s_t, tv)
            prog.wait("scalar", st_sems["bc"], prog.val("st_bc"))
            prog.step("scalar", lambda e: e.activation(
                out=bc_ev[0:32, :], in_=psum[1][0:32, :], func=AF.Copy), s_s, "s")
            es = prog.step("scalar", lambda e: e.activation(
                out=bc_ev[0:16, :], in_=bc_ev[0:16, :], func=AF.Copy, scale=-1.0), s_s, "s")
            prog.cnt[f"a2_es_{uu}"] = es
            prog.wait("gpsimd", s_s, es)
            prog.dma("gpsimd", bn_rh[b, :, lb * LT:(lb + 1) * LT], bc_ev[0:16, :], st_sems["bc"], "st_bc")
            stv = prog.dma("gpsimd", c_rh[b, :, lb * LT:(lb + 1) * LT], bc_ev[16:32, :], st_sems["bc"], "st_bc")

        def phase_a3_unit(uu):
            b, lb = divmod(uu, LB)
            prog.wait("tensor", s_v, prog.cnt[f"a2_ev_{uu}"])
            if uu == 0:
                prog.wait("tensor", s_s, dtcv)
            for ec in range(EC):
                idx = uu * EC + ec
                pp = idx % 2
                pt = psum[2 + pp]
                prog.wait("tensor", s_s, prog.cnt.get(f"a3_ev_{idx - 2}", 0))
                tv = prog.step("tensor", lambda e, pt=pt, ec=ec, uu=uu: e.matmul(
                    pt[:], dt_wh[:, ec * 128:(ec + 1) * 128],
                    dtlow_h[:, uu * LT:(uu + 1) * LT], start=True, stop=True), s_t, "t")
                prog.wait("scalar", s_t, tv)
                prog.wait("scalar", st_sems["lnt"], prog.cnt.get(f"a3_lntst_{pp}", 0))
                prog.step("scalar", lambda e, pt=pt, ec=ec: e.activation(
                    out=sig_t, in_=pt[:], func=AF.Sigmoid, scale=-1.0,
                    bias=dtb_neg[:, ec:ec + 1]), s_s, "s")
                sv = prog.step("scalar", lambda e, pp=pp: e.activation(
                    out=lnt_t[pp], in_=sig_t, func=AF.Ln), s_s, "s")
                prog.cnt[f"a3_ev_{idx}"] = sv
                prog.wait("gpsimd", s_s, sv)
                stv = prog.dma("gpsimd", lnt_f[b, ec, :, lb * LT:(lb + 1) * LT],
                               lnt_t[pp], st_sems["lnt"], "st_lnt")
                prog.cnt[f"a3_lntst_{pp}"] = stv

        INTERLEAVE_A = False
        for s_i in range(EC):
            phase_a1(EC + s_i)
            if INTERLEAVE_A and s_i % 4 == 3:
                uu = s_i // 4
                phase_a2_unit(uu)
                phase_a3_unit(uu)
        if not INTERLEAVE_A:
            for uu in range(BC * LB):
                phase_a2_unit(uu)
                phase_a3_unit(uu)

        reset(0)
        bbc_h = carve_h(8 * L)
        cbc_h = carve_h(8 * L)
        ident_h = carve_h(128)
        lnt_em = [carve(L), carve(L)]
        xcv_em = [carve(L), carve(L)]
        sg_em = [carve(L), carve(L)]
        ya_em = [carve(L), carve(L)]
        mdtu_h = carve_h(L)
        dA_b = [carve(L), carve(L)]
        w_hb = carve_h(L)
        s_hb = carve_h(L)
        q_hb = [carve_h(L), carve_h(L)]
        t1_f = carve(L)
        yo_hb = [carve_h(L), carve_h(L)]
        ya_st = [carve(L), carve(L)]
        idsrc = carve(128)

        store_barrier("sync")
        prog.wait("sync", s_t, prog.val("t"))
        prog.wait("sync", s_v, prog.val("v"))
        prog.wait("sync", s_s, prog.val("s"))
        ilv = prog.dma("sync", idsrc, ident_f, s_c, "c")
        prog.wait("vector", s_c, ilv)
        prog.step("vector", lambda e: e.tensor_copy(ident_h, idsrc), s_v, "v")

        bbc3 = bbc_h.rearrange("p (n l) -> p n l", n=8)
        cbc3 = cbc_h.rearrange("p (n l) -> p n l", n=8)

        def phase_b_pass(b, g):
            prog.wait("sync", s_v, prog.val("v"))
            prog.wait("sync", s_s, prog.val("s"))
            prog.wait("sync", st_sems["ya"], prog.val("st_ya"))
            prog.dma("sync", bbc3,
                     bn_rh[b, g * 8:(g + 1) * 8, :].unsqueeze(0).broadcast_to([128, 8, L]),
                     s_ld2, "ld2")
            bcv = prog.dma("sync", cbc3,
                           c_rh[b, g * 8:(g + 1) * 8, :].unsqueeze(0).broadcast_to([128, 8, L]),
                           s_ld2, "ld2")
            prog.wait("vector", s_ld2, bcv)
            prog.wait("vector", s_t, prog.val("t"))

            for ec in range(EC):
                pp = ec % 2
                prog.wait("sync", s_s, prog.cnt.get(f"b_exp_{pp}", 0))
                prog.wait("sync", s_v, prog.cnt.get(f"b_vec_{pp}", 0))
                prog.dma("sync", lnt_em[pp], lnt_f[b, ec], s_b[pp], f"b{pp}")
                lv = prog.dma("sync", xcv_em[pp], xcv_f[b, ec], s_b[pp], f"b{pp}")
                if g == 1:
                    prog.dma("sync", sg_em[pp], sg_f[b, ec], s_b[pp], f"b{pp}")
                    lv = prog.dma("sync", ya_em[pp], ya_f[ec], s_b[pp], f"b{pp}")
                prog.wait("vector", s_b[pp], lv)
                prog.on("vector", lambda e, pp=pp: e.tensor_tensor(
                    out=mdtu_h, in0=lnt_em[pp], in1=xcv_em[pp], op=ALU.mult))
                prog.wait("scalar", s_b[pp], lv)
                for n8 in range(8):
                    n = g * 8 + n8
                    d = n8 % 2
                    k = prog.bump("b_nsq")
                    prog.wait("scalar", s_v, prog.cnt.get(f"b_scanv_{k - 2}", 0))
                    sv = prog.step("scalar", lambda e, d=d, pp=pp, ec=ec, n=n: e.activation(
                        out=dA_b[d], in_=lnt_em[pp], func=AF.Exp,
                        scale=eA[:, ec * N + n:ec * N + n + 1]), s_s, "s")
                    prog.on("vector", lambda e, n8=n8: e.tensor_tensor(
                        out=w_hb, in0=mdtu_h, in1=bbc3[:, n8, :], op=ALU.mult))
                    prog.wait("vector", s_s, sv)
                    vv = prog.step("vector", lambda e, d=d: e.tensor_tensor_scan(
                        out=s_hb, data0=dA_b[d], data1=w_hb,
                        initial=0.0, op0=ALU.mult, op1=ALU.add), s_v, "v")
                    prog.cnt[f"b_scanv_{k}"] = vv
                    prog.wait("vector", s_t, prog.cnt.get(f"b_mmt_{k - 2}", 0))
                    qv = prog.step("vector", lambda e, d=d, n8=n8: e.tensor_tensor(
                        out=q_hb[d], in0=s_hb, in1=cbc3[:, n8, :], op=ALU.mult), s_v, "v")
                    prog.wait("tensor", s_v, qv)
                    if n8 == 0:
                        pc = prog.cnt.get("b_pycons", None)
                        if pc is not None:
                            prog.wait("tensor", {"s": s_s, "v": s_v}[pc[0]], pc[1])
                    for lt in range(4):
                        fn = lambda e, d=d, lt=lt, n8=n8: e.matmul(
                            psum[lt][:], ident_h, q_hb[d][:, lt * 512:(lt + 1) * 512],
                            start=(n8 == 0), stop=(n8 == 7))
                        if lt == 3:
                            tv = prog.step("tensor", fn, s_t, "t")
                            prog.cnt[f"b_mmt_{k}"] = tv
                        else:
                            prog.on("tensor", fn)
                prog.cnt[f"b_exp_{pp}"] = prog.val("s")
                mmv = prog.val("t")
                if g == 0:
                    gi = prog.bump("b_yagi")
                    prog.wait("scalar", s_t, mmv)
                    prog.wait("scalar", st_sems["ya"], prog.cnt.get(f"b_yasg_{gi - 2}", 0))
                    for lt in range(4):
                        fn = lambda e, lt=lt, pp=pp: e.activation(
                            out=ya_st[pp][:, lt * 512:(lt + 1) * 512], in_=psum[lt][:],
                            func=AF.Copy)
                        if lt == 3:
                            sv_e = prog.step("scalar", fn, s_s, "s")
                        else:
                            prog.on("scalar", fn)
                    prog.cnt["b_pycons"] = ("s", sv_e)
                    prog.wait("gpsimd", s_s, sv_e)
                    stv = prog.dma("gpsimd", ya_f[ec], ya_st[pp], st_sems["ya"], "st_ya")
                    prog.cnt[f"b_yasg_{gi}"] = stv
                else:
                    prog.wait("vector", s_t, mmv)
                    for lt in range(4):
                        prog.on("vector", lambda e, lt=lt, pp=pp, ec=ec: e.scalar_tensor_tensor(
                            out=t1_f[:, lt * 512:(lt + 1) * 512],
                            in0=xcv_em[pp][:, lt * 512:(lt + 1) * 512],
                            scalar=dD_sb[:, ec:ec + 1], in1=psum[lt][:],
                            op0=ALU.mult, op1=ALU.add))
                    yfv = prog.step("vector", lambda e, pp=pp: e.tensor_add(
                        t1_f, t1_f, ya_em[pp]), s_v, "v")
                    prog.cnt["b_pycons"] = ("v", yfv)
                    gi = prog.bump("b_ygi")
                    prog.wait("vector", st_sems["y"], prog.cnt.get(f"b_yst_{gi - 2}", 0))
                    vv = prog.step("vector", lambda e, pp=pp: e.tensor_tensor(
                        out=yo_hb[pp], in0=t1_f, in1=sg_em[pp], op=ALU.mult), s_v, "v")
                    prog.wait("gpsimd", s_v, vv)
                    stv = prog.dma("gpsimd", y_hb[b, ec], yo_hb[pp], st_sems["y"], "st_y")
                    prog.cnt[f"b_yst_{gi}"] = stv
                prog.cnt[f"b_vec_{pp}"] = prog.val("v")

        for b in range(BC):
            for g in range(2):
                phase_b_pass(b, g)

        reset(0)
        ow_stage = [carve(1024), carve(1024)]
        ow_h = carve_h(EC * 1024)
        yblk = [carve_h(EC * LT), carve_h(EC * LT)]
        mo_t = carve(8 * LT)

        store_barrier("sync")
        prog.wait("sync", s_st, prog.val("st"))
        prog.wait("sync", s_v, prog.val("v"))
        prog.wait("sync", s_s, prog.val("s"))

        for half in range(2):
            for ec in range(EC):
                pp = ec % 2
                prog.wait("sync", s_v, prog.cnt.get(f"c1w_{half}_{ec - 2}", 0))
                if half > 0 and ec < 2:
                    prog.wait("sync", s_t, prog.cnt.get("c1_t_last", 0))
                lv = prog.dma("sync", ow_stage[pp],
                              w_out_t[ec * 128:(ec + 1) * 128, half * 1024:(half + 1) * 1024],
                              s_ld, "ld")
                prog.wait("vector", s_ld, lv)
                cv = prog.step("vector", lambda e, pp=pp, ec=ec: e.tensor_copy(
                    ow_h[:, ec * 1024:(ec + 1) * 1024], ow_stage[pp]), s_v, "v")
                prog.cnt[f"c1w_{half}_{ec}"] = cv
            wcast = prog.val("v")
            def phase_c1_unit(uu, half=half, wcast=wcast):
                b, lb = divmod(uu, LB)
                pp = uu % 2
                idx = half * BC * LB + uu
                prog.wait("sync", s_t, prog.cnt.get(f"c1_t_{idx - 2}", 0))
                lv = prog.dma("sync", yblk[pp].rearrange("p (ec l) -> p ec l", ec=EC),
                              y_hb[b, :, :, lb * LT:(lb + 1) * LT].rearrange("ec p l -> p ec l"),
                              s_ld, "ld")
                prog.wait("tensor", s_ld, lv)
                if uu == 0:
                    prog.wait("tensor", s_v, wcast)
                for ho in range(8):
                    pt = psum[ho % 2]
                    prog.wait("tensor", s_v, prog.cnt.get(f"c1_ev_{idx}_{ho - 2}", 0))
                    if ho == 0:
                        prog.wait("vector", s_st, prog.cnt.get("c1_most", 0))
                    for ec in range(EC):
                        fn = lambda e, pt=pt, pp=pp, ec=ec, ho=ho: e.matmul(
                            pt[:], ow_h[:, ec * 1024 + ho * 128:ec * 1024 + (ho + 1) * 128],
                            yblk[pp][:, ec * LT:(ec + 1) * LT],
                            start=(ec == 0), stop=(ec == EC - 1))
                        if ec == EC - 1:
                            tv = prog.step("tensor", fn, s_t, "t")
                            prog.cnt[f"c1_t_{idx}"] = tv
                        else:
                            prog.on("tensor", fn)
                    prog.wait("vector", s_t, tv)
                    ev = prog.step("vector", lambda e, ho=ho, pt=pt: e.tensor_copy(
                        mo_t[:, ho * LT:(ho + 1) * LT], pt[:]), s_v, "v")
                    prog.cnt[f"c1_ev_{idx}_{ho}"] = ev
                prog.wait("gpsimd", s_v, prog.val("v"))
                stv = prog.dma("gpsimd",
                               mo_f[b, half * 8:(half + 1) * 8, :, lb * LT:(lb + 1) * LT]
                               .rearrange("ho p l -> p ho l"),
                               mo_t.rearrange("p (ho l) -> p ho l", ho=8), s_st, "st")
                prog.cnt["c1_most"] = stv
            for uu in range(BC * LB):
                phase_c1_unit(uu)
            prog.cnt["c1_t_last"] = prog.val("t")

        reset(0)
        mo_b = carve(HC * LT)
        x_b = [carve(HC * LT), carve(HC * LT)]
        sq_b = [carve(HC * LT), carve(HC * LT)]
        x2l_one = carve_h(HC * LT)
        x2l_b = [x2l_one, x2l_one]
        rows2 = [carve(1536), carve(1536)]

        mo3 = mo_b.rearrange("p (hc l) -> p hc l", hc=HC)
        xb3 = [x.rearrange("p (hc l) -> p hc l", hc=HC) for x in x_b]
        sqb3 = [x.rearrange("p (hc l) -> p hc l", hc=HC) for x in sq_b]
        x2l3 = [x.rearrange("p (hc l) -> p hc l", hc=HC) for x in x2l_b]

        prog.wait("sync", s_st, prog.val("st"))
        prog.wait("sync", s_v, prog.val("v"))
        prog.wait("sync", s_t, prog.val("t"))
        for uu in range(BC * LB):
            b, lb = divmod(uu, LB)
            d = uu % 2
            psA2, psB2 = (psum[0], psum[1]) if d == 0 else (psum[4], psum[5])
            psA3, psB3 = (psum[2], psum[3]) if d == 0 else (psum[6], psum[7])
            rm = rows2[d][0:1, 0:512]
            rq = rows2[d][0:1, 512:1024]
            rr = rows2[d][0:1, 1024:1536]
            prog.wait("sync", s_v, prog.cnt.get("c2_mocons", 0))
            prog.wait("sync", s_v, prog.cnt.get(f"c2_vfin_{d}", 0))
            prog.wait("sync", s_st, prog.cnt.get(f"c2_st_{d}", 0))
            prog.dma("sync", mo3,
                     mo_f[b, :, :, lb * LT:(lb + 1) * LT].rearrange("hc p l -> p hc l"),
                     s_ld, "ld")
            lv = prog.dma("sync", xb3[d],
                          x_fm[b, :, lb * LT:(lb + 1) * LT].rearrange("(hc p) l -> p hc l", p=128),
                          s_ld, "ld")
            prog.wait("scalar", s_ld, lv)
            prog.wait("tensor", s_ld, lv)
            ln_block(mo3, sqb3[d], rm, rq, rr, psA2, psB2, psA2, psB2,
                     prog.cnt.get(f"c2_vfin_{d}", 0), HC)
            tv = prog.val("t")
            prog.wait("vector", s_t, tv)
            prog.wait("vector", s_ld, lv)
            prog.on("vector", lambda e, psA2=psA2: e.tensor_tensor(
                out=mo3, in0=mo3,
                in1=psA2[:].unsqueeze(1).broadcast_to([128, HC, 512]), op=ALU.subtract))
            prog.on("vector", lambda e, psB2=psB2: e.tensor_tensor(
                out=mo3, in0=mo3,
                in1=psB2[:].unsqueeze(1).broadcast_to([128, HC, 512]), op=ALU.mult))
            vv1 = prog.step("vector", lambda e, d=d: e.tensor_add(x_b[d], x_b[d], mo_b),
                            s_v, "v")
            prog.cnt["c2_mocons"] = vv1
            prog.wait("scalar", s_v, vv1)
            prog.wait("tensor", s_v, vv1)
            ln_block(xb3[d], sqb3[d], rm, rq, rr, psA3, psB3, psA3, psB3, vv1, HC)
            tv = prog.val("t")
            prog.wait("vector", s_t, tv)
            prog.on("vector", lambda e, d=d, psA3=psA3: e.tensor_tensor(
                out=sqb3[d], in0=xb3[d],
                in1=psA3[:].unsqueeze(1).broadcast_to([128, HC, 512]), op=ALU.subtract))
            prog.wait("vector", s_st, prog.cnt.get("c2_st", 0))
            vv = prog.step("vector", lambda e, d=d, psB3=psB3: e.tensor_tensor(
                out=x2l3[d], in0=sqb3[d],
                in1=psB3[:].unsqueeze(1).broadcast_to([128, HC, 512]), op=ALU.mult), s_v, "v")
            prog.cnt[f"c2_vfin_{d}"] = vv
            prog.cnt[f"c2_v_{uu}"] = vv
            prog.wait("gpsimd", s_v, vv)
            prog.dma("gpsimd",
                     x2_f[b, :, :, lb * LT:(lb + 1) * LT].rearrange("hc p l -> p hc l"),
                     xb3[d], s_st, "st")
            stv = prog.dma("gpsimd",
                           x2ln_h[b, :, :, lb * LT:(lb + 1) * LT].rearrange("hc p l -> p hc l"),
                           x2l3[d], s_st, "st")
            prog.cnt[f"c2_st_{d}"] = stv
            prog.cnt["c2_st"] = stv

        reset(0)
        w1_stage = [carve(2048), carve(2048)]
        w1_h = carve_h(HC * 2048)
        w2_st = carve(HC * KF)
        w2_h = carve_h(HC * KF)
        x2l_s = [carve_h(HC * LT), carve_h(HC * LT)]
        h1_t = carve_h(HC * LT)
        ex_t = carve(LT)[0:KF, :]
        al_t = carve(LT)[0:KF, :]

        prog.wait("sync", s_st, prog.val("st"))
        prog.wait("sync", s_v, prog.val("v"))
        prog.wait("sync", s_t, prog.val("t"))
        prog.wait("sync", s_s, prog.val("s"))
        for dc in range(HC):
            pp = dc % 2
            prog.wait("sync", s_v, prog.cnt.get(f"c3w_{dc - 2}", 0))
            lv = prog.dma("sync", w1_stage[pp], w1[dc * 128:(dc + 1) * 128, :], s_ld, "ld")
            prog.wait("vector", s_ld, lv)
            cv = prog.step("vector", lambda e, pp=pp, dc=dc: e.tensor_copy(
                w1_h[:, dc * 2048:(dc + 1) * 2048], w1_stage[pp]), s_v, "v")
            prog.cnt[f"c3w_{dc}"] = cv
        lv = prog.dma("sync", w2_st.rearrange("p (dc k) -> p dc k", k=KF),
                      w2.rearrange("(dc p) k -> p dc k", p=128), s_ld, "ld")
        prog.wait("vector", s_ld, lv)
        wc = prog.step("vector", lambda e: e.tensor_copy(w2_h, w2_st), s_v, "v")

        def phase_c3a_unit(uu):
            b, lb = divmod(uu, LB)
            pp = uu % 2
            prog.wait("sync", s_t, prog.cnt.get(f"c3a_t_{uu - 2}", 0))
            lv = prog.dma("sync", x2l_s[pp].rearrange("p (hc l) -> p hc l", hc=HC),
                          x2ln_h[b, :, :, lb * LT:(lb + 1) * LT].rearrange("hc p l -> p hc l"),
                          s_ld, "ld")
            prog.wait("tensor", s_ld, lv)
            if uu == 0:
                prog.wait("tensor", s_v, wc)
            for eo in range(HC):
                pt = psum[eo % 2]
                prog.wait("tensor", s_s, prog.cnt.get(f"c3a_ev_{uu}_{eo - 2}", 0))
                for dc in range(HC):
                    fn = lambda e, pt=pt, pp=pp, dc=dc, eo=eo: e.matmul(
                        pt[:], w1_h[:, dc * 2048 + eo * 128:dc * 2048 + (eo + 1) * 128],
                        x2l_s[pp][:, dc * LT:(dc + 1) * LT],
                        start=(dc == 0), stop=(dc == HC - 1))
                    if dc == HC - 1:
                        tv = prog.step("tensor", fn, s_t, "t")
                    else:
                        prog.on("tensor", fn)
                prog.wait("scalar", s_t, tv)
                if eo == 0:
                    prog.wait("scalar", s_st, prog.cnt.get("c3a_h1st", 0))
                ev = prog.step("scalar", lambda e, eo=eo, pt=pt: e.activation(
                    out=h1_t[:, eo * LT:(eo + 1) * LT], in_=pt[:], func=AF.Relu,
                    bias=b1_sb[:, eo:eo + 1]), s_s, "s")
                prog.cnt[f"c3a_ev_{uu}_{eo}"] = ev
                prog.wait("tensor", s_s, ev)
                tv2 = prog.step("tensor", lambda e, eo=eo: e.matmul(
                    psum[2][0:KF, :], w2_h[:, eo * KF:(eo + 1) * KF],
                    h1_t[:, eo * LT:(eo + 1) * LT],
                    start=(eo == 0), stop=(eo == HC - 1), skip_group_check=True), s_t, "t")
                prog.cnt[f"c3a_t_{uu}"] = tv2
            tvl = prog.val("t")
            prog.wait("scalar", s_t, tvl)
            sv = prog.step("scalar", lambda e: e.activation(
                out=ex_t, in_=psum[2][0:KF, :], func=AF.Exp, bias=b2_col), s_s, "s")
            prog.wait("tensor", s_s, sv)
            tv3 = prog.step("tensor", lambda e: e.matmul(
                psum[3][0:1, :], one_col8, ex_t, start=True, stop=True), s_t, "t")
            prog.wait("vector", s_t, tv3)
            sv2 = prog.step("vector", lambda e: e.reciprocal(
                al_t[0:1, :], psum[3][0:1, :]), s_v, "v")
            prog.wait("tensor", s_v, sv2)
            tv4 = prog.step("tensor", lambda e: e.matmul(
                psum[4][0:KF, :], one_row1[0:1, 0:KF], al_t[0:1, :], start=True, stop=True), s_t, "t")
            prog.wait("vector", s_t, tv4)
            vv = prog.step("vector", lambda e: e.tensor_mul(al_t, ex_t, psum[4][0:KF, :]), s_v, "v")
            prog.wait("gpsimd", s_v, vv)
            prog.wait("gpsimd", s_s, prog.val("s"))
            stv_h = prog.dma("gpsimd",
                             h1_h[b, :, :, lb * LT:(lb + 1) * LT].rearrange("hc p l -> p hc l"),
                             h1_t.rearrange("p (hc l) -> p hc l", hc=HC), s_st, "st")
            prog.cnt["c3a_h1st"] = stv_h
            stv = prog.dma("gpsimd", al_f[b, :, lb * LT:(lb + 1) * LT], al_t, s_st, "st")
            prog.wait("scalar", s_st, stv)
        for uu in range(BC * LB):
            phase_c3a_unit(uu)

        reset(0)
        vt_st = [carve(KR), carve(KR)]
        vt_h = carve_h(HC * KR)
        ut_st = [carve(2048), carve(2048)]
        ut_h = carve_h(KRC * 2048)
        x2l_s2 = [carve_h(HC * LT), carve_h(HC * LT)]
        avx_h = carve_h(KRC * LT)
        alb = carve(KRC * LT)
        fo_t = carve(HC * LT)
        x2_s = carve(HC * LT)
        sq_c = carve(HC * LT)
        row_m3 = carve(512)[0:1, :]
        row_q3 = carve(512)[0:1, :]
        row_r3 = carve(512)[0:1, :]

        fo3 = fo_t.rearrange("p (hc l) -> p hc l", hc=HC)
        sqc3 = sq_c.rearrange("p (hc l) -> p hc l", hc=HC)
        x2s3 = x2_s.rearrange("p (hc l) -> p hc l", hc=HC)

        prog.wait("sync", s_v, prog.val("v"))
        prog.wait("sync", s_s, prog.val("s"))
        prog.wait("sync", s_t, prog.val("t"))
        for dc in range(HC):
            pp = dc % 2
            prog.wait("sync", s_v, prog.cnt.get(f"c3bw_{dc - 2}", 0))
            lv = prog.dma("sync", vt_st[pp], vt[dc * 128:(dc + 1) * 128, :], s_ld, "ld")
            prog.wait("vector", s_ld, lv)
            cv = prog.step("vector", lambda e, pp=pp, dc=dc: e.tensor_copy(
                vt_h[:, dc * KR:(dc + 1) * KR], vt_st[pp]), s_v, "v")
            prog.cnt[f"c3bw_{dc}"] = cv
        for kc in range(KRC):
            pp = kc % 2
            prog.wait("sync", s_v, prog.cnt.get(f"c3bu_{kc - 2}", 0))
            lv = prog.dma("sync", ut_st[pp], ut[kc * 128:(kc + 1) * 128, :], s_ld, "ld")
            prog.wait("vector", s_ld, lv)
            cv = prog.step("vector", lambda e, pp=pp, kc=kc: e.tensor_copy(
                ut_h[:, kc * 2048:(kc + 1) * 2048], ut_st[pp]), s_v, "v")
            prog.cnt[f"c3bu_{kc}"] = cv
        wc2 = prog.val("v")
        prog.wait("sync", s_st, prog.val("st"))

        prev_v_cell = [prog.val("v")]
        def phase_c3b_unit(uu):
            prev_v = prev_v_cell[0]
            b, lb = divmod(uu, LB)
            pp = uu % 2
            prog.wait("sync", s_t, prog.cnt.get(f"c3b_t_{uu - 2}", 0))
            prog.wait("sync", s_v, prog.cnt.get(f"c3b_v_{uu - 1}", 0))
            prog.wait("sync", s_st, prog.cnt.get("c3b_ost", 0))
            prog.dma("sync", x2l_s2[pp].rearrange("p (hc l) -> p hc l", hc=HC),
                     x2ln_h[b, :, :, lb * LT:(lb + 1) * LT].rearrange("hc p l -> p hc l"),
                     s_ld, "ld")
            prog.dma("sync", x2s3,
                     x2_f[b, :, :, lb * LT:(lb + 1) * LT].rearrange("hc p l -> p hc l"),
                     s_ld, "ld")
            for kc in range(KRC):
                for kk in range(2):
                    lva = prog.dma("sync",
                                   alb[kk * RF:(kk + 1) * RF, kc * LT:(kc + 1) * LT],
                                   al_f[b, 2 * kc + kk, lb * LT:(lb + 1) * LT]
                                   .unsqueeze(0).broadcast_to([RF, LT]),
                                   s_ld, "ld")
            prog.wait("tensor", s_ld, lva)
            if uu == 0:
                prog.wait("tensor", s_v, wc2)
            for kc in range(KRC):
                prog.wait("tensor", s_v, prog.cnt.get(f"c3b_av_{uu - 1}_{kc}", 0))
                if kc >= 2:
                    prog.wait("tensor", s_s, prog.cnt.get(f"c3b_lnsq_{uu - 1}", 0))
                for dc in range(HC):
                    fn = lambda e, kc=kc, dc=dc, pp=pp: e.matmul(
                        psum[kc][:], vt_h[:, dc * KR + kc * 128:dc * KR + (kc + 1) * 128],
                        x2l_s2[pp][:, dc * LT:(dc + 1) * LT],
                        start=(dc == 0), stop=(dc == HC - 1))
                    if dc == HC - 1:
                        tv = prog.step("tensor", fn, s_t, "t")
                    else:
                        prog.on("tensor", fn)
                prog.wait("vector", s_t, tv)
                av = prog.step("vector", lambda e, kc=kc: e.tensor_mul(
                    avx_h[:, kc * LT:(kc + 1) * LT], psum[kc][:],
                    alb[:, kc * LT:(kc + 1) * LT]), s_v, "v")
                prog.cnt[f"c3b_av_{uu}_{kc}"] = av
            avv = prog.val("v")
            prog.wait("tensor", s_v, avv)
            for dc in range(HC):
                pt = psum[4 + dc % 2]
                prog.wait("tensor", s_v, prog.cnt.get(f"c3b_fo_{uu}_{dc - 2}", 0))
                for kc in range(KRC):
                    fn = lambda e, pt=pt, dc=dc, kc=kc: e.matmul(
                        pt[:], ut_h[:, kc * 2048 + dc * 128:kc * 2048 + (dc + 1) * 128],
                        avx_h[:, kc * LT:(kc + 1) * LT],
                        start=(kc == 0), stop=(kc == KRC - 1))
                    if kc == KRC - 1:
                        tv = prog.step("tensor", fn, s_t, "t")
                        prog.cnt[f"c3b_t_{uu}"] = tv
                    else:
                        prog.on("tensor", fn)
                prog.wait("vector", s_t, tv)
                fv = prog.step("vector", lambda e, dc=dc, pt=pt: e.tensor_copy(
                    fo_t[:, dc * LT:(dc + 1) * LT], pt[:]), s_v, "v")
                prog.cnt[f"c3b_fo_{uu}_{dc}"] = fv
            fvl = prog.val("v")
            prog.wait("scalar", s_v, fvl)
            prog.wait("tensor", s_v, fvl)
            ln_block(fo3, sqc3, row_m3, row_q3, row_r3, psum[6], psum[7], psum[2], psum[3],
                     prev_v, HC)
            prog.cnt[f"c3b_lnsq_{uu}"] = prog.val("s")
            tv = prog.val("t")
            prog.wait("vector", s_t, tv)
            prog.wait("vector", s_ld, lva)
            prog.on("vector", lambda e: e.tensor_tensor(
                out=sqc3, in0=fo3,
                in1=psum[6][:].unsqueeze(1).broadcast_to([128, HC, 512]), op=ALU.subtract))
            prog.on("vector", lambda e: e.tensor_tensor(
                out=sqc3, in0=sqc3,
                in1=psum[7][:].unsqueeze(1).broadcast_to([128, HC, 512]), op=ALU.mult))
            vv = prog.step("vector", lambda e: e.tensor_add(x2_s, sq_c, x2_s), s_v, "v")
            prog.cnt[f"c3b_v_{uu}"] = vv
            prev_v_cell[0] = vv
            prog.wait("gpsimd", s_v, vv)
            stv = prog.dma("gpsimd",
                           out_fm[b, :, lb * LT:(lb + 1) * LT].rearrange("(hc p) l -> p hc l", p=128),
                           x2s3, s_st, "st")
            prog.cnt["c3b_ost"] = stv
        for uu in range(BC * LB):
            phase_c3b_unit(uu)

        prog.wait("sync", s_st, prog.val("st"))

        @block.sync
        def _(eng):
            for fn in prog.ops["sync"]:
                fn(eng)

        @block.tensor
        def _(eng):
            for fn in prog.ops["tensor"]:
                fn(eng)

        @block.vector
        def _(eng):
            for fn in prog.ops["vector"]:
                fn(eng)

        @block.scalar
        def _(eng):
            for fn in prog.ops["scalar"]:
                fn(eng)

        @block.gpsimd
        def _(eng):
            for fn in prog.ops["gpsimd"]:
                fn(eng)

    return nc


def _prep_core_inputs(inputs, core):
    d = inputs
    x = d["L0_attn_hidden_states"][2 * core:2 * core + 2]
    conv_w = d["L0_attn_conv_w"][:, 0, :]
    m = {
        "x_fm": x.transpose(0, 2, 1),
        "w_in_t": d["L0_attn_in_proj_w"].T,
        "conv_wt": conv_w.reshape(EC, 128, 4).transpose(1, 0, 2).reshape(128, EC * 4),
        "conv_bt": d["L0_attn_conv_b"].reshape(EC, 128).T,
        "w_xp_t": d["L0_attn_x_proj_w"].T,
        "w_dt_t": d["L0_attn_dt_proj_w"].T,
        "dt_bt": d["L0_attn_dt_proj_b"].reshape(EC, 128).T,
        "alog_t": d["L0_attn_A_log"].reshape(EC, 128, N).transpose(1, 0, 2).reshape(128, EC * N),
        "d_t": d["L0_attn_D"].reshape(EC, 128).T,
        "w_out_t": d["L0_attn_out_proj_w"].T,
        "w1": d["L0_ffn_W1"],
        "b1_t": d["L0_ffn_b1"].reshape(HC, 128).T,
        "w2": d["L0_ffn_W2"],
        "b2_t": d["L0_ffn_b2"].reshape(KF, 1),
        "vt": d["L0_ffn_V"].transpose(2, 0, 1).reshape(H, KR),
        "ut": d["L0_ffn_U"].transpose(0, 2, 1).reshape(KR, H),
        "ident_f": np.eye(128, dtype=np.float32),
    }
    return {k: np.ascontiguousarray(np.asarray(v, dtype=np.float32)) for k, v in m.items()}


def kernel(**inputs):
    nc = build_nc(debug=False)
    in_maps = [_prep_core_inputs(inputs, c) for c in range(NCORES)]
    res = run_bass_kernel_spmd(nc, in_maps, core_ids=list(range(NCORES)))
    outs = [r["out_fm"] for r in res.results]
    full = np.concatenate([o.transpose(0, 2, 1) for o in outs], axis=0)
    return np.ascontiguousarray(full.astype(np.float32))


# version 11 (final)
# speedup vs baseline: 1.5284x; 1.0574x over previous
import numpy as np
import concourse.bass as bass
import concourse.mybir as mybir
from concourse.bass_utils import run_bass_kernel_spmd

F32 = mybir.dt.float32
BF16 = mybir.dt.bfloat16
AF = mybir.ActivationFunctionType
ALU = mybir.AluOpType

NCORES = 8
BC = 2
L = 2048
H = 2048
HC = 16
E = 4096
EC = 32
EO = 64
N = 16
R = 128
LB = 4
LT = 512
KF = 8
RF = 64
KR = 512
KRC = 4

ENGINES = ("sync", "tensor", "vector", "scalar", "gpsimd")


class Prog:
    def __init__(self):
        self.ops = {e: [] for e in ENGINES}
        self.cnt = {}

    def on(self, eng, fn):
        self.ops[eng].append(fn)

    def bump(self, name, k=1):
        v = self.cnt.get(name, 0) + k
        self.cnt[name] = v
        return v

    def val(self, name):
        return self.cnt.get(name, 0)

    def wait(self, eng, sem_h, v):
        if v > 0:
            self.on(eng, lambda e, s=sem_h, vv=v: e.wait_ge(s, vv))

    def dma(self, eng, out, in_, sem_h, name):
        v = self.bump(name, 16)
        self.on(eng, lambda e, o=out, i=in_, s=sem_h: e.dma_start(out=o, in_=i).then_inc(s, 16))
        return v

    def step(self, eng, fn, sem_h, name):
        v = self.bump(name, 1)
        self.on(eng, lambda e, f=fn, s=sem_h: f(e).then_inc(s, 1))
        return v

    def run(self, eng, fn):
        self.on(eng, fn)


def build_nc(debug=False):
    nc = bass.Bass("TRN2", target_bir_lowering=False)

    x_fm = nc.dram_tensor("x_fm", [BC, H, L], F32, kind="ExternalInput").ap()
    w_in_t = nc.dram_tensor("w_in_t", [H, 2 * E], F32, kind="ExternalInput").ap()
    conv_wt = nc.dram_tensor("conv_wt", [128, EC * 4], F32, kind="ExternalInput").ap()
    conv_bt = nc.dram_tensor("conv_bt", [128, EC], F32, kind="ExternalInput").ap()
    w_xp_t = nc.dram_tensor("w_xp_t", [E, 160], F32, kind="ExternalInput").ap()
    w_dt_t = nc.dram_tensor("w_dt_t", [R, E], F32, kind="ExternalInput").ap()
    dt_bt = nc.dram_tensor("dt_bt", [128, EC], F32, kind="ExternalInput").ap()
    alog_t = nc.dram_tensor("alog_t", [128, EC * N], F32, kind="ExternalInput").ap()
    d_t = nc.dram_tensor("d_t", [128, EC], F32, kind="ExternalInput").ap()
    w_out_t = nc.dram_tensor("w_out_t", [E, H], F32, kind="ExternalInput").ap()
    w1 = nc.dram_tensor("w1", [H, H], F32, kind="ExternalInput").ap()
    b1_t = nc.dram_tensor("b1_t", [128, HC], F32, kind="ExternalInput").ap()
    w2 = nc.dram_tensor("w2", [H, KF], F32, kind="ExternalInput").ap()
    b2_t = nc.dram_tensor("b2_t", [KF, 1], F32, kind="ExternalInput").ap()
    vt = nc.dram_tensor("vt", [H, KR], F32, kind="ExternalInput").ap()
    ut = nc.dram_tensor("ut", [KR, H], F32, kind="ExternalInput").ap()
    ident_f = nc.dram_tensor("ident_f", [128, 128], F32, kind="ExternalInput").ap()

    out_fm = nc.dram_tensor("out_fm", [BC, H, L], F32, kind="ExternalOutput").ap()
    kind_s = "ExternalOutput" if debug else "Internal"
    xcv_f = nc.dram_tensor("xcv_f", [BC, EC, 128, L], F32, kind=kind_s).ap()
    xcv_h = nc.dram_tensor("xcv_h", [BC, EC, 128, L], BF16, kind=kind_s).ap()
    sg_f = nc.dram_tensor("sg_f", [BC, EC, 128, L], F32, kind=kind_s).ap()
    lnt_f = nc.dram_tensor("lnt_f", [BC, EC, 128, L], F32, kind=kind_s).ap()
    bn_rh = nc.dram_tensor("bn_rh", [BC, N, L], BF16, kind=kind_s).ap()
    c_rh = nc.dram_tensor("c_rh", [BC, N, L], BF16, kind=kind_s).ap()
    ya_f = nc.dram_tensor("ya_f", [EC, 128, L], F32, kind=kind_s).ap()
    y_hb = nc.dram_tensor("y_hb", [BC, EC, 128, L], BF16, kind=kind_s).ap()
    mo_f = nc.dram_tensor("mo_f", [BC, HC, 128, L], F32, kind=kind_s).ap()
    x2_f = nc.dram_tensor("x2_f", [BC, HC, 128, L], F32, kind=kind_s).ap()
    x2ln_h = nc.dram_tensor("x2ln_h", [BC, HC, 128, L], BF16, kind=kind_s).ap()
    h1_h = nc.dram_tensor("h1_h", [BC, HC, 128, L], BF16, kind=kind_s).ap()
    al_f = nc.dram_tensor("al_f", [BC, KF, L], F32, kind=kind_s).ap()
    if debug:
        xln_d = nc.dram_tensor("xln_d", [128, HC * BC * L], BF16, kind="ExternalOutput").ap()
        u_d = nc.dram_tensor("u_d", [BC, EC, 128, L], F32, kind="ExternalOutput").ap()
        scan_d = nc.dram_tensor("scan_d", [6, 128, N * LT], F32, kind="ExternalOutput").ap()
        dbg2 = nc.dram_tensor("dbg2", [3, 128, 512], F32, kind="ExternalOutput").ap()

    prog = Prog()
    SB_F32 = 52000

    from contextlib import ExitStack
    with ExitStack() as _stack:
        arena_t = _stack.enter_context(nc.sbuf_tensor("arena", [128, SB_F32], F32))
        pers_t = _stack.enter_context(nc.sbuf_tensor("persist", [128, 1200], F32))
        psum = [_stack.enter_context(nc.psum_tensor(f"pb{i}", [128, 512], F32)) for i in range(8)]
        pb0, pb1, pb2, pb3, pb4, pb5, pb6, pb7 = psum
        s_c = _stack.enter_context(nc.semaphore("s_c"))
        s_ld2 = _stack.enter_context(nc.semaphore("s_ld2"))
        s_w = [_stack.enter_context(nc.semaphore(f"s_w{i}")) for i in range(2)]
        s_b = [_stack.enter_context(nc.semaphore(f"s_b{i}")) for i in range(2)]
        st_sems = {k: _stack.enter_context(nc.semaphore(f"st_{k}"))
                   for k in ("sg", "xcv", "lnt", "bc", "ya", "y", "mo", "x2", "h1", "out", "dbg")}
        s_ld = _stack.enter_context(nc.semaphore("s_ld"))
        s_st = _stack.enter_context(nc.semaphore("s_st"))
        s_t = _stack.enter_context(nc.semaphore("s_t"))
        s_v = _stack.enter_context(nc.semaphore("s_v"))
        s_s = _stack.enter_context(nc.semaphore("s_s"))
        block = _stack.enter_context(nc.Block())

        def store_barrier(eng):
            for k, sem in st_sems.items():
                prog.wait(eng, sem, prog.val("st_" + k))
        psum = [pb0, pb1, pb2, pb3, pb4, pb5, pb6, pb7]
        arena = arena_t.ap()
        pers = pers_t.ap()

        po = [0]

        def pcarve(n):
            a = po[0]
            po[0] += n
            assert po[0] <= 1200
            return pers[:, a:a + n]

        eA = pcarve(EC * N)
        cw_sb = pcarve(EC * 4)
        cb_sb = pcarve(EC)
        dtb_sb = pcarve(EC)
        dtb_neg = pcarve(EC)
        dD_sb = pcarve(EC)
        b1_sb = pcarve(HC)
        b2_col = pcarve(1)[0:KF, :]
        ones_col = pcarve(1)
        one_row1 = pcarve(128)[0:1, :]
        one_col8 = pcarve(1)[0:KF, :]
        eps_col = pcarve(1)

        ao = [0]

        def carve(n):
            a = ao[0]
            ao[0] += n
            assert ao[0] <= SB_F32, f"arena overflow {ao[0]}"
            return arena[:, a:a + n]

        def carve_h(n_bf16):
            assert n_bf16 % 2 == 0
            return carve(n_bf16 // 2).bitcast(BF16)

        def reset(keep=0):
            ao[0] = keep

        prog.dma("sync", eA, alog_t, s_c, "c")
        prog.dma("sync", cw_sb, conv_wt, s_c, "c")
        prog.dma("sync", cb_sb, conv_bt, s_c, "c")
        prog.dma("sync", dtb_sb, dt_bt, s_c, "c")
        prog.dma("sync", dD_sb, d_t, s_c, "c")
        prog.dma("sync", b1_sb, b1_t, s_c, "c")
        cldv = prog.dma("sync", b2_col, b2_t, s_c, "c")
        prog.wait("scalar", s_c, cldv)
        prog.step("scalar", lambda e: e.activation(out=eA, in_=eA, func=AF.Exp), s_s, "s")
        prog.step("scalar", lambda e: e.activation(out=dtb_neg, in_=dtb_sb, func=AF.Copy, scale=-1.0), s_s, "s")
        prog.run("vector", lambda e: e.memset(ones_col, 1.0))
        prog.run("vector", lambda e: e.memset(one_row1, 1.0))
        prog.run("vector", lambda e: e.memset(one_col8, 1.0))
        prog.run("vector", lambda e: e.memset(eps_col, 1e-5))
        consts_v = prog.step("vector", lambda e: e.engine_nop(), s_v, "v")

        def ln_block(xt3, sq3, row_m, row_q, row_r, pm, pr, pst_a, pst_b, prev_v, nch):
            prog.wait("scalar", s_v, prev_v)
            sv = prog.step("scalar", lambda e: e.activation(out=sq3, in_=xt3, func=AF.Square), s_s, "s")
            prog.wait("tensor", s_s, sv)
            for c in range(nch):
                fn = lambda e, c=c: e.matmul(pst_a[0:1, :], ones_col, xt3[:, c, :],
                                             start=(c == 0), stop=(c == nch - 1))
                prog.on("tensor", fn)
            for c in range(nch):
                fn = lambda e, c=c: e.matmul(pst_b[0:1, :], ones_col, sq3[:, c, :],
                                             start=(c == 0), stop=(c == nch - 1))
                if c == nch - 1:
                    prog.step("tensor", fn, s_t, "t")
                else:
                    prog.on("tensor", fn)
            tv = prog.val("t")
            prog.wait("scalar", s_t, tv)
            prog.step("scalar", lambda e: e.activation(out=row_m, in_=pst_a[0:1, :],
                                                       func=AF.Copy, scale=1.0 / (nch * 128)), s_s, "s")
            prog.step("scalar", lambda e: e.activation(out=row_q, in_=pst_b[0:1, :],
                                                       func=AF.Copy, scale=1.0 / (nch * 128)), s_s, "s")
            sv2 = prog.step("scalar", lambda e: e.activation(out=row_r, in_=row_m, func=AF.Square), s_s, "s")
            prog.wait("vector", s_s, sv2)
            vv = prog.step("vector", lambda e: e.tensor_sub(row_q, row_q, row_r), s_v, "v")
            prog.wait("scalar", s_v, vv)
            sv3 = prog.step("scalar", lambda e: e.activation(out=row_r, in_=row_q, func=AF.Sqrt, bias=eps_col[0:1, :]), s_s, "s")
            prog.wait("vector", s_s, sv3)
            vv2 = prog.step("vector", lambda e: e.reciprocal(row_r, row_r), s_v, "v")
            prog.wait("tensor", s_v, vv2)
            prog.wait("tensor", s_v, prev_v)
            prog.on("tensor", lambda e: e.matmul(pm[:], one_row1, row_m, start=True, stop=True))
            prog.step("tensor", lambda e: e.matmul(pr[:], one_row1, row_r, start=True, stop=True), s_t, "t")
            return prog.val("t")

        reset()
        xln_h = carve_h(HC * (BC * L))
        xt_a = carve(HC * LT)
        sq_a = carve(HC * LT)
        row_m = carve(512)[0:1, :]
        row_q = carve(512)[0:1, :]
        row_r = carve(512)[0:1, :]
        A0_KEEP = (HC * (BC * L)) // 2

        xt3 = xt_a.rearrange("p (hc l) -> p hc l", hc=HC)
        sq3 = sq_a.rearrange("p (hc l) -> p hc l", hc=HC)
        xln3 = xln_h.rearrange("p (hc t) -> p hc t", hc=HC)
        prev_v = consts_v
        for uu in range(BC * LB):
            b, lb = divmod(uu, LB)
            prog.wait("sync", s_v, prev_v)
            lv = prog.dma("sync", xt3,
                          x_fm[b, :, lb * LT:(lb + 1) * LT].rearrange("(hc p) l -> p hc l", p=128),
                          s_c, "c")
            prog.wait("scalar", s_c, lv)
            prog.wait("tensor", s_c, lv)
            ln_block(xt3, sq3, row_m, row_q, row_r, psum[0], psum[1], psum[2], psum[3],
                     prev_v, HC)
            tv = prog.val("t")
            prog.wait("vector", s_t, tv)
            prog.on("vector", lambda e: e.tensor_tensor(
                out=sq3, in0=xt3,
                in1=psum[0][:].unsqueeze(1).broadcast_to([128, HC, 512]),
                op=ALU.subtract))
            dst = xln3[:, :, uu * LT:(uu + 1) * LT]
            prev_v = prog.step("vector", lambda e, d=dst: e.tensor_tensor(
                out=d, in0=sq3,
                in1=psum[1][:].unsqueeze(1).broadcast_to([128, HC, 512]),
                op=ALU.mult), s_v, "v")
        ln1_done_v = prev_v
        if debug:
            prog.wait("gpsimd", s_v, ln1_done_v)
            prog.dma("gpsimd", xln_d, xln_h, st_sems["dbg"], "st_dbg")

        reset(A0_KEEP)
        wbuf = [carve(16 * 128), carve(16 * 128)]
        wbuf_h = [carve_h(16 * 128), carve_h(16 * 128)]
        sg_t = [carve(LT), carve(LT)]
        A1_GATE_END = ao[0]
        u_row = [carve(3 + L), carve(3 + L)]
        c0 = carve(L)
        c1 = carve(L)
        xcv_t = [carve(L)]
        xcvh_t = [carve_h(L)]

        prog.run("vector", lambda e: e.memset(u_row[0][:, 0:3], 0.0))
        prog.run("vector", lambda e: e.memset(u_row[1][:, 0:3], 0.0))

        wb3_src = w_in_t

        def a1_weights(eo):
            pp = eo % 2
            prog.wait("sync", s_s, prog.cnt.get(f"a1_wcv_{eo - 2}", 0))
            if eo < 2:
                prog.wait("sync", s_v, ln1_done_v)
            lv = prog.dma("sync", wbuf[pp].rearrange("p (hc e) -> p hc e", e=128),
                          wb3_src[:, eo * 128:(eo + 1) * 128].rearrange("(hc p) e -> p hc e", p=128),
                          s_w[pp], f"w{pp}")
            prog.wait("scalar", s_w[pp], lv)
            prog.wait("scalar", s_t, prog.cnt.get(f"a1_t_{eo - 2}", 0))
            cv = prog.step("scalar", lambda e, pp=pp: e.activation(
                out=wbuf_h[pp], in_=wbuf[pp], func=AF.Copy), s_s, "s")
            prog.cnt[f"a1_wcv_{eo}"] = cv

        def phase_a1(eo):
            pp = eo % 2
            if eo + 1 < 2 * EC:
                a1_weights(eo + 1)
            prog.wait("tensor", s_s, prog.cnt[f"a1_wcv_{eo}"])
            if eo == 0:
                prog.wait("tensor", s_v, ln1_done_v)
            is_u = eo < EC
            ec = eo if is_u else eo - EC
            for b in range(BC):
                for lb in range(LB):
                    if is_u:
                        pt = psum[b * LB + lb]
                        w_ev = prog.cnt.get(f"a1_ev_{eo - 1}_{b}_{lb}", None)
                    else:
                        bank = 4 + (b * LB + lb) % 4
                        pt = psum[bank]
                        w_ev = prog.cnt.get(f"a1_gbank_{bank}", None)
                    if w_ev is not None:
                        prog.wait("tensor", s_v if w_ev[0] == "v" else s_s, w_ev[1])
                    for hc in range(HC):
                        fn = lambda e, pt=pt, pp=pp, hc=hc, b=b, lb=lb: e.matmul(
                            pt[:], wbuf_h[pp][:, hc * 128:(hc + 1) * 128],
                            xln3[:, hc, (b * LB + lb) * LT:(b * LB + lb + 1) * LT],
                            start=(hc == 0), stop=(hc == HC - 1))
                        if hc == HC - 1:
                            tv = prog.step("tensor", fn, s_t, "t")
                            prog.cnt[f"a1_t_{eo}"] = tv
                        else:
                            prog.on("tensor", fn)
                    if is_u:
                        prog.wait("vector", s_t, tv)
                        ev = prog.step("vector", lambda e, b=b, lb=lb, pt=pt: e.tensor_copy(
                            u_row[b][:, 3 + lb * LT: 3 + (lb + 1) * LT], pt[:]), s_v, "v")
                        prog.cnt[f"a1_ev_{eo}_{b}_{lb}"] = ("v", ev)
                    else:
                        gp = (eo * BC * LB + b * LB + lb) % 2
                        prog.wait("scalar", st_sems["sg"], prog.cnt.get(f"a1_sgst_{gp}", 0))
                        prog.wait("scalar", s_t, tv)
                        ev = prog.step("scalar", lambda e, gp=gp, pt=pt: e.activation(
                            out=sg_t[gp], in_=pt[:], func=AF.Silu), s_s, "s")
                        prog.cnt[f"a1_gbank_{bank}"] = ("s", ev)
                        prog.wait("gpsimd", s_s, ev)
                        stv = prog.dma("gpsimd", sg_f[b, ec, :, lb * LT:(lb + 1) * LT],
                                       sg_t[gp], st_sems["sg"], "st_sg")
                        prog.cnt[f"a1_sgst_{gp}"] = stv
                        prog.cnt[f"a1_sgst_ec_{ec}"] = stv
            if is_u:
                for b in range(BC):
                    if debug:
                        lastev = prog.cnt[f"a1_ev_{eo}_{b}_{LB - 1}"][1]
                        prog.wait("gpsimd", s_v, lastev)
                        prog.dma("gpsimd", u_d[b, ec], u_row[b][:, 3:3 + L], st_sems["dbg"], "st_dbg")
                    prog.wait("vector", s_s, prog.cnt.get("a1_silu", 0))
                    prog.on("vector", lambda e, b=b, ec=ec: e.tensor_scalar_mul(
                        c0, u_row[b][:, 0:L], cw_sb[:, ec * 4 + 0:ec * 4 + 1]))
                    prog.on("vector", lambda e, b=b, ec=ec: e.scalar_tensor_tensor(
                        out=c1, in0=u_row[b][:, 1:1 + L], scalar=cw_sb[:, ec * 4 + 1:ec * 4 + 2],
                        in1=c0, op0=ALU.mult, op1=ALU.add))
                    prog.on("vector", lambda e, b=b, ec=ec: e.scalar_tensor_tensor(
                        out=c0, in0=u_row[b][:, 2:2 + L], scalar=cw_sb[:, ec * 4 + 2:ec * 4 + 3],
                        in1=c1, op0=ALU.mult, op1=ALU.add))
                    vv = prog.step("vector", lambda e, b=b, ec=ec: e.scalar_tensor_tensor(
                        out=c1, in0=u_row[b][:, 3:3 + L], scalar=cw_sb[:, ec * 4 + 3:ec * 4 + 4],
                        in1=c0, op0=ALU.mult, op1=ALU.add), s_v, "v")
                    prog.wait("scalar", s_v, vv)
                    prog.wait("scalar", st_sems["xcv"], prog.cnt.get("a1_xcvst", 0))
                    sv = prog.step("scalar", lambda e, ec=ec: e.activation(
                        out=xcv_t[0], in_=c1, func=AF.Silu, bias=cb_sb[:, ec:ec + 1]), s_s, "s")
                    prog.cnt["a1_silu"] = sv
                    sv2 = prog.step("scalar", lambda e: e.activation(
                        out=xcvh_t[0], in_=xcv_t[0], func=AF.Copy), s_s, "s")
                    prog.wait("gpsimd", s_s, sv2)
                    prog.dma("gpsimd", xcv_f[b, ec], xcv_t[0], st_sems["xcv"], "st_xcv")
                    stv = prog.dma("gpsimd", xcv_h[b, ec], xcvh_t[0], st_sems["xcv"], "st_xcv")
                    prog.cnt["a1_xcvst"] = stv

        a1_weights(0)
        for eo in range(EC):
            phase_a1(eo)
        a1u_xcv = prog.val("st_xcv")
        a1u_v = prog.val("v")
        a1u_s = prog.val("s")
        for bank in range(4, 8):
            prog.cnt[f"a1_gbank_{bank}"] = ("v", a1u_v)

        reset(A1_GATE_END)
        wstage = carve(2560)
        xp_wh = carve_h(EC * 160)
        dtlow_h = carve_h(BC * L)
        xcvs = [carve_h(LT), carve_h(LT)]
        bc_ev = carve_h(LT)
        dt_wh = carve_h(EC * 128)
        lnt_t = [carve(LT), carve(LT)]
        sig_t = carve(LT)

        prog.wait("sync", s_v, a1u_v)
        prog.wait("sync", s_s, a1u_s)
        prog.wait("sync", st_sems["xcv"], a1u_xcv)
        for hh in range(2):
            lv = prog.dma("sync", wstage.rearrange("p (ec r) -> p ec r", r=160),
                          w_xp_t[hh * 2048:(hh + 1) * 2048, :]
                          .rearrange("(ec p) r -> p ec r", p=128), s_c, "c")
            prog.wait("scalar", s_c, lv)
            cv = prog.step("scalar", lambda e, hh=hh: e.activation(
                out=xp_wh[:, hh * 2560:(hh + 1) * 2560], in_=wstage,
                func=AF.Copy), s_s, "s")
            prog.wait("sync", s_s, cv)
        xpcv = cv
        for hh in range(2):
            lv = prog.dma("sync", wstage[:, 0:2048], w_dt_t[:, hh * 2048:(hh + 1) * 2048],
                          s_c, "c")
            prog.wait("scalar", s_c, lv)
            dtcv = prog.step("scalar", lambda e, hh=hh: e.activation(
                out=dt_wh[:, hh * 2048:(hh + 1) * 2048], in_=wstage[:, 0:2048],
                func=AF.Copy), s_s, "s")
            if hh == 0:
                prog.wait("sync", s_s, dtcv)

        def phase_a2_unit(uu):
            b, lb = divmod(uu, LB)
            for ec in range(EC):
                pp = ec % 2
                idx = uu * EC + ec
                prog.wait("sync", s_t, prog.cnt.get(f"a2_t_{idx - 2}", 0))
                lv = prog.dma("sync", xcvs[pp], xcv_h[b, ec, :, lb * LT:(lb + 1) * LT],
                              s_b[pp], f"b{pp}")
                prog.wait("tensor", s_b[pp], lv)
                if idx == 0:
                    prog.wait("tensor", s_s, xpcv)
                    prog.wait("tensor", s_v, a1u_v)
                if ec == 0:
                    prog.wait("tensor", s_v, prog.cnt.get(f"a2_ev_{uu - 1}", 0))
                    prog.wait("tensor", s_s, prog.cnt.get(f"a2_es_{uu - 1}", 0))
                prog.on("tensor", lambda e, pp=pp, ec=ec: e.matmul(
                    psum[0][:], xp_wh[:, ec * 160:ec * 160 + 128], xcvs[pp],
                    start=(ec == 0), stop=(ec == EC - 1), skip_group_check=True))
                tv = prog.step("tensor", lambda e, pp=pp, ec=ec: e.matmul(
                    psum[1][0:32, :], xp_wh[:, ec * 160 + 128:(ec + 1) * 160], xcvs[pp],
                    start=(ec == 0), stop=(ec == EC - 1), skip_group_check=True), s_t, "t")
                prog.cnt[f"a2_t_{idx}"] = tv
            prog.wait("vector", s_t, tv)
            ev = prog.step("vector", lambda e, uu=uu: e.tensor_copy(
                dtlow_h[:, uu * LT:(uu + 1) * LT], psum[0][:]), s_v, "v")
            prog.cnt[f"a2_ev_{uu}"] = ev
            prog.wait("scalar", s_t, tv)
            prog.wait("scalar", st_sems["bc"], prog.val("st_bc"))
            prog.step("scalar", lambda e: e.activation(
                out=bc_ev[0:32, :], in_=psum[1][0:32, :], func=AF.Copy), s_s, "s")
            es = prog.step("scalar", lambda e: e.activation(
                out=bc_ev[0:16, :], in_=bc_ev[0:16, :], func=AF.Copy, scale=-1.0), s_s, "s")
            prog.cnt[f"a2_es_{uu}"] = es
            prog.wait("gpsimd", s_s, es)
            prog.dma("gpsimd", bn_rh[b, :, lb * LT:(lb + 1) * LT], bc_ev[0:16, :], st_sems["bc"], "st_bc")
            stv = prog.dma("gpsimd", c_rh[b, :, lb * LT:(lb + 1) * LT], bc_ev[16:32, :], st_sems["bc"], "st_bc")

        def phase_a3_unit(uu):
            b, lb = divmod(uu, LB)
            prog.wait("tensor", s_v, prog.cnt[f"a2_ev_{uu}"])
            if uu == 0:
                prog.wait("tensor", s_s, dtcv)
            for ec in range(EC):
                idx = uu * EC + ec
                pp = idx % 2
                pt = psum[2 + pp]
                prog.wait("tensor", s_s, prog.cnt.get(f"a3_ev_{idx - 2}", 0))
                tv = prog.step("tensor", lambda e, pt=pt, ec=ec, uu=uu: e.matmul(
                    pt[:], dt_wh[:, ec * 128:(ec + 1) * 128],
                    dtlow_h[:, uu * LT:(uu + 1) * LT], start=True, stop=True), s_t, "t")
                prog.wait("scalar", s_t, tv)
                prog.wait("scalar", st_sems["lnt"], prog.cnt.get(f"a3_lntst_{pp}", 0))
                prog.step("scalar", lambda e, pt=pt, ec=ec: e.activation(
                    out=sig_t, in_=pt[:], func=AF.Sigmoid, scale=-1.0,
                    bias=dtb_neg[:, ec:ec + 1]), s_s, "s")
                sv = prog.step("scalar", lambda e, pp=pp: e.activation(
                    out=lnt_t[pp], in_=sig_t, func=AF.Ln), s_s, "s")
                prog.cnt[f"a3_ev_{idx}"] = sv
                prog.wait("gpsimd", s_s, sv)
                stv = prog.dma("gpsimd", lnt_f[b, ec, :, lb * LT:(lb + 1) * LT],
                               lnt_t[pp], st_sems["lnt"], "st_lnt")
                prog.cnt[f"a3_lntst_{pp}"] = stv

        for s_i in range(EC):
            phase_a1(EC + s_i)
            if s_i % 4 == 3:
                uu = s_i // 4
                phase_a2_unit(uu)
                phase_a3_unit(uu)

        reset(0)
        bbc_h = carve_h(8 * L)
        cbc_h = carve_h(8 * L)
        ident_h = carve_h(128)
        lnt_em = [carve(L), carve(L)]
        xcv_em = [carve(L), carve(L)]
        sg_em = [carve(L), carve(L)]
        ya_em = [carve(L), carve(L)]
        mdtu_h = carve_h(L)
        dA_b = [carve(L), carve(L)]
        w_hb = carve_h(L)
        s_hb = carve_h(L)
        q_hb = [carve_h(L), carve_h(L)]
        t1_f = carve(L)
        yo_hb = [carve_h(L), carve_h(L)]
        ya_st = [carve(L), carve(L)]
        idsrc = carve(128)

        store_barrier("sync")
        prog.wait("sync", s_t, prog.val("t"))
        prog.wait("sync", s_v, prog.val("v"))
        prog.wait("sync", s_s, prog.val("s"))
        ilv = prog.dma("sync", idsrc, ident_f, s_c, "c")
        prog.wait("vector", s_c, ilv)
        prog.step("vector", lambda e: e.tensor_copy(ident_h, idsrc), s_v, "v")

        bbc3 = bbc_h.rearrange("p (n l) -> p n l", n=8)
        cbc3 = cbc_h.rearrange("p (n l) -> p n l", n=8)

        def phase_b_pass(b, g):
            prog.wait("sync", s_v, prog.val("v"))
            prog.wait("sync", s_s, prog.val("s"))
            prog.wait("sync", st_sems["ya"], prog.val("st_ya"))
            prog.dma("sync", bbc3,
                     bn_rh[b, g * 8:(g + 1) * 8, :].unsqueeze(0).broadcast_to([128, 8, L]),
                     s_ld2, "ld2")
            bcv = prog.dma("sync", cbc3,
                           c_rh[b, g * 8:(g + 1) * 8, :].unsqueeze(0).broadcast_to([128, 8, L]),
                           s_ld2, "ld2")
            prog.wait("vector", s_ld2, bcv)
            prog.wait("vector", s_t, prog.val("t"))

            for ec in range(EC):
                pp = ec % 2
                prog.wait("sync", s_s, prog.cnt.get(f"b_exp_{pp}", 0))
                prog.wait("sync", s_v, prog.cnt.get(f"b_vec_{pp}", 0))
                prog.dma("sync", lnt_em[pp], lnt_f[b, ec], s_b[pp], f"b{pp}")
                lv = prog.dma("sync", xcv_em[pp], xcv_f[b, ec], s_b[pp], f"b{pp}")
                if g == 1:
                    prog.dma("sync", sg_em[pp], sg_f[b, ec], s_b[pp], f"b{pp}")
                    lv = prog.dma("sync", ya_em[pp], ya_f[ec], s_b[pp], f"b{pp}")
                prog.wait("vector", s_b[pp], lv)
                prog.on("vector", lambda e, pp=pp: e.tensor_tensor(
                    out=mdtu_h, in0=lnt_em[pp], in1=xcv_em[pp], op=ALU.mult))
                prog.wait("scalar", s_b[pp], lv)
                for n8 in range(8):
                    n = g * 8 + n8
                    d = n8 % 2
                    k = prog.bump("b_nsq")
                    prog.wait("scalar", s_v, prog.cnt.get(f"b_scanv_{k - 2}", 0))
                    sv = prog.step("scalar", lambda e, d=d, pp=pp, ec=ec, n=n: e.activation(
                        out=dA_b[d], in_=lnt_em[pp], func=AF.Exp,
                        scale=eA[:, ec * N + n:ec * N + n + 1]), s_s, "s")
                    prog.on("vector", lambda e, n8=n8: e.tensor_tensor(
                        out=w_hb, in0=mdtu_h, in1=bbc3[:, n8, :], op=ALU.mult))
                    prog.wait("vector", s_s, sv)
                    vv = prog.step("vector", lambda e, d=d: e.tensor_tensor_scan(
                        out=s_hb, data0=dA_b[d], data1=w_hb,
                        initial=0.0, op0=ALU.mult, op1=ALU.add), s_v, "v")
                    prog.cnt[f"b_scanv_{k}"] = vv
                    prog.wait("vector", s_t, prog.cnt.get(f"b_mmt_{k - 2}", 0))
                    qv = prog.step("vector", lambda e, d=d, n8=n8: e.tensor_tensor(
                        out=q_hb[d], in0=s_hb, in1=cbc3[:, n8, :], op=ALU.mult), s_v, "v")
                    prog.wait("tensor", s_v, qv)
                    if n8 == 0:
                        pc = prog.cnt.get("b_pycons", None)
                        if pc is not None:
                            prog.wait("tensor", {"s": s_s, "v": s_v}[pc[0]], pc[1])
                    for lt in range(4):
                        fn = lambda e, d=d, lt=lt, n8=n8: e.matmul(
                            psum[lt][:], ident_h, q_hb[d][:, lt * 512:(lt + 1) * 512],
                            start=(n8 == 0), stop=(n8 == 7))
                        if lt == 3:
                            tv = prog.step("tensor", fn, s_t, "t")
                            prog.cnt[f"b_mmt_{k}"] = tv
                        else:
                            prog.on("tensor", fn)
                prog.cnt[f"b_exp_{pp}"] = prog.val("s")
                mmv = prog.val("t")
                if g == 0:
                    gi = prog.bump("b_yagi")
                    prog.wait("scalar", s_t, mmv)
                    prog.wait("scalar", st_sems["ya"], prog.cnt.get(f"b_yasg_{gi - 2}", 0))
                    for lt in range(4):
                        fn = lambda e, lt=lt, pp=pp: e.activation(
                            out=ya_st[pp][:, lt * 512:(lt + 1) * 512], in_=psum[lt][:],
                            func=AF.Copy)
                        if lt == 3:
                            sv_e = prog.step("scalar", fn, s_s, "s")
                        else:
                            prog.on("scalar", fn)
                    prog.cnt["b_pycons"] = ("s", sv_e)
                    prog.wait("gpsimd", s_s, sv_e)
                    stv = prog.dma("gpsimd", ya_f[ec], ya_st[pp], st_sems["ya"], "st_ya")
                    prog.cnt[f"b_yasg_{gi}"] = stv
                else:
                    prog.wait("vector", s_t, mmv)
                    for lt in range(4):
                        prog.on("vector", lambda e, lt=lt, pp=pp, ec=ec: e.scalar_tensor_tensor(
                            out=t1_f[:, lt * 512:(lt + 1) * 512],
                            in0=xcv_em[pp][:, lt * 512:(lt + 1) * 512],
                            scalar=dD_sb[:, ec:ec + 1], in1=psum[lt][:],
                            op0=ALU.mult, op1=ALU.add))
                    yfv = prog.step("vector", lambda e, pp=pp: e.tensor_add(
                        t1_f, t1_f, ya_em[pp]), s_v, "v")
                    prog.cnt["b_pycons"] = ("v", yfv)
                    gi = prog.bump("b_ygi")
                    prog.wait("vector", st_sems["y"], prog.cnt.get(f"b_yst_{gi - 2}", 0))
                    vv = prog.step("vector", lambda e, pp=pp: e.tensor_tensor(
                        out=yo_hb[pp], in0=t1_f, in1=sg_em[pp], op=ALU.mult), s_v, "v")
                    prog.wait("gpsimd", s_v, vv)
                    stv = prog.dma("gpsimd", y_hb[b, ec], yo_hb[pp], st_sems["y"], "st_y")
                    prog.cnt[f"b_yst_{gi}"] = stv
                prog.cnt[f"b_vec_{pp}"] = prog.val("v")

        for b in range(BC):
            for g in range(2):
                phase_b_pass(b, g)

        reset(0)
        ow_stage = [carve(1024), carve(1024)]
        ow_h = carve_h(EC * 1024)
        yblk = [carve_h(EC * LT), carve_h(EC * LT)]
        mo_t = carve(8 * LT)

        store_barrier("sync")
        prog.wait("sync", s_st, prog.val("st"))
        prog.wait("sync", s_v, prog.val("v"))
        prog.wait("sync", s_s, prog.val("s"))

        for half in range(2):
            for ec in range(EC):
                pp = ec % 2
                prog.wait("sync", s_v, prog.cnt.get(f"c1w_{half}_{ec - 2}", 0))
                if half > 0 and ec < 2:
                    prog.wait("sync", s_t, prog.cnt.get("c1_t_last", 0))
                lv = prog.dma("sync", ow_stage[pp],
                              w_out_t[ec * 128:(ec + 1) * 128, half * 1024:(half + 1) * 1024],
                              s_ld, "ld")
                prog.wait("vector", s_ld, lv)
                cv = prog.step("vector", lambda e, pp=pp, ec=ec: e.tensor_copy(
                    ow_h[:, ec * 1024:(ec + 1) * 1024], ow_stage[pp]), s_v, "v")
                prog.cnt[f"c1w_{half}_{ec}"] = cv
            wcast = prog.val("v")
            def phase_c1_unit(uu, half=half, wcast=wcast):
                b, lb = divmod(uu, LB)
                pp = uu % 2
                idx = half * BC * LB + uu
                prog.wait("sync", s_t, prog.cnt.get(f"c1_t_{idx - 2}", 0))
                lv = prog.dma("sync", yblk[pp].rearrange("p (ec l) -> p ec l", ec=EC),
                              y_hb[b, :, :, lb * LT:(lb + 1) * LT].rearrange("ec p l -> p ec l"),
                              s_ld, "ld")
                prog.wait("tensor", s_ld, lv)
                if uu == 0:
                    prog.wait("tensor", s_v, wcast)
                for ho in range(8):
                    pt = psum[ho % 2]
                    prog.wait("tensor", s_v, prog.cnt.get(f"c1_ev_{idx}_{ho - 2}", 0))
                    if ho == 0:
                        prog.wait("vector", s_st, prog.cnt.get("c1_most", 0))
                    for ec in range(EC):
                        fn = lambda e, pt=pt, pp=pp, ec=ec, ho=ho: e.matmul(
                            pt[:], ow_h[:, ec * 1024 + ho * 128:ec * 1024 + (ho + 1) * 128],
                            yblk[pp][:, ec * LT:(ec + 1) * LT],
                            start=(ec == 0), stop=(ec == EC - 1))
                        if ec == EC - 1:
                            tv = prog.step("tensor", fn, s_t, "t")
                            prog.cnt[f"c1_t_{idx}"] = tv
                        else:
                            prog.on("tensor", fn)
                    prog.wait("vector", s_t, tv)
                    ev = prog.step("vector", lambda e, ho=ho, pt=pt: e.tensor_copy(
                        mo_t[:, ho * LT:(ho + 1) * LT], pt[:]), s_v, "v")
                    prog.cnt[f"c1_ev_{idx}_{ho}"] = ev
                prog.wait("gpsimd", s_v, prog.val("v"))
                stv = prog.dma("gpsimd",
                               mo_f[b, half * 8:(half + 1) * 8, :, lb * LT:(lb + 1) * LT]
                               .rearrange("ho p l -> p ho l"),
                               mo_t.rearrange("p (ho l) -> p ho l", ho=8), s_st, "st")
                prog.cnt["c1_most"] = stv
            for uu in range(BC * LB):
                phase_c1_unit(uu)
            prog.cnt["c1_t_last"] = prog.val("t")

        reset(0)
        mo_b = carve(HC * LT)
        x_b = [carve(HC * LT), carve(HC * LT)]
        sq_b = [carve(HC * LT), carve(HC * LT)]
        x2l_one = carve_h(HC * LT)
        x2l_b = [x2l_one, x2l_one]
        rows2 = [carve(1536), carve(1536)]

        mo3 = mo_b.rearrange("p (hc l) -> p hc l", hc=HC)
        xb3 = [x.rearrange("p (hc l) -> p hc l", hc=HC) for x in x_b]
        sqb3 = [x.rearrange("p (hc l) -> p hc l", hc=HC) for x in sq_b]
        x2l3 = [x.rearrange("p (hc l) -> p hc l", hc=HC) for x in x2l_b]

        prog.wait("sync", s_st, prog.val("st"))
        prog.wait("sync", s_v, prog.val("v"))
        prog.wait("sync", s_t, prog.val("t"))
        for uu in range(BC * LB):
            b, lb = divmod(uu, LB)
            d = uu % 2
            psA2, psB2 = (psum[0], psum[1]) if d == 0 else (psum[4], psum[5])
            psA3, psB3 = (psum[2], psum[3]) if d == 0 else (psum[6], psum[7])
            rm = rows2[d][0:1, 0:512]
            rq = rows2[d][0:1, 512:1024]
            rr = rows2[d][0:1, 1024:1536]
            prog.wait("sync", s_v, prog.cnt.get("c2_mocons", 0))
            prog.wait("sync", s_v, prog.cnt.get(f"c2_vfin_{d}", 0))
            prog.wait("sync", s_st, prog.cnt.get(f"c2_st_{d}", 0))
            prog.dma("sync", mo3,
                     mo_f[b, :, :, lb * LT:(lb + 1) * LT].rearrange("hc p l -> p hc l"),
                     s_ld, "ld")
            lv = prog.dma("sync", xb3[d],
                          x_fm[b, :, lb * LT:(lb + 1) * LT].rearrange("(hc p) l -> p hc l", p=128),
                          s_ld, "ld")
            prog.wait("scalar", s_ld, lv)
            prog.wait("tensor", s_ld, lv)
            ln_block(mo3, sqb3[d], rm, rq, rr, psA2, psB2, psA2, psB2,
                     prog.cnt.get(f"c2_vfin_{d}", 0), HC)
            tv = prog.val("t")
            prog.wait("vector", s_t, tv)
            prog.wait("vector", s_ld, lv)
            prog.on("vector", lambda e, psA2=psA2: e.tensor_tensor(
                out=mo3, in0=mo3,
                in1=psA2[:].unsqueeze(1).broadcast_to([128, HC, 512]), op=ALU.subtract))
            prog.on("vector", lambda e, psB2=psB2: e.tensor_tensor(
                out=mo3, in0=mo3,
                in1=psB2[:].unsqueeze(1).broadcast_to([128, HC, 512]), op=ALU.mult))
            vv1 = prog.step("vector", lambda e, d=d: e.tensor_add(x_b[d], x_b[d], mo_b),
                            s_v, "v")
            prog.cnt["c2_mocons"] = vv1
            prog.wait("scalar", s_v, vv1)
            prog.wait("tensor", s_v, vv1)
            ln_block(xb3[d], sqb3[d], rm, rq, rr, psA3, psB3, psA3, psB3, vv1, HC)
            tv = prog.val("t")
            prog.wait("vector", s_t, tv)
            prog.on("vector", lambda e, d=d, psA3=psA3: e.tensor_tensor(
                out=sqb3[d], in0=xb3[d],
                in1=psA3[:].unsqueeze(1).broadcast_to([128, HC, 512]), op=ALU.subtract))
            prog.wait("vector", s_st, prog.cnt.get("c2_st", 0))
            vv = prog.step("vector", lambda e, d=d, psB3=psB3: e.tensor_tensor(
                out=x2l3[d], in0=sqb3[d],
                in1=psB3[:].unsqueeze(1).broadcast_to([128, HC, 512]), op=ALU.mult), s_v, "v")
            prog.cnt[f"c2_vfin_{d}"] = vv
            prog.cnt[f"c2_v_{uu}"] = vv
            prog.wait("gpsimd", s_v, vv)
            prog.dma("gpsimd",
                     x2_f[b, :, :, lb * LT:(lb + 1) * LT].rearrange("hc p l -> p hc l"),
                     xb3[d], s_st, "st")
            stv = prog.dma("gpsimd",
                           x2ln_h[b, :, :, lb * LT:(lb + 1) * LT].rearrange("hc p l -> p hc l"),
                           x2l3[d], s_st, "st")
            prog.cnt[f"c2_st_{d}"] = stv
            prog.cnt["c2_st"] = stv

        reset(0)
        w1_stage = [carve(2048), carve(2048)]
        w1_h = carve_h(HC * 2048)
        w2_st = carve(HC * KF)
        w2_h = carve_h(HC * KF)
        x2l_s = [carve_h(HC * LT), carve_h(HC * LT)]
        h1_t = carve_h(HC * LT)
        ex_t = carve(LT)[0:KF, :]
        al_t = carve(LT)[0:KF, :]

        prog.wait("sync", s_st, prog.val("st"))
        prog.wait("sync", s_v, prog.val("v"))
        prog.wait("sync", s_t, prog.val("t"))
        prog.wait("sync", s_s, prog.val("s"))
        for dc in range(HC):
            pp = dc % 2
            prog.wait("sync", s_v, prog.cnt.get(f"c3w_{dc - 2}", 0))
            lv = prog.dma("sync", w1_stage[pp], w1[dc * 128:(dc + 1) * 128, :], s_ld, "ld")
            prog.wait("vector", s_ld, lv)
            cv = prog.step("vector", lambda e, pp=pp, dc=dc: e.tensor_copy(
                w1_h[:, dc * 2048:(dc + 1) * 2048], w1_stage[pp]), s_v, "v")
            prog.cnt[f"c3w_{dc}"] = cv
        lv = prog.dma("sync", w2_st.rearrange("p (dc k) -> p dc k", k=KF),
                      w2.rearrange("(dc p) k -> p dc k", p=128), s_ld, "ld")
        prog.wait("vector", s_ld, lv)
        wc = prog.step("vector", lambda e: e.tensor_copy(w2_h, w2_st), s_v, "v")

        def phase_c3a_unit(uu):
            b, lb = divmod(uu, LB)
            pp = uu % 2
            prog.wait("sync", s_t, prog.cnt.get(f"c3a_t_{uu - 2}", 0))
            lv = prog.dma("sync", x2l_s[pp].rearrange("p (hc l) -> p hc l", hc=HC),
                          x2ln_h[b, :, :, lb * LT:(lb + 1) * LT].rearrange("hc p l -> p hc l"),
                          s_ld, "ld")
            prog.wait("tensor", s_ld, lv)
            if uu == 0:
                prog.wait("tensor", s_v, wc)
            for eo in range(HC):
                pt = psum[eo % 2]
                prog.wait("tensor", s_s, prog.cnt.get(f"c3a_ev_{uu}_{eo - 2}", 0))
                for dc in range(HC):
                    fn = lambda e, pt=pt, pp=pp, dc=dc, eo=eo: e.matmul(
                        pt[:], w1_h[:, dc * 2048 + eo * 128:dc * 2048 + (eo + 1) * 128],
                        x2l_s[pp][:, dc * LT:(dc + 1) * LT],
                        start=(dc == 0), stop=(dc == HC - 1))
                    if dc == HC - 1:
                        tv = prog.step("tensor", fn, s_t, "t")
                    else:
                        prog.on("tensor", fn)
                prog.wait("scalar", s_t, tv)
                if eo == 0:
                    prog.wait("scalar", s_st, prog.cnt.get("c3a_h1st", 0))
                ev = prog.step("scalar", lambda e, eo=eo, pt=pt: e.activation(
                    out=h1_t[:, eo * LT:(eo + 1) * LT], in_=pt[:], func=AF.Relu,
                    bias=b1_sb[:, eo:eo + 1]), s_s, "s")
                prog.cnt[f"c3a_ev_{uu}_{eo}"] = ev
                prog.wait("tensor", s_s, ev)
                tv2 = prog.step("tensor", lambda e, eo=eo: e.matmul(
                    psum[2][0:KF, :], w2_h[:, eo * KF:(eo + 1) * KF],
                    h1_t[:, eo * LT:(eo + 1) * LT],
                    start=(eo == 0), stop=(eo == HC - 1), skip_group_check=True), s_t, "t")
                prog.cnt[f"c3a_t_{uu}"] = tv2
            tvl = prog.val("t")
            prog.wait("scalar", s_t, tvl)
            sv = prog.step("scalar", lambda e: e.activation(
                out=ex_t, in_=psum[2][0:KF, :], func=AF.Exp, bias=b2_col), s_s, "s")
            prog.wait("tensor", s_s, sv)
            tv3 = prog.step("tensor", lambda e: e.matmul(
                psum[3][0:1, :], one_col8, ex_t, start=True, stop=True), s_t, "t")
            prog.wait("vector", s_t, tv3)
            sv2 = prog.step("vector", lambda e: e.reciprocal(
                al_t[0:1, :], psum[3][0:1, :]), s_v, "v")
            prog.wait("tensor", s_v, sv2)
            tv4 = prog.step("tensor", lambda e: e.matmul(
                psum[4][0:KF, :], one_row1[0:1, 0:KF], al_t[0:1, :], start=True, stop=True), s_t, "t")
            prog.wait("vector", s_t, tv4)
            vv = prog.step("vector", lambda e: e.tensor_mul(al_t, ex_t, psum[4][0:KF, :]), s_v, "v")
            prog.wait("gpsimd", s_v, vv)
            prog.wait("gpsimd", s_s, prog.val("s"))
            stv_h = prog.dma("gpsimd",
                             h1_h[b, :, :, lb * LT:(lb + 1) * LT].rearrange("hc p l -> p hc l"),
                             h1_t.rearrange("p (hc l) -> p hc l", hc=HC), s_st, "st")
            prog.cnt["c3a_h1st"] = stv_h
            stv = prog.dma("gpsimd", al_f[b, :, lb * LT:(lb + 1) * LT], al_t, s_st, "st")
            prog.wait("scalar", s_st, stv)
        for uu in range(BC * LB):
            phase_c3a_unit(uu)

        reset(0)
        vt_st = [carve(KR), carve(KR)]
        vt_h = carve_h(HC * KR)
        ut_st = [carve(2048), carve(2048)]
        ut_h = carve_h(KRC * 2048)
        x2l_s2 = [carve_h(HC * LT), carve_h(HC * LT)]
        avx_h = carve_h(KRC * LT)
        alb = carve(KRC * LT)
        fo_t = carve(HC * LT)
        x2_s = carve(HC * LT)
        sq_c = carve(HC * LT)
        row_m3 = carve(512)[0:1, :]
        row_q3 = carve(512)[0:1, :]
        row_r3 = carve(512)[0:1, :]

        fo3 = fo_t.rearrange("p (hc l) -> p hc l", hc=HC)
        sqc3 = sq_c.rearrange("p (hc l) -> p hc l", hc=HC)
        x2s3 = x2_s.rearrange("p (hc l) -> p hc l", hc=HC)

        prog.wait("sync", s_v, prog.val("v"))
        prog.wait("sync", s_s, prog.val("s"))
        prog.wait("sync", s_t, prog.val("t"))
        for dc in range(HC):
            pp = dc % 2
            prog.wait("sync", s_v, prog.cnt.get(f"c3bw_{dc - 2}", 0))
            lv = prog.dma("sync", vt_st[pp], vt[dc * 128:(dc + 1) * 128, :], s_ld, "ld")
            prog.wait("vector", s_ld, lv)
            cv = prog.step("vector", lambda e, pp=pp, dc=dc: e.tensor_copy(
                vt_h[:, dc * KR:(dc + 1) * KR], vt_st[pp]), s_v, "v")
            prog.cnt[f"c3bw_{dc}"] = cv
        for kc in range(KRC):
            pp = kc % 2
            prog.wait("sync", s_v, prog.cnt.get(f"c3bu_{kc - 2}", 0))
            lv = prog.dma("sync", ut_st[pp], ut[kc * 128:(kc + 1) * 128, :], s_ld, "ld")
            prog.wait("vector", s_ld, lv)
            cv = prog.step("vector", lambda e, pp=pp, kc=kc: e.tensor_copy(
                ut_h[:, kc * 2048:(kc + 1) * 2048], ut_st[pp]), s_v, "v")
            prog.cnt[f"c3bu_{kc}"] = cv
        wc2 = prog.val("v")
        prog.wait("sync", s_st, prog.val("st"))

        prev_v_cell = [prog.val("v")]
        def phase_c3b_unit(uu):
            prev_v = prev_v_cell[0]
            b, lb = divmod(uu, LB)
            pp = uu % 2
            prog.wait("sync", s_t, prog.cnt.get(f"c3b_t_{uu - 2}", 0))
            prog.wait("sync", s_v, prog.cnt.get(f"c3b_v_{uu - 1}", 0))
            prog.wait("sync", s_st, prog.cnt.get("c3b_ost", 0))
            prog.dma("sync", x2l_s2[pp].rearrange("p (hc l) -> p hc l", hc=HC),
                     x2ln_h[b, :, :, lb * LT:(lb + 1) * LT].rearrange("hc p l -> p hc l"),
                     s_ld, "ld")
            prog.dma("sync", x2s3,
                     x2_f[b, :, :, lb * LT:(lb + 1) * LT].rearrange("hc p l -> p hc l"),
                     s_ld, "ld")
            for kc in range(KRC):
                for kk in range(2):
                    lva = prog.dma("sync",
                                   alb[kk * RF:(kk + 1) * RF, kc * LT:(kc + 1) * LT],
                                   al_f[b, 2 * kc + kk, lb * LT:(lb + 1) * LT]
                                   .unsqueeze(0).broadcast_to([RF, LT]),
                                   s_ld, "ld")
            prog.wait("tensor", s_ld, lva)
            if uu == 0:
                prog.wait("tensor", s_v, wc2)
            for kc in range(KRC):
                prog.wait("tensor", s_v, prog.cnt.get(f"c3b_av_{uu - 1}_{kc}", 0))
                if kc >= 2:
                    prog.wait("tensor", s_s, prog.cnt.get(f"c3b_lnsq_{uu - 1}", 0))
                for dc in range(HC):
                    fn = lambda e, kc=kc, dc=dc, pp=pp: e.matmul(
                        psum[kc][:], vt_h[:, dc * KR + kc * 128:dc * KR + (kc + 1) * 128],
                        x2l_s2[pp][:, dc * LT:(dc + 1) * LT],
                        start=(dc == 0), stop=(dc == HC - 1))
                    if dc == HC - 1:
                        tv = prog.step("tensor", fn, s_t, "t")
                    else:
                        prog.on("tensor", fn)
                prog.wait("vector", s_t, tv)
                av = prog.step("vector", lambda e, kc=kc: e.tensor_mul(
                    avx_h[:, kc * LT:(kc + 1) * LT], psum[kc][:],
                    alb[:, kc * LT:(kc + 1) * LT]), s_v, "v")
                prog.cnt[f"c3b_av_{uu}_{kc}"] = av
            avv = prog.val("v")
            prog.wait("tensor", s_v, avv)
            for dc in range(HC):
                pt = psum[4 + dc % 2]
                prog.wait("tensor", s_v, prog.cnt.get(f"c3b_fo_{uu}_{dc - 2}", 0))
                for kc in range(KRC):
                    fn = lambda e, pt=pt, dc=dc, kc=kc: e.matmul(
                        pt[:], ut_h[:, kc * 2048 + dc * 128:kc * 2048 + (dc + 1) * 128],
                        avx_h[:, kc * LT:(kc + 1) * LT],
                        start=(kc == 0), stop=(kc == KRC - 1))
                    if kc == KRC - 1:
                        tv = prog.step("tensor", fn, s_t, "t")
                        prog.cnt[f"c3b_t_{uu}"] = tv
                    else:
                        prog.on("tensor", fn)
                prog.wait("vector", s_t, tv)
                fv = prog.step("vector", lambda e, dc=dc, pt=pt: e.tensor_copy(
                    fo_t[:, dc * LT:(dc + 1) * LT], pt[:]), s_v, "v")
                prog.cnt[f"c3b_fo_{uu}_{dc}"] = fv
            fvl = prog.val("v")
            prog.wait("scalar", s_v, fvl)
            prog.wait("tensor", s_v, fvl)
            ln_block(fo3, sqc3, row_m3, row_q3, row_r3, psum[6], psum[7], psum[2], psum[3],
                     prev_v, HC)
            prog.cnt[f"c3b_lnsq_{uu}"] = prog.val("s")
            tv = prog.val("t")
            prog.wait("vector", s_t, tv)
            prog.wait("vector", s_ld, lva)
            prog.on("vector", lambda e: e.tensor_tensor(
                out=sqc3, in0=fo3,
                in1=psum[6][:].unsqueeze(1).broadcast_to([128, HC, 512]), op=ALU.subtract))
            prog.on("vector", lambda e: e.tensor_tensor(
                out=sqc3, in0=sqc3,
                in1=psum[7][:].unsqueeze(1).broadcast_to([128, HC, 512]), op=ALU.mult))
            vv = prog.step("vector", lambda e: e.tensor_add(x2_s, sq_c, x2_s), s_v, "v")
            prog.cnt[f"c3b_v_{uu}"] = vv
            prev_v_cell[0] = vv
            prog.wait("gpsimd", s_v, vv)
            stv = prog.dma("gpsimd",
                           out_fm[b, :, lb * LT:(lb + 1) * LT].rearrange("(hc p) l -> p hc l", p=128),
                           x2s3, s_st, "st")
            prog.cnt["c3b_ost"] = stv
        for uu in range(BC * LB):
            phase_c3b_unit(uu)

        prog.wait("sync", s_st, prog.val("st"))

        @block.sync
        def _(eng):
            for fn in prog.ops["sync"]:
                fn(eng)

        @block.tensor
        def _(eng):
            for fn in prog.ops["tensor"]:
                fn(eng)

        @block.vector
        def _(eng):
            for fn in prog.ops["vector"]:
                fn(eng)

        @block.scalar
        def _(eng):
            for fn in prog.ops["scalar"]:
                fn(eng)

        @block.gpsimd
        def _(eng):
            for fn in prog.ops["gpsimd"]:
                fn(eng)

    return nc


def _prep_core_inputs(inputs, core):
    d = inputs
    x = d["L0_attn_hidden_states"][2 * core:2 * core + 2]
    conv_w = d["L0_attn_conv_w"][:, 0, :]
    m = {
        "x_fm": x.transpose(0, 2, 1),
        "w_in_t": d["L0_attn_in_proj_w"].T,
        "conv_wt": conv_w.reshape(EC, 128, 4).transpose(1, 0, 2).reshape(128, EC * 4),
        "conv_bt": d["L0_attn_conv_b"].reshape(EC, 128).T,
        "w_xp_t": d["L0_attn_x_proj_w"].T,
        "w_dt_t": d["L0_attn_dt_proj_w"].T,
        "dt_bt": d["L0_attn_dt_proj_b"].reshape(EC, 128).T,
        "alog_t": d["L0_attn_A_log"].reshape(EC, 128, N).transpose(1, 0, 2).reshape(128, EC * N),
        "d_t": d["L0_attn_D"].reshape(EC, 128).T,
        "w_out_t": d["L0_attn_out_proj_w"].T,
        "w1": d["L0_ffn_W1"],
        "b1_t": d["L0_ffn_b1"].reshape(HC, 128).T,
        "w2": d["L0_ffn_W2"],
        "b2_t": d["L0_ffn_b2"].reshape(KF, 1),
        "vt": d["L0_ffn_V"].transpose(2, 0, 1).reshape(H, KR),
        "ut": d["L0_ffn_U"].transpose(0, 2, 1).reshape(KR, H),
        "ident_f": np.eye(128, dtype=np.float32),
    }
    return {k: np.ascontiguousarray(np.asarray(v, dtype=np.float32)) for k, v in m.items()}


def kernel(**inputs):
    nc = build_nc(debug=False)
    in_maps = [_prep_core_inputs(inputs, c) for c in range(NCORES)]
    res = run_bass_kernel_spmd(nc, in_maps, core_ids=list(range(NCORES)))
    outs = [r["out_fm"] for r in res.results]
    full = np.concatenate([o.transpose(0, 2, 1) for o in outs], axis=0)
    return np.ascontiguousarray(full.astype(np.float32))
